# version 05
# baseline (speedup 1.0000x reference)
import contextlib
import numpy as np

import concourse.bass as bass
import concourse.mybir as mybir
from concourse.bass_utils import run_bass_kernel_spmd

F32 = mybir.dt.float32
BF16 = mybir.dt.bfloat16
I32 = mybir.dt.int32
U32 = mybir.dt.uint32
AF = mybir.ActivationFunctionType
ALU = mybir.AluOpType
AX = mybir.AxisListType

S = 4096
D = 1024
NDC = 8
TG = 512
NTG = S // TG
NT128 = S // 128
H0 = 16
WIN = 256
CBS = 32
NB = S // CBS
SBS = 64
TOPK = 16
NSB = S // SBS
NSEL = NSB * TOPK // 64 * 64
NEXP = 8
NEG = -100.0
SCALE0 = 0.125

ENGS = ("sync", "tensor", "vector", "scalar", "gpsimd")


class _Cnt(dict):

    def __init__(self, kb):
        super().__init__()
        self.kb = kb

    def _r(self, name):
        return self.kb.phase_map.get(name, name)

    def __getitem__(self, name):
        return super().get(self._r(name), 0)

    def get(self, name, default=0):
        return super().get(self._r(name), default)

    def bump(self, name, by):
        r = self._r(name)
        super().__setitem__(r, super().get(r, 0) + by)
        return super().__getitem__(r)


class KB:

    def __init__(self, nc, ctx):
        self.nc = nc
        self.ctx = ctx
        self.q = {e: [] for e in ENGS}
        self.sems = {}
        self.cnt = _Cnt(self)
        self.phase_map = {}
        self.phase_used = {"hw": 0, "sw": 0}

    def new_phase(self):
        self.phase_map.clear()
        self.phase_used = {"hw": 0, "sw": 0}

    def sem(self, name):
        name = self.phase_map.get(name, name)
        if name not in self.sems:
            self.sems[name] = self.ctx.enter_context(self.nc.semaphore(name))
            if name not in self.cnt:
                dict.__setitem__(self.cnt, name, 0)
        return self.sems[name]

    def _pooled(self, eng, name):
        if name in self.phase_map:
            return self.phase_map[name]
        qt = "sw" if eng == "gpsimd" else "hw"
        i = self.phase_used[qt]
        self.phase_used[qt] += 1
        pname = f"dp_{qt}{i}"
        self.phase_map[name] = pname
        return pname

    def raw(self, eng, fn):
        self.q[eng].append(fn)

    def wait(self, eng, name, val):
        if val <= 0:
            return
        s = self.sem(name)
        self.raw(eng, lambda e, s=s, v=val: e.wait_ge(s, v))

    def wait_edge(self, eng, edge):
        if edge is not None:
            self.wait(eng, edge[0], edge[1])

    def op(self, eng, fn, serialize=True):
        name = "sd_" + eng
        s = self.sem(name)
        n = self.cnt[name]
        if serialize and n > 0:
            self.raw(eng, lambda e, s=s, n=n: e.wait_ge(s, n))
        self.raw(eng, lambda e, s=s, fn=fn: fn(e).then_inc(s, 1))
        dict.__setitem__(self.cnt, name, n + 1)
        return (name, n + 1)

    def mm(self, fn, inc=False):
        if inc:
            s = self.sem("sd_pe")
            self.raw("tensor", lambda e, s=s, fn=fn: fn(e).then_inc(s, 1))
            v = self.cnt.bump("sd_pe", 1)
            return ("sd_pe", v)
        self.raw("tensor", lambda e, fn=fn: fn(e))
        return None

    def dma(self, eng, name, fn):
        assert name.startswith("d_"), name
        pname = self._pooled(eng, name)
        s = self.sem(pname)
        self.raw(eng, lambda e, s=s, fn=fn: fn(e).then_inc(s, 16))
        v = self.cnt.bump(pname, 16)
        return (pname, v)

    def barrier(self):
        snap = [(n, v) for n, v in dict.items(self.cnt) if v > 0]
        for eng in ENGS:
            for n, v in snap:
                self.wait(eng, n, v)

    def finalize(self):
        with self.nc.Block() as block:
            for eng in ENGS:
                def runner(e, eng=eng):
                    for f in self.q[eng]:
                        f(e)
                getattr(block, eng)(runner)



def _chunked_T(w_t):
    din, m = w_t.shape
    return np.ascontiguousarray(w_t.reshape(din // 128, 128, m).transpose(1, 0, 2))


def host_prep(inp):
    d = {}
    perm = np.concatenate([
        np.array([h * 192 + c * 64 + dd for h in range(16) for dd in range(64)])
        for c in range(3)])
    wqkv_t = np.ascontiguousarray(inp["L0_attn_w_qkv"].T[:, perm])
    d["wqkvT"] = wqkv_t
    d["bqkv"] = np.ascontiguousarray(inp["L0_attn_b_qkv"][perm])[None, :]
    d["wcompT"] = np.ascontiguousarray(
        inp["L0_attn_w_comp"].reshape(D, CBS, D).transpose(1, 2, 0))
    d["bcomp"] = inp["L0_attn_b_comp"][None, :]
    d["wscoreT"] = np.ascontiguousarray(
        inp["L0_attn_w_score"].T.reshape(NDC, 128, 1).transpose(1, 0, 2))
    pk = perm[D:2 * D]
    pv = perm[2 * D:3 * D]
    d["wkrows"] = np.ascontiguousarray(inp["L0_attn_w_qkv"][pk])
    d["wvrows"] = np.ascontiguousarray(inp["L0_attn_w_qkv"][pv])
    d["wsdo"] = np.ascontiguousarray(inp["L0_attn_w_score"][0].reshape(NDC, 128).T)
    d["bscore"] = inp["L0_attn_b_score"][None, :]
    d["wgateT"] = np.ascontiguousarray(
        inp["L0_attn_w_gate"].T.reshape(NDC, 128, 3).transpose(1, 0, 2))
    d["bgate"] = inp["L0_attn_b_gate"][None, :]
    d["woT"] = np.ascontiguousarray(inp["L0_attn_w_o"].T)
    d["bo"] = inp["L0_attn_b_o"][None, :]
    d["wg0T"] = np.ascontiguousarray(
        inp["L0_ffn_Wg"].T.reshape(NDC, 128, NEXP).transpose(1, 0, 2))
    d["gbank"] = inp["L0_ffn_gate_bank"]
    d["ubank"] = inp["L0_ffn_up_bank"]
    d["dbank"] = inp["L0_ffn_down_bank"]
    d["wqk1T"] = np.ascontiguousarray(
        np.concatenate([inp["L1_attn_Wq"].T, inp["L1_attn_Wk"].T], axis=1))
    d["wv1T"] = np.ascontiguousarray(inp["L1_attn_Wv"].T)
    d["wo1T"] = np.ascontiguousarray(inp["L1_attn_Wo"].T)
    d["wg1T"] = np.ascontiguousarray(
        inp["L1_ffn_gate_w"].T.reshape(NDC, 128, NEXP).transpose(1, 0, 2))
    d["w11T"] = np.ascontiguousarray(inp["L1_ffn_w1"].transpose(0, 2, 1))
    d["w21T"] = np.ascontiguousarray(inp["L1_ffn_w2"].transpose(0, 2, 1))
    d["w31T"] = np.ascontiguousarray(inp["L1_ffn_w3"].transpose(0, 2, 1))
    d["b11"] = np.ascontiguousarray(
        inp["L1_ffn_b1"].reshape(NEXP, 8, 128).transpose(2, 0, 1))
    d["b31"] = np.ascontiguousarray(
        inp["L1_ffn_b3"].reshape(NEXP, 8, 128).transpose(2, 0, 1))
    d["b21"] = inp["L1_ffn_b2"]
    d["sw1T"] = np.ascontiguousarray(inp["L1_ffn_sw1"].T)
    d["sw2T"] = np.ascontiguousarray(inp["L1_ffn_sw2"].T)
    d["sw3T"] = np.ascontiguousarray(inp["L1_ffn_sw3"].T)

    ii = np.arange(128)
    d["c_tri_ge"] = (ii[:, None] >= ii[None, :]).astype(np.float32)
    d["c_tri_le"] = (ii[:, None] <= ii[None, :]).astype(np.float32)
    d["c_ident"] = np.eye(128, dtype=np.float32)
    hd = 64
    inv = 1.0 / (10000.0 ** (np.arange(0, hd, 2, dtype=np.float32) / hd))
    pos = np.arange(S, dtype=np.float32)
    ang = pos[None, :] * np.concatenate([inv, inv])[:, None]
    cos = np.cos(ang).astype(np.float32)
    sin = np.sin(ang).astype(np.float32)
    d["c_cos"] = np.ascontiguousarray(np.concatenate([cos, cos], axis=0))
    d["c_sin"] = np.ascontiguousarray(np.concatenate([sin, sin], axis=0))
    P = np.zeros((128, 128), dtype=np.float32)
    for base in (0, 64):
        for r in range(32):
            P[base + r, base + r + 32] = -1.0
            P[base + 32 + r, base + r] = 1.0
    d["c_PT"] = np.ascontiguousarray(P.T)

    x = np.asarray(inp["L0_attn_x"])
    maps = []
    for b in range(8):
        m = dict(d)
        m["x_fm"] = np.ascontiguousarray(x[b].T).reshape(NDC, 128, S)
        maps.append(m)
    return maps



def build(dbg=(), upto=None, only=None):
    nc = bass.Bass("TRN2", target_bir_lowering=False, debug=False)
    dbg = set(dbg)

    def inp(name, shape, dtype=F32):
        return nc.dram_tensor(name, list(shape), dtype, kind="ExternalInput").ap()

    def scr(name, shape, dtype):
        kind = "ExternalOutput" if name in dbg else "Internal"
        return nc.dram_tensor(name, list(shape), dtype, kind=kind).ap()

    g = {}
    g["x_fm"] = inp("x_fm", (NDC, 128, S))
    g["wqkvT"] = inp("wqkvT", (D, 3 * D))
    g["bqkv"] = inp("bqkv", (1, 3 * D))
    g["wcompT"] = inp("wcompT", (CBS, D, D))
    g["bcomp"] = inp("bcomp", (1, D))
    g["wscoreT"] = inp("wscoreT", (128, NDC, 1))
    g["wkrows"] = inp("wkrows", (D, D))
    g["wvrows"] = inp("wvrows", (D, D))
    g["wsdo"] = inp("wsdo", (128, NDC))
    g["bscore"] = inp("bscore", (1, 1))
    g["wgateT"] = inp("wgateT", (128, NDC, 3))
    g["bgate"] = inp("bgate", (1, 3))
    g["woT"] = inp("woT", (D, D))
    g["bo"] = inp("bo", (1, D))
    g["wg0T"] = inp("wg0T", (128, NDC, NEXP))
    g["gbank"] = inp("gbank", (NEXP, D, D))
    g["ubank"] = inp("ubank", (NEXP, D, D))
    g["dbank"] = inp("dbank", (NEXP, D, D))
    g["wqk1T"] = inp("wqk1T", (D, 2 * D))
    g["wv1T"] = inp("wv1T", (D, D))
    g["wo1T"] = inp("wo1T", (D, D))
    g["wg1T"] = inp("wg1T", (128, NDC, NEXP))
    g["w11T"] = inp("w11T", (NEXP, D, D))
    g["w21T"] = inp("w21T", (NEXP, D, D))
    g["w31T"] = inp("w31T", (NEXP, D, D))
    g["b11"] = inp("b11", (128, NEXP, NEXP))
    g["b31"] = inp("b31", (128, NEXP, NEXP))
    g["b21"] = inp("b21", (NEXP, D))
    g["sw1T"] = inp("sw1T", (D, 2 * D))
    g["sw2T"] = inp("sw2T", (D, 2 * D))
    g["sw3T"] = inp("sw3T", (2 * D, D))
    for cn, shp in (("c_tri_ge", (128, 128)), ("c_tri_le", (128, 128)),
                    ("c_ident", (128, 128)), ("c_cos", (128, S)), ("c_sin", (128, S)),
                    ("c_PT", (128, 128))):
        g[cn] = inp(cn, shp)

    for ln in (0, 1):
        g[f"xh{ln}"] = scr(f"xh{ln}", (NDC, 128, S), BF16)
        g[f"xl{ln}"] = scr(f"xl{ln}", (NDC, 128, S), BF16)
    g["q0T"] = scr("q0T", (NDC, 128, S), BF16)
    g["k0T"] = scr("k0T", (NDC, 128, S), BF16)
    g["v0T"] = scr("v0T", (NDC, 128, S), BF16)
    g["k0tm"] = scr("k0tm", (S, D), BF16)
    g["v0tm"] = scr("v0tm", (S, D), BF16)
    g["scrow"] = scr("scrow", (2, S), F32)
    g["scb"] = scr("scb", (2, NSB, SBS), F32)
    g["selidx"] = scr("selidx", (2, NSEL), I32)
    g["sk0tm"] = scr("sk0tm", (NSEL, D), BF16)
    g["sv0tm"] = scr("sv0tm", (NSEL, D), BF16)
    g["sk0T"] = scr("sk0T", (NDC, 128, NSEL), BF16)
    g["ck0T"] = scr("ck0T", (NDC, 128, NB), BF16)
    g["cv0T"] = scr("cv0T", (NDC, 128, NB), BF16)
    g["ck0tm"] = scr("ck0tm", (NB, D), BF16)
    g["cv0tm"] = scr("cv0tm", (NB, D), BF16)
    g["negm"] = scr("negm", (4, S), BF16)
    g["mrow"] = scr("mrow", (1, S), F32)
    g["qnorm"] = scr("qnorm", (1, S), F32)
    g["oc0T"] = scr("oc0T", (NDC, 128, 384), BF16)
    g["os0T"] = scr("os0T", (NDC, 128, 1280), BF16)
    g["ow0T"] = scr("ow0T", (NDC, 128, S), BF16)
    g["s0T"] = scr("s0T", (NDC, 128, S), BF16)
    g["r0lg"] = scr("r0lg", (NEXP, S), F32)
    g["r0neg"] = scr("r0neg", (NEXP, S), BF16)
    g["h0"] = scr("h0", (NEXP * NDC, 128, S), BF16)
    g["x1"] = scr("x1", (NDC, 128, S), F32)
    g["q1T"] = scr("q1T", (NDC, 128, S), BF16)
    g["k1T"] = scr("k1T", (NDC, 128, S), BF16)
    g["v1tm"] = scr("v1tm", (S, D), BF16)
    g["o1"] = scr("o1", (NDC, 128, S), BF16)
    g["o1v"] = g["o1"]
    g["s1T"] = scr("s1T", (NDC, 128, S), BF16)
    g["r1lg"] = scr("r1lg", (NEXP, S), F32)
    g["r1w"] = scr("r1w", (NEXP, S), F32)
    g["r1neg"] = scr("r1neg", (NEXP, S), BF16)
    g["r1act"] = scr("r1act", (NEXP, S), BF16)
    g["h1"] = scr("h1", (NEXP * NDC, 128, S), BF16)
    g["x31"] = scr("x31", (NEXP * NDC, 128, S), BF16)
    g["y1T"] = scr("y1T", (NDC, 128, S), F32)
    g["out_fm"] = nc.dram_tensor("out_fm", [NDC, 128, S], F32, kind="ExternalOutput").ap()

    with contextlib.ExitStack() as ctx:
        kb = KB(nc, ctx)
        phases = _phases(kb, g, ctx)
        for name, fn in phases:
            if only is not None and name not in ("INIT",) + tuple(only):
                continue
            kb.new_phase()
            with contextlib.ExitStack() as pctx:
                fn(pctx)
            kb.barrier()
            if upto is not None and name == upto:
                break
        kb.finalize()
    return nc


def _phases(kb, g, ctx):
    nc = kb.nc

    ones_bf = ctx.enter_context(nc.sbuf_tensor("ones_bf", [1, TG], BF16))
    ones_f = ctx.enter_context(nc.sbuf_tensor("ones_f", [1, TG], F32))
    onescol_bf = ctx.enter_context(nc.sbuf_tensor("onescol_bf", [128, 1], BF16))
    twoscol_bf = ctx.enter_context(nc.sbuf_tensor("twoscol_bf", [128, 1], BF16))
    onescol_f = ctx.enter_context(nc.sbuf_tensor("onescol_f", [128, 1], F32))
    tri_ge = ctx.enter_context(nc.sbuf_tensor("tri_ge", [128, 128], BF16))
    tri_le = ctx.enter_context(nc.sbuf_tensor("tri_le", [128, 128], BF16))
    ident = ctx.enter_context(nc.sbuf_tensor("ident", [128, 128], BF16))
    ptrot = ctx.enter_context(nc.sbuf_tensor("ptrot", [128, 128], BF16))
    kb.consts = dict(ones_bf=ones_bf, ones_f=ones_f, onescol_bf=onescol_bf,
                     twoscol_bf=twoscol_bf, onescol_f=onescol_f, tri_ge=tri_ge,
                     tri_le=tri_le, ident=ident, ptrot=ptrot)
    pbE = ctx.enter_context(nc.psum_tensor("pbE", [128, 4 * TG], F32))
    kb.psum = [pbE[:, i * TG:(i + 1) * TG] for i in range(4)]
    kb.psum += [ctx.enter_context(nc.psum_tensor(f"pb{i}", [128, TG], F32))[:, :]
                for i in range(4, 7)]
    kb.psum_quad = pbE
    kb.psum_bf = ctx.enter_context(nc.psum_tensor("pbbf", [128, 1024], BF16))
    kb.bank_free = [None] * 8

    CF_VALS = (1e-5, 0.5, -0.5, 1.0, -1.0)
    cf = ctx.enter_context(nc.sbuf_tensor("cf", [128, len(CF_VALS)], F32))
    for j, v in enumerate(CF_VALS):
        nc.const_aps.register(F32, v, cf[:, j:j + 1])

    def init(pctx):
        for j, v in enumerate(CF_VALS):
            kb.op("gpsimd", lambda e, j=j, v=v: e.memset(cf[:, j:j + 1], v))
        kb.op("gpsimd", lambda e: e.memset(ones_bf[:, :], 1.0))
        kb.op("gpsimd", lambda e: e.memset(ones_f[:, :], 1.0))
        kb.op("gpsimd", lambda e: e.memset(onescol_bf[:, :], 1.0))
        kb.op("gpsimd", lambda e: e.memset(twoscol_bf[:, :], 2.0))
        kb.op("gpsimd", lambda e: e.memset(onescol_f[:, :], 1.0))
        kb.dma("gpsimd", "d_init", lambda e: e.dma_start(out=tri_ge[:, :], in_=g["c_tri_ge"]))
        kb.dma("gpsimd", "d_init", lambda e: e.dma_start(out=tri_le[:, :], in_=g["c_tri_le"]))
        kb.dma("gpsimd", "d_init", lambda e: e.dma_start(out=ident[:, :], in_=g["c_ident"]))
        kb.dma("gpsimd", "d_init", lambda e: e.dma_start(out=ptrot[:, :], in_=g["c_PT"]))

    out = [("INIT", init)]
    out.append(("LN0", lambda pctx: emit_ln(kb, g, pctx, g["x_fm"], g["xh0"], g["xl0"])))
    out.append(("QKV0", lambda pctx: emit_qkv0(kb, g, pctx)))
    out.append(("SEL0", lambda pctx: emit_sel0(kb, g, pctx)))
    out.append(("COMP0", lambda pctx: emit_comp0(kb, g, pctx)))
    out.append(("OCOMP0", lambda pctx: emit_banded(kb, g, pctx, "oc0", 384, 1,
                g["q0T"], g["ck0T"], g["cv0tm"], 1, g["oc0T"])))
    out.append(("OSEL0", lambda pctx: emit_banded(kb, g, pctx, "os0", 1280, NSEL // 128,
                g["q0T"], g["sk0T"], g["sv0tm"], 0, g["os0T"])))
    out.append(("OWIN0", lambda pctx: emit_banded(kb, g, pctx, "ow0", S, S // 128,
                g["q0T"], g["k0T"], g["v0tm"], 0, g["ow0T"])))
    out.append(("GCOMB0", lambda pctx: emit_gcomb0(kb, g, pctx)))
    out.append(("FFN0A", lambda pctx: emit_ffn0a(kb, g, pctx)))
    out.append(("FFN0B", lambda pctx: emit_ffn0b(kb, g, pctx)))
    out.append(("LN1", lambda pctx: emit_ln(kb, g, pctx, g["x1"], g["xh1"], g["xl1"])))
    out.append(("QKV1", lambda pctx: emit_qkv1(kb, g, pctx)))
    out.append(("ATT1", lambda pctx: emit_att1(kb, g, pctx)))
    out.append(("WO1", lambda pctx: emit_linear(kb, g, pctx, "wo1", g["wo1T"], D,
                lambda oc: (g["s1T"], oc), in_hbm=g["o1v"])))
    out.append(("FFN1A", lambda pctx: emit_ffn1a(kb, g, pctx)))
    out.append(("FFN1B", lambda pctx: emit_ffn1b(kb, g, pctx)))
    out.append(("SHFIN1", lambda pctx: emit_shfin1(kb, g, pctx)))
    return out


def load_xin(kb, pctx, tag, in_hbm, cols=S):
    nc = kb.nc
    xin = pctx.enter_context(nc.sbuf_tensor(f"{tag}_xin", [128, NDC * cols], BF16))
    lde = None
    for c in range(NDC):
        lde = kb.dma("sync", f"d_{tag}x", lambda e, c=c: e.dma_start(
            out=xin[:, c * cols:(c + 1) * cols], in_=in_hbm[c, :, 0:cols]))
    return xin, lde


def emit_linear(kb, g, pctx, tag, wT_hbm, nout, out_map, in_hbm=None, xin=None,
                bias_hbm=None, banks=(0, 1, 2, 3, 4, 5), epi=None):
    nc = kb.nc
    E = pctx.enter_context
    OS = 512
    nos = nout // OS
    if xin is None:
        xin, lde = load_xin(kb, pctx, tag, in_hbm)
        kb.wait_edge("tensor", lde)
    wsl = [E(nc.sbuf_tensor(f"{tag}_w{i}", [128, NDC * OS], BF16)) for i in range(2)]
    ones = kb.consts["ones_bf"]
    brow = None
    if bias_hbm is not None:
        brow_f = E(nc.sbuf_tensor(f"{tag}_bf", [1, nout], F32))
        brow = E(nc.sbuf_tensor(f"{tag}_bb", [1, nout], BF16))
        be = kb.dma("sync", f"d_{tag}b", lambda e: e.dma_start(out=brow_f[:, :], in_=bias_hbm))
        kb.wait_edge("vector", be)
        bce = kb.op("vector", lambda e: e.tensor_copy(brow[:, :], brow_f[:, :]))
        kb.wait_edge("tensor", bce)
    obuf = [E(nc.sbuf_tensor(f"{tag}_o{i}", [128, TG], BF16)) for i in range(4)]
    obuf_edge = [None] * 4
    slab_last = [None, None]
    nb = len(banks)
    ngrp = 0
    for osl in range(nos):
        sl = osl % 2
        kb.wait_edge("gpsimd", slab_last[sl])
        we = kb.dma("gpsimd", f"d_{tag}w{sl}", lambda e, osl=osl, sl=sl: e.dma_start(
            out=wsl[sl][:, :].rearrange("p (dc o) -> p dc o", dc=NDC),
            in_=wT_hbm.rearrange("(dc p) o -> p dc o", p=128)[:, :, osl * OS:(osl + 1) * OS]))
        kb.wait_edge("tensor", we)
        for tg in range(NTG):
            for o4 in range(OS // 128):
                oc = osl * 4 + o4
                om = out_map(oc)
                if om is None:
                    continue
                bk = banks[ngrp % nb]
                ngrp += 1
                kb.wait_edge("tensor", kb.bank_free[bk])
                nmm = NDC + (1 if brow is not None else 0)
                for dc in range(NDC):
                    last = (dc == nmm - 1)
                    mme = kb.mm(lambda e, sl=sl, dc=dc, o4=o4, tg=tg, bk=bk, last=last: e.matmul(
                        kb.psum[bk][:, :],
                        wsl[sl][:, dc * OS + o4 * 128: dc * OS + (o4 + 1) * 128],
                        xin[:, dc * S + tg * TG: dc * S + (tg + 1) * TG],
                        start=(dc == 0), stop=last), inc=last)
                if brow is not None:
                    mme = kb.mm(lambda e, oc=oc, bk=bk: e.matmul(
                        kb.psum[bk][:, :], brow[0:1, oc * 128:(oc + 1) * 128],
                        ones[:, :], start=False, stop=True), inc=True)
                slab_last[sl] = mme
                if epi is not None:
                    epi(oc, tg, bk, mme)
                else:
                    ob = ngrp % 4
                    kb.wait_edge("vector", obuf_edge[ob])
                    kb.wait_edge("vector", mme)
                    ve = kb.op("vector", lambda e, ob=ob, bk=bk: e.tensor_copy(
                        obuf[ob][:, :], kb.psum[bk][:, :]))
                    kb.bank_free[bk] = ve
                    kb.wait_edge("sync", ve)
                    ohbm, ocl = om
                    obuf_edge[ob] = kb.dma("sync", f"d_{tag}s{ob}", lambda e, ob=ob, ohbm=ohbm,
                                           ocl=ocl, tg=tg: e.dma_start(
                        out=ohbm[ocl, :, tg * TG:(tg + 1) * TG], in_=obuf[ob][:, :]))
    return xin


def emit_linear_tm(kb, g, pctx, tag, wT_hbm, xin, out_tm_hbm, bias_row=None,
                   banks=(5, 6)):
    nc = kb.nc
    E = pctx.enter_context
    wsb = E(nc.sbuf_tensor(f"{tag}_wtm", [128, NDC * D], BF16))
    we = kb.dma("gpsimd", f"d_{tag}wt", lambda e: e.dma_start(
        out=wsb[:, :].rearrange("p (dc o) -> p dc o", dc=NDC),
        in_=wT_hbm.rearrange("(dc p) o -> p dc o", p=128)))
    kb.wait_edge("tensor", we)
    obuf = [E(nc.sbuf_tensor(f"{tag}_to{i}", [128, TG], BF16)) for i in range(4)]
    obuf_edge = [None] * 4
    onesr = kb.consts["ones_bf"][:, 0:128]
    ngrp = 0
    for t1 in range(NT128):
        for half in range(2):
            bk = banks[ngrp % len(banks)]
            kb.wait_edge("tensor", kb.bank_free[bk])
            for dc in range(NDC):
                last = (dc == NDC - 1 and bias_row is None)
                mme = kb.mm(lambda e, dc=dc, t1=t1, half=half, bk=bk, last=last: e.matmul(
                    kb.psum[bk][:, :],
                    xin[:, dc * S + t1 * 128: dc * S + (t1 + 1) * 128],
                    wsb[:, dc * D + half * TG: dc * D + (half + 1) * TG],
                    start=(dc == 0), stop=last), inc=last)
            if bias_row is not None:
                mme = kb.mm(lambda e, half=half, bk=bk: e.matmul(
                    kb.psum[bk][:, :], onesr, bias_row[0:1, half * TG:(half + 1) * TG],
                    start=False, stop=True), inc=True)
            ob = ngrp % 4
            ngrp += 1
            kb.wait_edge("vector", obuf_edge[ob])
            kb.wait_edge("vector", mme)
            ve = kb.op("vector", lambda e, ob=ob, bk=bk: e.tensor_copy(
                obuf[ob][:, :], kb.psum[bk][:, :]))
            kb.bank_free[bk] = ve
            kb.wait_edge("sync", ve)
            obuf_edge[ob] = kb.dma("sync", f"d_{tag}ts{ob}", lambda e, ob=ob, t1=t1,
                                   half=half: e.dma_start(
                out=out_tm_hbm[t1 * 128:(t1 + 1) * 128, half * TG:(half + 1) * TG],
                in_=obuf[ob][:, :]))


def emit_qkv0(kb, g, pctx):
    def omap(oc):
        if oc < 8:
            return (g["q0T"], oc)
        if oc < 16:
            return (g["k0T"], oc - 8)
        return (g["v0T"], oc - 16)
    xin = emit_linear(kb, g, pctx, "qkv0", g["wqkvT"], 3 * D, omap,
                      in_hbm=g["xh0"], bias_hbm=g["bqkv"])
    nc = kb.nc
    E = pctx.enter_context
    brow_f = E(nc.sbuf_tensor("qkv0tm_bf", [1, 2 * D], F32))
    brow = E(nc.sbuf_tensor("qkv0tm_bb", [1, 2 * D], BF16))
    be = kb.dma("sync", "d_qkv0tmb", lambda e: e.dma_start(
        out=brow_f[:, :], in_=g["bqkv"][:, D:3 * D]))
    kb.wait_edge("vector", be)
    bce = kb.op("vector", lambda e: e.tensor_copy(brow[:, :], brow_f[:, :]))
    kb.wait_edge("tensor", bce)
    kv = g["wqkvT"]
    emit_linear_tm(kb, g, pctx, "k0tm", kv[:, D:2 * D], xin, g["k0tm"],
                   bias_row=brow[:, 0:D])
    emit_linear_tm(kb, g, pctx, "v0tm", kv[:, 2 * D:3 * D], xin, g["v0tm"],
                   bias_row=brow[:, D:2 * D])


SEL0_PARTS = "ABC"


def emit_sel0(kb, g, pctx):
    nc = kb.nc
    E = pctx.enter_context
    tag = "sel0"
    wrow = E(nc.sbuf_tensor(f"{tag}_wrow", [128, NDC * D], F32))
    wsdo = E(nc.sbuf_tensor(f"{tag}_wsdo", [128, NDC], F32))
    weff = E(nc.sbuf_tensor(f"{tag}_weff", [128, 2 * NDC], F32))
    wh = E(nc.sbuf_tensor(f"{tag}_wh", [128, 2 * NDC], BF16))
    wl = E(nc.sbuf_tensor(f"{tag}_wl", [128, 2 * NDC], BF16))
    de = kb.dma("sync", f"d_{tag}ws", lambda e: e.dma_start(out=wsdo[:, :], in_=g["wsdo"]))
    kb.wait_edge("tensor", de)
    for w, src_w in enumerate(("wkrows", "wvrows")):
        kb.wait_edge("sync", kb.bank_free[5])
        lde = None
        for doc in range(NDC):
            lde = kb.dma("sync", f"d_{tag}wr{w}", lambda e, doc=doc, src_w=src_w: e.dma_start(
                out=wrow[:, doc * D:(doc + 1) * D],
                in_=g[src_w].rearrange("(doc p) di -> doc p di", p=128)[doc]))
        kb.wait_edge("tensor", lde)
        ve_prev = kb.bank_free[5]
        for di in range(NDC):
            kb.wait_edge("tensor", ve_prev)
            for doc in range(NDC):
                mme = kb.mm(lambda e, di=di, doc=doc: e.matmul(
                    kb.psum[5][:, 0:1],
                    wrow[:, doc * D + di * 128: doc * D + (di + 1) * 128],
                    wsdo[:, doc:doc + 1], start=(doc == 0), stop=(doc == NDC - 1)),
                    inc=(doc == NDC - 1))
            kb.wait_edge("vector", mme)
            ve_prev = kb.op("vector", lambda e, w=w, di=di: e.tensor_copy(
                weff[:, w * NDC + di: w * NDC + di + 1], kb.psum[5][:, 0:1]))
        kb.bank_free[5] = ve_prev
    he = kb.op("vector", lambda e: e.tensor_copy(wh[:, :], weff[:, :]))
    le = kb.op("vector", lambda e: e.tensor_sub(wl[:, :], weff[:, :], wh[:, :]))
    kb.wait_edge("tensor", le)

    xh = [E(nc.sbuf_tensor(f"{tag}_xh{i}", [128, NDC * TG], BF16)) for i in range(2)]
    xl = [E(nc.sbuf_tensor(f"{tag}_xl{i}", [128, NDC * TG], BF16)) for i in range(2)]
    kt = [E(nc.sbuf_tensor(f"{tag}_kt{i}", [128, NDC * TG], BF16)) for i in range(2)]
    qt = [E(nc.sbuf_tensor(f"{tag}_qt{i}", [128, NDC * TG], BF16)) for i in range(2)]
    sqq = [E(nc.sbuf_tensor(f"{tag}_sqq{i}", [128, TG], BF16)) for i in range(2)]
    sqk = [E(nc.sbuf_tensor(f"{tag}_sqk{i}", [128, TG], BF16)) for i in range(2)]
    scrk = E(nc.sbuf_tensor(f"{tag}_scrk", [1, S], F32))
    scrv = E(nc.sbuf_tensor(f"{tag}_scrv", [1, S], F32))
    qn = E(nc.sbuf_tensor(f"{tag}_qn", [1, S], F32))
    k2 = E(nc.sbuf_tensor(f"{tag}_k2", [1, S], F32))
    kmax = E(nc.sbuf_tensor(f"{tag}_kmax", [1, 2], F32))
    nm = E(nc.sbuf_tensor(f"{tag}_nm", [1, S], BF16))
    onesc = kb.consts["onescol_bf"]

    slot_pe = [None, None]
    sq_ve = [None, None]
    for tg in range(NTG):
        sl = tg % 2
        kb.wait_edge("sync", slot_pe[sl])
        de = None
        for t, hb in ((xh, "xh0"), (xl, "xl0"), (kt, "k0T"), (qt, "q0T")):
            de = kb.dma("sync", f"d_{tag}l{sl}", lambda e, t=t, hb=hb, tg=tg, sl=sl:
                        e.dma_start(out=t[sl][:, :].rearrange("p (dc t) -> p dc t", dc=NDC),
                                    in_=g[hb][:, :, tg * TG:(tg + 1) * TG].rearrange(
                                        "dc p t -> p dc t")))
        kb.wait_edge("vector", de)
        kb.wait_edge("tensor", de)
        for b in range(4):
            kb.wait_edge("tensor", kb.bank_free[b])
        for dc in range(NDC):
            sv = dc % 2
            kb.wait_edge("vector", sq_ve[sv])
            v1 = kb.op("vector", lambda e, sl=sl, dc=dc, sv=sv: e.tensor_mul(
                sqq[sv][:, :], qt[sl][:, dc * TG:(dc + 1) * TG], qt[sl][:, dc * TG:(dc + 1) * TG]))
            v2 = kb.op("vector", lambda e, sl=sl, dc=dc, sv=sv: e.tensor_mul(
                sqk[sv][:, :], kt[sl][:, dc * TG:(dc + 1) * TG], kt[sl][:, dc * TG:(dc + 1) * TG]))
            kb.wait_edge("tensor", v2)
            for w in range(2):
                for ti, (wt_, xt_) in enumerate(((wh, xh), (wh, xl), (wl, xh))):
                    kb.mm(lambda e, sl=sl, dc=dc, w=w, wt_=wt_, xt_=xt_, ti=ti: e.matmul(
                        kb.psum[w][0:1, :], wt_[:, w * NDC + dc:w * NDC + dc + 1],
                        xt_[sl][:, dc * TG:(dc + 1) * TG],
                        start=(dc == 0 and ti == 0), stop=(dc == NDC - 1 and ti == 2)))
            kb.mm(lambda e, sv=sv, dc=dc: e.matmul(kb.psum[2][0:1, :], onesc[:, :],
                                                   sqq[sv][:, :], start=(dc == 0), stop=(dc == NDC - 1)))
            mme = kb.mm(lambda e, sv=sv, dc=dc: e.matmul(kb.psum[3][0:1, :], onesc[:, :],
                                                         sqk[sv][:, :], start=(dc == 0),
                                                         stop=(dc == NDC - 1)), inc=True)
            sq_ve[sv] = mme
            slot_pe[sl] = mme
        kb.wait_edge("vector", mme)
        kb.wait_edge("scalar", mme)
        e0 = kb.op("vector", lambda e, tg=tg: e.tensor_copy(scrk[:, tg * TG:(tg + 1) * TG], kb.psum[0][0:1, :]))
        e1 = kb.op("vector", lambda e, tg=tg: e.tensor_copy(scrv[:, tg * TG:(tg + 1) * TG], kb.psum[1][0:1, :]))
        e2 = kb.op("scalar", lambda e, tg=tg: e.activation(qn[:, tg * TG:(tg + 1) * TG],
                                                           kb.psum[2][0:1, :], AF.Sqrt))
        e3 = kb.op("vector", lambda e, tg=tg: e.tensor_copy(k2[:, tg * TG:(tg + 1) * TG], kb.psum[3][0:1, :]))
        kb.bank_free[0], kb.bank_free[1], kb.bank_free[2], kb.bank_free[3] = e0, e1, e2, e3

    ve = kb.op("vector", lambda e: e.reduce_max(kmax[:, 0:1], k2[:, :], axis=AX.X))
    kb.wait_edge("scalar", ve)
    se = kb.op("scalar", lambda e: e.activation(kmax[:, 1:2], kmax[:, 0:1], AF.Sqrt))
    kb.wait_edge("vector", se)
    kb.wait_edge("vector", e2)
    ve = kb.op("vector", lambda e: e.tensor_scalar(nm[:, :], qn[:, :], kmax[:, 1:2], -1.0,
                                                   ALU.mult, ALU.mult))
    kb.wait_edge("sync", ve)
    kb.dma("sync", "d_selr", lambda e: e.dma_start(out=g["negm"][0:1, :], in_=nm[:, :]))
    kb.wait_edge("sync", e2)
    kb.dma("sync", "d_selr", lambda e: e.dma_start(out=g["qnorm"][0:1, :], in_=qn[:, :]))
    kb.wait_edge("sync", e1)
    kb.dma("sync", "d_selr", lambda e: e.dma_start(
        out=g["scb"][0:1, :, :].rearrange("a n j -> a (n j)"), in_=scrk[:, :]))
    kb.dma("sync", "d_selr", lambda e: e.dma_start(
        out=g["scb"][1:2, :, :].rearrange("a n j -> a (n j)"), in_=scrv[:, :]))
    dsel = kb.cnt["d_selr"]

    if "B" not in SEL0_PARTS:
        return
    sc64 = E(nc.sbuf_tensor(f"{tag}_sc64", [NSB, SBS], F32))
    sc64r = E(nc.sbuf_tensor(f"{tag}_sc64r", [NSB, SBS], F32))
    v8 = E(nc.sbuf_tensor(f"{tag}_v8", [NSB, 8], F32))
    idx16 = E(nc.sbuf_tensor(f"{tag}_idx16", [NSB, TOPK], U32))
    iot = E(nc.sbuf_tensor(f"{tag}_iot", [NSB, TOPK], I32))
    idxg = E(nc.sbuf_tensor(f"{tag}_idxg", [NSB, TOPK], I32))
    ge = kb.op("gpsimd", lambda e: e.iota(iot[:, :], pattern=[[0, TOPK]], base=0,
                                          channel_multiplier=SBS))
    for w in range(2):
        kb.wait("sync", "d_selr", dsel)
        kb.wait("sync", "sd_vector", kb.cnt["sd_vector"])
        le = kb.dma("sync", f"d_{tag}t{w}", lambda e, w=w: e.dma_start(out=sc64[:, :], in_=g["scb"][w]))
        kb.wait_edge("vector", le)
        kb.op("vector", lambda e: e.max(out=v8[:, :], in_=sc64[:, :]))
        kb.op("vector", lambda e: e.max_index(out=idx16[:, 0:8], in_max=v8[:, :], in_values=sc64[:, :]))
        kb.op("vector", lambda e: e.match_replace(out=sc64r[:, :], in_to_replace=v8[:, :],
                                                  in_values=sc64[:, :], imm_value=-1e30))
        kb.op("vector", lambda e: e.max(out=v8[:, :], in_=sc64r[:, :]))
        ie = kb.op("vector", lambda e: e.max_index(out=idx16[:, 8:16], in_max=v8[:, :], in_values=sc64r[:, :]))
        kb.wait_edge("vector", ge)
        ae = kb.op("vector", lambda e: e.tensor_tensor(idxg[:, :], idx16[:, :], iot[:, :], ALU.add))
        kb.wait_edge("sync", ae)
        kb.dma("sync", f"d_{tag}t{w}", lambda e, w=w: e.dma_start(
            out=g["selidx"][w].rearrange("(n r) -> n r", r=TOPK), in_=idxg[:, :]))
        kb.wait("vector", f"d_{tag}t{w}", kb.cnt[f"d_{tag}t{w}"])

    if "C" not in SEL0_PARTS:
        return
    idxs = E(nc.sbuf_tensor(f"{tag}_idxs", [128, NSEL // 128], I32))
    gat = E(nc.sbuf_tensor(f"{tag}_gat", [128, D], BF16))
    tsb = E(nc.sbuf_tensor(f"{tag}_tsb", [128, 128], BF16))
    ident = kb.consts["ident"]
    for w, (src_tm, dst_tm) in enumerate((("k0tm", "sk0tm"), ("v0tm", "sv0tm"))):
        kb.wait("gpsimd", f"d_{tag}t{w}", kb.cnt[f"d_{tag}t{w}"])
        ie = None
        for i in range(NSEL // 128):
            ie = kb.dma("gpsimd", f"d_{tag}gi{w}", lambda e, w=w, i=i: e.dma_start(
                out=idxs[:, i:i + 1],
                in_=g["selidx"][w].rearrange("(i p) -> i p", p=128)[i][:, None]))
        for i in range(NSEL // 128):
            kb.wait_edge("gpsimd", ie)
            ge = kb.dma("gpsimd", f"d_{tag}g{w}", lambda e, i=i, src_tm=src_tm: e.indirect_dma_start(
                out=gat[:, :], out_offset=None, in_=g[src_tm][:, :],
                in_offset=bass.IndirectOffsetOnAxis(ap=idxs[:, i:i + 1], axis=0)))
            kb.wait_edge("sync", ge)
            ste = kb.dma("sync", f"d_{tag}g{w}s", lambda e, i=i, dst_tm=dst_tm: e.dma_start(
                out=g[dst_tm][i * 128:(i + 1) * 128, :], in_=gat[:, :]))
            if w == 0:
                for dc in range(NDC):
                    kb.wait_edge("tensor", ge)
                    kb.wait_edge("tensor", kb.bank_free[7])
                    te = kb.mm(lambda e, dc=dc: e.transpose(
                        kb.psum_bf[:, 0:128], gat[:, dc * 128:(dc + 1) * 128], ident[:, :]),
                        inc=True)
                    kb.wait_edge("vector", te)
                    kb.wait("vector", f"d_{tag}g{w}s", kb.cnt[f"d_{tag}g{w}s"])
                    ve = kb.op("vector", lambda e: e.tensor_copy(tsb[:, :], kb.psum_bf[:, 0:128]))
                    kb.bank_free[7] = ve
                    kb.wait_edge("sync", ve)
                    kb.dma("sync", f"d_{tag}g{w}s", lambda e, dc=dc, i=i: e.dma_start(
                        out=g["sk0T"][dc, :, i * 128:(i + 1) * 128], in_=tsb[:, :]))
            kb.wait("gpsimd", f"d_{tag}g{w}s", kb.cnt[f"d_{tag}g{w}s"])
            if w == 0:
                kb.wait_edge("gpsimd", kb.bank_free[7])


def emit_comp0(kb, g, pctx):
    nc = kb.nc
    E = pctx.enter_context
    tag = "cmp0"
    stg = [E(nc.sbuf_tensor(f"{tag}_st{i}", [128, S], BF16)) for i in range(2)]
    rkv = E(nc.sbuf_tensor(f"{tag}_rkv", [128, NDC * 2 * S], BF16))
    wsl = [E(nc.sbuf_tensor(f"{tag}_w{i}", [128, NDC * TG], BF16)) for i in range(2)]
    bcf = E(nc.sbuf_tensor(f"{tag}_bcf", [1, D], F32))
    bcb = E(nc.sbuf_tensor(f"{tag}_bcb", [1, D], BF16))
    csb = E(nc.sbuf_tensor(f"{tag}_csb", [128, 2 * NB], BF16))
    tsb = E(nc.sbuf_tensor(f"{tag}_tsb", [128, 128], BF16))
    sqt = E(nc.sbuf_tensor(f"{tag}_sqt", [128, NB], BF16))
    ck2 = E(nc.sbuf_tensor(f"{tag}_ck2", [1, NB + 2], F32))
    qnr = E(nc.sbuf_tensor(f"{tag}_qnr", [1, S], F32))
    nm2 = E(nc.sbuf_tensor(f"{tag}_nm2", [1, S], BF16))
    ones = kb.consts["ones_bf"]
    onesc = kb.consts["onescol_bf"]
    ident = kb.consts["ident"]

    be = kb.dma("sync", f"d_{tag}b", lambda e: e.dma_start(out=bcf[:, :], in_=g["bcomp"]))
    kb.wait_edge("vector", be)
    bce = kb.op("vector", lambda e: e.tensor_copy(bcb[:, :], bcf[:, :]))
    qle = kb.dma("sync", f"d_{tag}q", lambda e: e.dma_start(out=qnr[:, :], in_=g["qnorm"][0:1, :]))

    ve = None
    for dc in range(NDC):
        for w, hb in enumerate(("k0T", "v0T")):
            sl = (dc * 2 + w) % 2
            kb.wait_edge("sync", ve)
            le = kb.dma("sync", f"d_{tag}l{sl}", lambda e, hb=hb, dc=dc, sl=sl: e.dma_start(
                out=stg[sl][:, :], in_=g[hb][dc]))
            kb.wait_edge("vector", le)
            ve = kb.op("vector", lambda e, dc=dc, w=w, sl=sl: e.tensor_copy(
                rkv[:, :].rearrange("p (dc j two n) -> p dc j two n", dc=NDC, j=CBS, two=2)[:, dc, :, w, :],
                stg[sl][:, :].rearrange("p (n j) -> p j n", j=CBS)))

    kb.wait_edge("tensor", ve)
    slab_last = [None, None]
    for sw in range(2):
        for b in range(4):
            kb.wait_edge("tensor", kb.bank_free[b])
        for j in range(CBS):
            sl = j % 2
            kb.wait_edge("gpsimd", slab_last[sl])
            we = kb.dma("gpsimd", f"d_{tag}w{sl}", lambda e, j=j, sw=sw, sl=sl: e.dma_start(
                out=wsl[sl][:, :].rearrange("p (dc o) -> p dc o", dc=NDC),
                in_=g["wcompT"][j].rearrange("(dc p) o -> p dc o", p=128)[:, :, sw * TG:(sw + 1) * TG]))
            kb.wait_edge("tensor", we)
            for b in range(4):
                oc = sw * 4 + b
                for dic in range(NDC):
                    mme = kb.mm(lambda e, sl=sl, b=b, dic=dic, j=j: e.matmul(
                        kb.psum[b][:, 0:2 * NB],
                        wsl[sl][:, dic * TG + b * 128:(dic * TG + (b + 1) * 128)],
                        rkv[:, :].rearrange("p (dc r) -> p dc r", dc=NDC)[:, dic,
                            j * 2 * NB:(j + 1) * 2 * NB],
                        start=(j == 0 and dic == 0), stop=False),
                        inc=(dic == NDC - 1))
            slab_last[sl] = mme
        for b in range(4):
            oc = sw * 4 + b
            kb.wait_edge("tensor", bce)
            mme = kb.mm(lambda e, b=b, oc=oc: e.matmul(
                kb.psum[b][:, 0:2 * NB], bcb[0:1, oc * 128:(oc + 1) * 128], ones[:, 0:2 * NB],
                start=False, stop=True), inc=True)
            kb.wait_edge("vector", mme)
            ve = kb.op("vector", lambda e, b=b: e.tensor_copy(csb[:, :], kb.psum[b][:, 0:2 * NB]))
            kb.bank_free[b] = ve
            kb.wait_edge("sync", ve)
            kb.dma("sync", f"d_{tag}s", lambda e, oc=oc: e.dma_start(
                out=g["ck0T"][oc], in_=csb[:, 0:NB]))
            kb.dma("sync", f"d_{tag}s", lambda e, oc=oc: e.dma_start(
                out=g["cv0T"][oc], in_=csb[:, NB:2 * NB]))
            kb.wait_edge("scalar", ve)
            sqe = kb.op("scalar", lambda e: e.activation(sqt[:, :], csb[:, 0:NB], AF.Square))
            kb.wait_edge("tensor", sqe)
            if oc == 0:
                kb.wait_edge("tensor", kb.bank_free[4])
            sme = kb.mm(lambda e, oc=oc: e.matmul(kb.psum[4][0:1, 0:NB], onesc[:, :], sqt[:, :],
                                                  start=(oc == 0), stop=(oc == 7)),
                        inc=(oc == 7))
            for w, dst in enumerate(("ck0tm", "cv0tm")):
                kb.wait_edge("tensor", kb.bank_free[7])
                te = kb.mm(lambda e, w=w: e.transpose(
                    kb.psum_bf[:, 0:128], csb[:, w * NB:(w + 1) * NB], ident[:, :]), inc=True)
                kb.wait_edge("vector", te)
                kb.wait("vector", f"d_{tag}s", kb.cnt[f"d_{tag}s"])
                ve2 = kb.op("vector", lambda e: e.tensor_copy(tsb[:, :], kb.psum_bf[:, 0:128]))
                kb.bank_free[7] = ve2
                kb.wait_edge("sync", ve2)
                kb.dma("sync", f"d_{tag}s", lambda e, dst=dst, oc=oc: e.dma_start(
                    out=g[dst][:, oc * 128:(oc + 1) * 128], in_=tsb[:, :]))
            kb.wait("vector", f"d_{tag}s", kb.cnt[f"d_{tag}s"])
            kb.wait_edge("vector", sqe)
            kb.wait_edge("vector", ("sd_pe", kb.cnt["sd_pe"]))
    kb.wait_edge("vector", sme)
    ve = kb.op("vector", lambda e: e.reduce_max(ck2[:, NB:NB + 1], kb.psum[4][0:1, 0:NB], axis=AX.X))
    kb.bank_free[4] = ve
    kb.wait_edge("scalar", ve)
    se = kb.op("scalar", lambda e: e.activation(ck2[:, NB + 1:NB + 2], ck2[:, NB:NB + 1], AF.Sqrt))
    kb.wait_edge("vector", se)
    kb.wait("vector", f"d_{tag}q", kb.cnt[f"d_{tag}q"])
    ve = kb.op("vector", lambda e: e.tensor_scalar(nm2[:, :], qnr[:, :], ck2[:, NB + 1:NB + 2], -1.0,
                                                   ALU.mult, ALU.mult))
    kb.wait_edge("sync", ve)
    kb.dma("sync", f"d_{tag}s", lambda e: e.dma_start(out=g["negm"][1:2, :], in_=nm2[:, :]))


def emit_ln(kb, g, pctx, x_hbm, xh_hbm, xl_hbm):
    nc = kb.nc
    E = pctx.enter_context
    ph = "ln_" + xh_hbm.tensor.name
    xt = [E(nc.sbuf_tensor(f"{ph}_x{i}", [128, NDC * TG], F32)) for i in range(2)]
    xb = E(nc.sbuf_tensor(f"{ph}_xb", [128, NDC * TG], BF16))
    xlo = E(nc.sbuf_tensor(f"{ph}_xlo", [128, NDC * TG], BF16))
    sq = E(nc.sbuf_tensor(f"{ph}_sq", [128, NDC * TG], BF16))
    xp = E(nc.sbuf_tensor(f"{ph}_xp", [128, NDC * TG], BF16))
    rows = E(nc.sbuf_tensor(f"{ph}_rows", [1, 6 * TG], F32))
    xnf = E(nc.sbuf_tensor(f"{ph}_xnf", [128, NDC * TG], F32))
    xh = E(nc.sbuf_tensor(f"{ph}_xh", [128, NDC * TG], BF16))
    xl = E(nc.sbuf_tensor(f"{ph}_xl", [128, NDC * TG], BF16))
    onesc = kb.consts["onescol_bf"]
    twosc = kb.consts["twoscol_bf"]
    onesrow_f = kb.consts["ones_f"][:, 0:128]
    mean = rows[:, 0 * TG:1 * TG]
    m2 = rows[:, 1 * TG:2 * TG]
    var = rows[:, 2 * TG:3 * TG]
    sd = rows[:, 3 * TG:4 * TG]
    rcp = rows[:, 4 * TG:5 * TG]
    mrs = rows[:, 5 * TG:6 * TG]

    prev_mm = None
    st_edges = []
    for tg in range(NTG):
        sl = tg % 2
        kb.wait_edge("sync", st_edges[tg - 2][1] if tg >= 2 else None)
        ld = kb.dma("sync", f"d_{ph}{sl}", lambda e, tg=tg, sl=sl: e.dma_start(
            out=xt[sl][:, :].rearrange("p (c t) -> p c t", c=NDC),
            in_=x_hbm[:, :, tg * TG:(tg + 1) * TG].rearrange("c p t -> p c t")))
        kb.wait_edge("vector", ld)
        kb.wait_edge("vector", prev_mm)
        kb.wait_edge("scalar", prev_mm)
        ce = se = None
        for c in range(NDC):
            ce0 = kb.op("vector", lambda e, c=c, sl=sl: e.tensor_copy(
                xb[:, c * TG:(c + 1) * TG], xt[sl][:, c * TG:(c + 1) * TG]))
            kb.op("vector", lambda e, c=c, sl=sl: e.tensor_sub(
                xlo[:, c * TG:(c + 1) * TG], xt[sl][:, c * TG:(c + 1) * TG],
                xb[:, c * TG:(c + 1) * TG]))
            ce = kb.op("vector", lambda e, c=c: e.tensor_mul(
                xp[:, c * TG:(c + 1) * TG], xb[:, c * TG:(c + 1) * TG],
                xlo[:, c * TG:(c + 1) * TG]))
            kb.wait_edge("scalar", ce0)
            se = kb.op("scalar", lambda e, c=c: e.activation(
                sq[:, c * TG:(c + 1) * TG], xb[:, c * TG:(c + 1) * TG], AF.Square))
        kb.wait_edge("tensor", se)
        kb.wait_edge("tensor", ce)
        kb.wait_edge("tensor", kb.bank_free[0])
        kb.wait_edge("tensor", kb.bank_free[1])
        for c in range(NDC):
            kb.mm(lambda e, c=c: e.matmul(kb.psum[0][0:1, :], onesc[:, :],
                                          xb[:, c * TG:(c + 1) * TG],
                                          start=(c == 0), stop=False))
            kb.mm(lambda e, c=c: e.matmul(kb.psum[0][0:1, :], onesc[:, :],
                                          xlo[:, c * TG:(c + 1) * TG],
                                          start=False, stop=(c == NDC - 1)))
            kb.mm(lambda e, c=c: e.matmul(kb.psum[1][0:1, :], onesc[:, :],
                                          sq[:, c * TG:(c + 1) * TG],
                                          start=(c == 0), stop=False))
            prev_mm = kb.mm(lambda e, c=c: e.matmul(kb.psum[1][0:1, :], twosc[:, :],
                                                    xp[:, c * TG:(c + 1) * TG],
                                                    start=False, stop=(c == NDC - 1)),
                            inc=(c == NDC - 1))
        kb.wait_edge("vector", prev_mm)
        kb.op("vector", lambda e: e.tensor_scalar_mul(mean, kb.psum[0][0:1, :], 1.0 / D))
        kb.op("vector", lambda e: e.tensor_scalar_mul(m2, kb.psum[1][0:1, :], 1.0 / D))
        kb.op("vector", lambda e: e.tensor_tensor(var, mean, mean, ALU.mult))
        ve = kb.op("vector", lambda e: e.tensor_sub(var, m2, var))
        kb.bank_free[0] = kb.bank_free[1] = ve
        kb.wait_edge("scalar", ve)
        se = kb.op("scalar", lambda e: e.activation(sd, var, AF.Sqrt, bias=1e-5))
        kb.wait_edge("vector", se)
        kb.op("vector", lambda e: e.reciprocal(rcp, sd))
        ve = kb.op("vector", lambda e: e.tensor_mul(mrs, mean, rcp))
        kb.wait_edge("tensor", ve)
        kb.wait_edge("tensor", kb.bank_free[2])
        kb.wait_edge("tensor", kb.bank_free[3])
        kb.mm(lambda e: e.matmul(kb.psum[2][:, :], onesrow_f, rcp, start=True, stop=True))
        bce = kb.mm(lambda e: e.matmul(kb.psum[3][:, :], onesrow_f, mrs, start=True, stop=True),
                    inc=True)
        kb.wait_edge("vector", bce)
        kb.wait("vector", "sd_scalar", kb.cnt["sd_scalar"])
        kb.wait("vector", "sd_gpsimd", kb.cnt["sd_gpsimd"])
        kb.wait("scalar", f"d_{ph}s", kb.cnt.get(f"d_{ph}s", 0))
        kb.wait("gpsimd", f"d_{ph}s", kb.cnt.get(f"d_{ph}s", 0))
        sub_e = []
        for c in range(NDC):
            kb.op("vector", lambda e, c=c, sl=sl: e.tensor_mul(
                xnf[:, c * TG:(c + 1) * TG], xt[sl][:, c * TG:(c + 1) * TG], kb.psum[2][:, :]))
            sub_e.append(kb.op("vector", lambda e, c=c: e.tensor_sub(
                xnf[:, c * TG:(c + 1) * TG], xnf[:, c * TG:(c + 1) * TG], kb.psum[3][:, :])))
        he_l = []
        for c in range(NDC):
            kb.wait_edge("scalar", sub_e[c])
            he = kb.op("scalar", lambda e, c=c: e.copy(
                xh[:, c * TG:(c + 1) * TG], xnf[:, c * TG:(c + 1) * TG]))
            he_l.append(he)
        for c in range(NDC):
            kb.wait_edge("gpsimd", he_l[c])
            le = kb.op("gpsimd", lambda e, c=c: e.tensor_sub(
                xl[:, c * TG:(c + 1) * TG], xnf[:, c * TG:(c + 1) * TG],
                xh[:, c * TG:(c + 1) * TG]))
        kb.wait_edge("sync", he_l[-1])
        kb.wait_edge("sync", le)
        kb.dma("sync", f"d_{ph}s", lambda e, tg=tg: e.dma_start(
            out=xh_hbm[:, :, tg * TG:(tg + 1) * TG].rearrange("c p t -> p c t"),
            in_=xh[:, :].rearrange("p (c t) -> p c t", c=NDC)))
        kb.dma("sync", f"d_{ph}s", lambda e, tg=tg: e.dma_start(
            out=xl_hbm[:, :, tg * TG:(tg + 1) * TG].rearrange("c p t -> p c t"),
            in_=xl[:, :].rearrange("p (c t) -> p c t", c=NDC)))
        st_edges.append(((f"d_{ph}s", kb.cnt[f"d_{ph}s"]), sub_e[-1]))
        kb.bank_free[2] = kb.bank_free[3] = sub_e[-1]


def emit_banded(kb, g, pctx, tag, nq, kv_nc, qT_hbm, kT_hbm, vtm_hbm, negm_row, out_hbm):
    nc = kb.nc
    E = pctx.enter_context
    MAXKC = min(kv_nc, 8)
    q_sb = [E(nc.sbuf_tensor(f"{tag}_q{i}", [128, NDC * TG], BF16)) for i in range(2)]
    k_sb = [E(nc.sbuf_tensor(f"{tag}_k{i}", [128, NDC * MAXKC * 128], BF16)) for i in range(2)]
    v_sb = [E(nc.sbuf_tensor(f"{tag}_v{i}", [128, MAXKC * D], BF16)) for i in range(2)]
    e_sb = E(nc.sbuf_tensor(f"{tag}_e", [128, MAXKC * TG], BF16))
    mcol = E(nc.sbuf_tensor(f"{tag}_mc", [128, 8], F32))
    nmr = E(nc.sbuf_tensor(f"{tag}_nmr", [1, TG], BF16))
    mrow_f = E(nc.sbuf_tensor(f"{tag}_mrf", [1, TG], F32))
    rs_sb = E(nc.sbuf_tensor(f"{tag}_rs", [1, TG], F32))
    rb_sb = E(nc.sbuf_tensor(f"{tag}_rb", [128, TG], F32))
    o_sb = [E(nc.sbuf_tensor(f"{tag}_o{i}", [128, TG], BF16)) for i in range(4)]
    onesc = kb.consts["onescol_bf"]
    onesrow_b = kb.consts["ones_bf"][:, 0:128]
    onesrow_f = kb.consts["ones_f"][:, 0:128]
    tri_ge, tri_le = kb.consts["tri_ge"], kb.consts["tri_le"]

    ngr = (nq + TG - 1) // TG
    slot_done = [None, None]
    o_store = [None] * 4
    for qg in range(ngr):
        qlo = qg * TG
        W = min(TG, nq - qlo)
        ns = W // 128
        s_first, s_last = qlo // 128, (qlo + W - 1) // 128
        kclo = max(0, s_first - 2)
        kchi = min(kv_nc - 1, s_last + 2)
        nkc = kchi - kclo + 1
        sl = qg % 2
        kb.wait_edge("sync", slot_done[sl])
        lde = kb.dma("sync", f"d_{tag}q{sl}", lambda e, qlo=qlo, W=W, sl=sl: e.dma_start(
            out=q_sb[sl][:, :].rearrange("p (dc t) -> p dc t", t=TG)[:, :, 0:W],
            in_=qT_hbm[:, :, qlo:qlo + W].rearrange("dc p t -> p dc t")))
        lde = kb.dma("sync", f"d_{tag}q{sl}", lambda e, kclo=kclo, nkc=nkc, sl=sl: e.dma_start(
            out=k_sb[sl][:, :].rearrange("p (dc t) -> p dc t", t=MAXKC * 128)[:, :, 0:nkc * 128],
            in_=kT_hbm[:, :, kclo * 128:(kclo + nkc) * 128].rearrange("dc p t -> p dc t")))
        lde = kb.dma("sync", f"d_{tag}q{sl}", lambda e, kclo=kclo, nkc=nkc, sl=sl: e.dma_start(
            out=v_sb[sl][:, 0:nkc * D].rearrange("p (kc d) -> p kc d", d=D),
            in_=vtm_hbm[kclo * 128:(kclo + nkc) * 128, :].rearrange("(kc p) d -> p kc d", p=128)))
        kb.wait_edge("tensor", lde)

        for si in range(ns):
            s = s_first + si
            kcs = [kc for kc in range(kclo, kchi + 1) if abs(kc - s) <= 2]
            for ki, kc in enumerate(kcs):
                bk = 3 + (ki % 2)
                kb.wait_edge("tensor", kb.bank_free[bk])
                for dc in range(NDC):
                    mqe = kb.mm(lambda e, dc=dc, s=s, kc=kc, sl=sl, bk=bk, qlo=qlo, kclo=kclo: e.matmul(
                        kb.psum[bk][:, 0:128],
                        q_sb[sl][:, dc * TG + s * 128 - qlo: dc * TG + (s + 1) * 128 - qlo],
                        k_sb[sl][:, dc * MAXKC * 128 + (kc - kclo) * 128:
                                 dc * MAXKC * 128 + (kc - kclo + 1) * 128],
                        start=(dc == 0), stop=(dc == NDC - 1)), inc=(dc == NDC - 1))
                kb.wait_edge("vector", mqe)
                if ki == 0:
                    ve = kb.op("vector", lambda e, si=si, bk=bk: e.reduce_max(
                        mcol[:, si:si + 1], kb.psum[bk][:, 0:128], axis=AX.X))
                else:
                    ve = kb.op("vector", lambda e, bk=bk: e.reduce_max(
                        mcol[:, 7:8], kb.psum[bk][:, 0:128], axis=AX.X))
                    ve = kb.op("vector", lambda e, si=si: e.tensor_max(
                        mcol[:, si:si + 1], mcol[:, si:si + 1], mcol[:, 7:8]))
                kb.bank_free[bk] = ve
            kb.wait_edge("sync", ve)
            kb.dma("sync", f"d_{tag}mc", lambda e, si=si, s=s: e.dma_start(
                out=g["mrow"][0:1, s * 128:(s + 1) * 128],
                in_=mcol[:, si:si + 1]))
        kb.wait("sync", f"d_{tag}mc", kb.cnt[f"d_{tag}mc"])
        mre = kb.dma("sync", f"d_{tag}mr", lambda e, qlo=qlo, W=W: e.dma_start(
            out=mrow_f[:, 0:W], in_=g["mrow"][0:1, qlo:qlo + W]))
        kb.wait_edge("vector", mre)
        nve = kb.op("vector", lambda e, W=W: e.tensor_scalar_mul(nmr[:, 0:W], mrow_f[:, 0:W], -1.0))
        kb.wait_edge("tensor", nve)

        exp_edges = []
        for kci in range(nkc):
            kc = kclo + kci
            a = max(0, (kc - 2) * 128 - qlo)
            b = min(W, (kc + 3) * 128 - qlo)
            eb = kb.psum[kci % 2]
            if a > 0 or b < W:
                me = kb.op("vector", lambda e, kci=kci, W=W: e.memset(e_sb[:, kci * TG: kci * TG + W], 0.0))
            kb.wait_edge("tensor", kb.bank_free[kci % 2])
            for dc in range(NDC):
                kb.mm(lambda e, dc=dc, kci=kci, a=a, b=b, sl=sl, eb=eb: e.matmul(
                    eb[:, 0:b - a],
                    k_sb[sl][:, dc * MAXKC * 128 + kci * 128: dc * MAXKC * 128 + (kci + 1) * 128],
                    q_sb[sl][:, dc * TG + a: dc * TG + b], start=(dc == 0), stop=False))
            mme = kb.mm(lambda e, a=a, b=b, eb=eb: e.matmul(
                eb[:, 0:b - a], onesrow_b, nmr[:, a:b],
                start=False, stop=True), inc=True)
            kb.wait_edge("scalar", mme)
            if a > 0 or b < W:
                kb.wait_edge("scalar", me)
            xe = kb.op("scalar", lambda e, kci=kci, a=a, b=b, eb=eb: e.activation(
                e_sb[:, kci * TG + a: kci * TG + b], eb[:, 0:b - a], AF.Exp, scale=SCALE0))
            kb.bank_free[kci % 2] = xe
            exp_edges.append(xe)
            for s_m, tri in ((kc - 2, tri_le), (kc + 2, tri_ge)):
                if s_m < s_first or s_m > s_last:
                    continue
                off = s_m * 128 - qlo
                kb.wait_edge("vector", xe)
                kb.op("vector", lambda e, kci=kci, off=off, tri=tri: e.tensor_mul(
                    e_sb[:, kci * TG + off: kci * TG + off + 128],
                    e_sb[:, kci * TG + off: kci * TG + off + 128], tri[:, :]))
        kb.wait("tensor", "sd_scalar", kb.cnt["sd_scalar"])
        kb.wait("tensor", "sd_vector", kb.cnt["sd_vector"])
        kb.wait_edge("tensor", kb.bank_free[2])
        for kci in range(nkc):
            sme = kb.mm(lambda e, kci=kci, W=W, nkc=nkc: e.matmul(
                kb.psum[2][0:1, 0:W], onesc[:, :], e_sb[:, kci * TG: kci * TG + W],
                start=(kci == 0), stop=(kci == nkc - 1)), inc=(kci == nkc - 1))
        kb.wait_edge("vector", sme)
        re = kb.op("vector", lambda e, W=W: e.reciprocal(rs_sb[:, 0:W], kb.psum[2][0:1, 0:W]))
        kb.bank_free[2] = re
        kb.wait_edge("tensor", re)
        kb.wait_edge("tensor", kb.bank_free[0])
        bce = kb.mm(lambda e, W=W: e.matmul(kb.psum[0][:, 0:W], onesrow_f, rs_sb[:, 0:W],
                                            start=True, stop=True), inc=True)
        kb.wait_edge("scalar", bce)
        ce = kb.op("scalar", lambda e, W=W: e.copy(rb_sb[:, 0:W], kb.psum[0][:, 0:W]))
        kb.bank_free[0] = ce
        pe_last = None
        for half in range(2):
            for eo in range(4):
                ec = half * 4 + eo
                bk = 3 + eo
                kb.wait_edge("tensor", kb.bank_free[bk])
                for kci in range(nkc):
                    pe_last = kb.mm(lambda e, kci=kci, ec=ec, sl=sl, W=W, bk=bk, nkc=nkc: e.matmul(
                        kb.psum[bk][:, 0:W], v_sb[sl][:, kci * D + ec * 128: kci * D + (ec + 1) * 128],
                        e_sb[:, kci * TG: kci * TG + W],
                        start=(kci == 0), stop=(kci == nkc - 1)), inc=(kci == nkc - 1))
                ob = eo if half == 0 else eo
                ob = (half * 4 + eo) % 4
                kb.wait_edge("vector", pe_last)
                kb.wait_edge("vector", ce)
                kb.wait_edge("vector", o_store[ob])
                ve = kb.op("vector", lambda e, bk=bk, ob=ob, W=W: e.tensor_mul(
                    o_sb[ob][:, 0:W], kb.psum[bk][:, 0:W], rb_sb[:, 0:W]))
                kb.bank_free[bk] = ve
                kb.wait_edge("sync", ve)
                o_store[ob] = kb.dma("sync", f"d_{tag}o{ob}", lambda e, ec=ec, ob=ob, qlo=qlo,
                                     W=W: e.dma_start(
                    out=out_hbm[ec, :, qlo:qlo + W], in_=o_sb[ob][:, 0:W]))
        slot_done[sl] = pe_last


def emit_gcomb0(kb, g, pctx):
    nc = kb.nc
    E = pctx.enter_context
    tag = "gc0"
    xin, lde = load_xin(kb, pctx, tag, g["xh0"])
    kb.wait_edge("tensor", lde)
    wg_f = E(nc.sbuf_tensor(f"{tag}_wgf", [128, NDC * 3], F32))
    wg_b = E(nc.sbuf_tensor(f"{tag}_wgb", [128, NDC * 3], BF16))
    bg_f = E(nc.sbuf_tensor(f"{tag}_bgf", [1, 3], F32))
    bg_b = E(nc.sbuf_tensor(f"{tag}_bgb", [1, 3], BF16))
    wo_sb = E(nc.sbuf_tensor(f"{tag}_wo", [128, NDC * D], BF16))
    bo_f = E(nc.sbuf_tensor(f"{tag}_bof", [1, D], F32))
    bo_b = E(nc.sbuf_tensor(f"{tag}_bob", [1, D], BF16))
    grow = [E(nc.sbuf_tensor(f"{tag}_gr{c}", [1, TG], F32)) for c in range(3)]
    gb = [E(nc.sbuf_tensor(f"{tag}_gb{c}", [128, TG], F32)) for c in range(3)]
    ot = [E(nc.sbuf_tensor(f"{tag}_ot{c}", [128, TG], BF16)) for c in range(3)]
    tt = [E(nc.sbuf_tensor(f"{tag}_tt{c}", [128, TG], F32)) for c in range(2)]
    comb = E(nc.sbuf_tensor(f"{tag}_cb", [128, NDC * TG], BF16))
    o_sb = [E(nc.sbuf_tensor(f"{tag}_o{i}", [128, TG], BF16)) for i in range(4)]
    ones = kb.consts["ones_bf"]
    onesrow_f = kb.consts["ones_f"][:, 0:128]
    e1 = kb.dma("gpsimd", f"d_{tag}w", lambda e: e.dma_start(
        out=wg_f[:, :], in_=g["wgateT"].rearrange("p dc c -> p (dc c)")))
    e2 = kb.dma("sync", f"d_{tag}b", lambda e: e.dma_start(out=bg_f[:, :], in_=g["bgate"]))
    e3 = kb.dma("sync", f"d_{tag}b", lambda e: e.dma_start(out=bo_f[:, :], in_=g["bo"]))
    kb.dma("gpsimd", f"d_{tag}w", lambda e: e.dma_start(
        out=wo_sb[:, :].rearrange("p (dc o) -> p dc o", dc=NDC),
        in_=g["woT"].rearrange("(dc p) o -> p dc o", p=128)))
    kb.wait("vector", f"d_{tag}w", kb.cnt[f"d_{tag}w"])
    kb.wait("vector", f"d_{tag}b", kb.cnt[f"d_{tag}b"])
    c1 = kb.op("vector", lambda e: e.tensor_copy(wg_b[:, :], wg_f[:, :]))
    c2 = kb.op("vector", lambda e: e.tensor_copy(bg_b[:, :], bg_f[:, :]))
    c3 = kb.op("vector", lambda e: e.tensor_copy(bo_b[:, :], bo_f[:, :]))
    kb.wait_edge("tensor", c3)

    o_store = [None] * 4
    oc_loads = None
    for tg in range(NTG):
        for c in range(3):
            kb.wait_edge("tensor", kb.bank_free[c])
            for dc in range(NDC):
                kb.mm(lambda e, c=c, dc=dc, tg=tg: e.matmul(
                    kb.psum[c][0:1, :], wg_b[:, dc * 3 + c: dc * 3 + c + 1],
                    xin[:, dc * S + tg * TG:(dc * S + (tg + 1) * TG)],
                    start=(dc == 0), stop=False))
            gme = kb.mm(lambda e, c=c: e.matmul(
                kb.psum[c][0:1, :], bg_b[0:1, c:c + 1], ones[:, :],
                start=False, stop=True), inc=True)
        kb.wait_edge("scalar", gme)
        for c in range(3):
            se = kb.op("scalar", lambda e, c=c: e.activation(grow[c][:, :], kb.psum[c][0:1, :],
                                                             AF.Sigmoid))
            kb.bank_free[c] = se
        kb.wait_edge("tensor", se)
        for c in range(3):
            kb.wait_edge("tensor", kb.bank_free[3 + c])
            be = kb.mm(lambda e, c=c: e.matmul(kb.psum[3 + c][:, :], onesrow_f, grow[c][:, :],
                                               start=True, stop=True), inc=True)
        kb.wait_edge("scalar", be)
        kb.wait("scalar", "sd_vector", kb.cnt["sd_vector"])
        for c in range(3):
            ce = kb.op("scalar", lambda e, c=c: e.copy(gb[c][:, :], kb.psum[3 + c][:, :]))
            kb.bank_free[3 + c] = ce
        nb_r = 384 - tg * TG
        ns_r = 1280 - tg * TG
        kb.wait("vector", "sd_pe", kb.cnt.get("sd_pe", 0))
        for dc in range(NDC):
            lw = kb.dma("sync", f"d_{tag}ow", lambda e, dc=dc, tg=tg: e.dma_start(
                out=ot[2][:, :], in_=g["ow0T"][dc, :, tg * TG:(tg + 1) * TG]))
            if ns_r > 0:
                lw = kb.dma("sync", f"d_{tag}ow", lambda e, dc=dc, tg=tg, w=min(TG, ns_r): e.dma_start(
                    out=ot[1][:, 0:w], in_=g["os0T"][dc, :, tg * TG: tg * TG + w]))
            if nb_r > 0:
                lw = kb.dma("sync", f"d_{tag}ow", lambda e, dc=dc, tg=tg, w=min(TG, nb_r): e.dma_start(
                    out=ot[0][:, 0:w], in_=g["oc0T"][dc, :, tg * TG: tg * TG + w]))
            kb.wait_edge("vector", lw)
            kb.wait_edge("vector", ce)
            ve = kb.op("vector", lambda e, dc=dc: e.tensor_mul(tt[0][:, :], ot[2][:, :], gb[2][:, :]))
            if ns_r > 0:
                w = min(TG, ns_r)
                kb.op("vector", lambda e, w=w: e.tensor_mul(tt[1][:, 0:w], ot[1][:, 0:w], gb[1][:, 0:w]))
                ve = kb.op("vector", lambda e, w=w: e.tensor_add(tt[0][:, 0:w], tt[0][:, 0:w], tt[1][:, 0:w]))
            if nb_r > 0:
                w = min(TG, nb_r)
                kb.op("vector", lambda e, w=w: e.tensor_mul(tt[1][:, 0:w], ot[0][:, 0:w], gb[0][:, 0:w]))
                ve = kb.op("vector", lambda e, w=w: e.tensor_add(tt[0][:, 0:w], tt[0][:, 0:w], tt[1][:, 0:w]))
            ve = kb.op("vector", lambda e, dc=dc: e.tensor_copy(comb[:, dc * TG:(dc + 1) * TG], tt[0][:, :]))
            kb.wait("sync", "sd_vector", kb.cnt["sd_vector"])
        kb.wait_edge("tensor", ve)
        for oc in range(NDC):
            bk = oc % 6
            kb.wait_edge("tensor", kb.bank_free[bk])
            for dc in range(NDC):
                kb.mm(lambda e, oc=oc, dc=dc, bk=bk: e.matmul(
                    kb.psum[bk][:, :], wo_sb[:, dc * D + oc * 128: dc * D + (oc + 1) * 128],
                    comb[:, dc * TG:(dc + 1) * TG], start=(dc == 0), stop=False))
            mme = kb.mm(lambda e, oc=oc, bk=bk: e.matmul(
                kb.psum[bk][:, :], bo_b[0:1, oc * 128:(oc + 1) * 128], ones[:, :],
                start=False, stop=True), inc=True)
            ob = oc % 4
            kb.wait_edge("vector", mme)
            kb.wait_edge("vector", o_store[ob])
            ve2 = kb.op("vector", lambda e, ob=ob, bk=bk: e.tensor_copy(o_sb[ob][:, :], kb.psum[bk][:, :]))
            kb.bank_free[bk] = ve2
            kb.wait_edge("sync", ve2)
            o_store[ob] = kb.dma("sync", f"d_{tag}s{ob}", lambda e, oc=oc, ob=ob, tg=tg: e.dma_start(
                out=g["s0T"][oc, :, tg * TG:(tg + 1) * TG], in_=o_sb[ob][:, :]))


def emit_router(kb, g, pctx, tag, xh_hbm, xl_hbm, wgT_name, lg_hbm):
    nc = kb.nc
    E = pctx.enter_context
    wg_f = E(nc.sbuf_tensor(f"{tag}_wgf", [128, NDC * NEXP], F32))
    wg_h = E(nc.sbuf_tensor(f"{tag}_wgh", [128, NDC * NEXP], BF16))
    wg_l = E(nc.sbuf_tensor(f"{tag}_wgl", [128, NDC * NEXP], BF16))
    xh = [E(nc.sbuf_tensor(f"{tag}_xh{i}", [128, TG], BF16)) for i in range(2)]
    xl = [E(nc.sbuf_tensor(f"{tag}_xl{i}", [128, TG], BF16)) for i in range(2)]
    lg_sb = E(nc.sbuf_tensor(f"{tag}_lg", [NEXP, TG], F32))
    we = kb.dma("sync", f"d_{tag}w", lambda e: e.dma_start(
        out=wg_f[:, :], in_=g[wgT_name].rearrange("p dc c -> p (dc c)")))
    kb.wait_edge("vector", we)
    kb.op("vector", lambda e: e.tensor_copy(wg_h[:, :], wg_f[:, :]))
    ce = kb.op("vector", lambda e: e.tensor_sub(wg_l[:, :], wg_f[:, :], wg_h[:, :]))
    kb.wait_edge("tensor", ce)
    slot_pe = [None, None]
    for tg in range(NTG):
        for dc in range(NDC):
            sl = dc % 2
            kb.wait_edge("sync", slot_pe[sl])
            de = kb.dma("sync", f"d_{tag}l{sl}", lambda e, dc=dc, tg=tg, sl=sl: e.dma_start(
                out=xh[sl][:, :], in_=xh_hbm[dc, :, tg * TG:(tg + 1) * TG]))
            de = kb.dma("sync", f"d_{tag}l{sl}", lambda e, dc=dc, tg=tg, sl=sl: e.dma_start(
                out=xl[sl][:, :], in_=xl_hbm[dc, :, tg * TG:(tg + 1) * TG]))
            kb.wait_edge("tensor", de)
            if dc == 0:
                kb.wait_edge("tensor", kb.bank_free[6])
            for ti, (wt_, xt_) in enumerate(((wg_h, xh), (wg_h, xl), (wg_l, xh))):
                mme = kb.mm(lambda e, dc=dc, sl=sl, ti=ti, wt_=wt_, xt_=xt_: e.matmul(
                    kb.psum[6][0:NEXP, :], wt_[:, dc * NEXP:(dc + 1) * NEXP], xt_[sl][:, :],
                    start=(dc == 0 and ti == 0), stop=(dc == NDC - 1 and ti == 2)),
                    inc=(ti == 2))
            slot_pe[sl] = mme
        kb.wait_edge("vector", mme)
        ve = kb.op("vector", lambda e: e.tensor_copy(lg_sb[:, :], kb.psum[6][0:NEXP, :]))
        kb.bank_free[6] = ve
        kb.wait_edge("sync", ve)
        kb.dma("sync", f"d_{tag}s", lambda e, tg=tg: e.dma_start(
            out=lg_hbm[:, tg * TG:(tg + 1) * TG], in_=lg_sb[:, :]))
        kb.wait("vector", f"d_{tag}s", kb.cnt[f"d_{tag}s"])


def emit_ffn0a(kb, g, pctx):
    nc = kb.nc
    E = pctx.enter_context
    tag = "f0a"
    emit_router(kb, g, pctx, tag + "r", g["xh0"], g["xl0"], "wg0T", g["r0lg"])
    lrow = E(nc.sbuf_tensor(f"{tag}_lr", [32, NEXP * 128], F32))
    mx = E(nc.sbuf_tensor(f"{tag}_mx", [32, 128], F32))
    msk = E(nc.sbuf_tensor(f"{tag}_msk", [32, 128], F32))
    nrow = E(nc.sbuf_tensor(f"{tag}_nr", [32, 128], BF16))
    kb.wait("sync", f"d_{tag}rs", kb.cnt.get(f"d_{tag}rs", 0))
    lde = None
    for e_ in range(NEXP):
        lde = kb.dma("sync", f"d_{tag}lg", lambda e, e_=e_: e.dma_start(
            out=lrow[:, e_ * 128:(e_ + 1) * 128],
            in_=g["r0lg"][e_].rearrange("(a b) -> a b", b=128)))
    kb.wait_edge("vector", lde)
    ve = kb.op("vector", lambda e: e.tensor_copy(mx[:, :], lrow[:, 0:128]))
    for e_ in range(1, NEXP):
        ve = kb.op("vector", lambda e, e_=e_: e.tensor_max(mx[:, :], mx[:, :],
                                                           lrow[:, e_ * 128:(e_ + 1) * 128]))
    for e_ in range(NEXP):
        kb.op("vector", lambda e, e_=e_: e.tensor_tensor(msk[:, :], lrow[:, e_ * 128:(e_ + 1) * 128],
                                                         mx[:, :], ALU.is_lt))
        ne = kb.op("vector", lambda e: e.tensor_scalar_mul(nrow[:, :], msk[:, :], NEG))
        kb.wait_edge("sync", ne)
        kb.dma("sync", f"d_{tag}nr", lambda e, e_=e_: e.dma_start(
            out=g["r0neg"][e_].rearrange("(a b) -> a b", b=128), in_=nrow[:, :]))
        kb.wait("vector", f"d_{tag}nr", kb.cnt[f"d_{tag}nr"])
    xin, lde = load_xin(kb, pctx, tag, g["xh0"])
    kb.wait_edge("tensor", lde)
    gw = [E(nc.sbuf_tensor(f"{tag}_gw{i}", [128, NDC * D], BF16)) for i in range(2)]
    uw = [E(nc.sbuf_tensor(f"{tag}_uw{i}", [128, NDC * D], BF16)) for i in range(2)]
    nr_sb = E(nc.sbuf_tensor(f"{tag}_nrs", [1, S], BF16))
    tmp = E(nc.sbuf_tensor(f"{tag}_tmp", [128, TG], F32))
    h_sb = [E(nc.sbuf_tensor(f"{tag}_h{i}", [128, NDC * TG], BF16)) for i in range(2)]
    onesrow_b = kb.consts["ones_bf"][:, 0:128]
    slab_last = [None, None]
    h_store = [None] * 2
    nre = None
    for e_ in range(NEXP):
        sl = e_ % 2
        kb.wait_edge("gpsimd", slab_last[sl])
        for wsl, bank in ((gw, "gbank"), (uw, "ubank")):
            we = kb.dma("gpsimd", f"d_{tag}w{sl}", lambda e, e_=e_, sl=sl, wsl=wsl, bank=bank:
                        e.dma_start(out=wsl[sl][:, :].rearrange("p (dc o) -> p dc o", dc=NDC),
                                    in_=g[bank][e_].rearrange("(dc p) o -> p dc o", p=128)))
        kb.wait_edge("sync", nre)
        nre_ld = kb.dma("sync", f"d_{tag}nl", lambda e, e_=e_: e.dma_start(
            out=nr_sb[:, :], in_=g["r0neg"][e_:e_ + 1, :]))
        kb.wait_edge("tensor", we)
        kb.wait_edge("tensor", nre_ld)
        for tg in range(NTG):
            for hc in range(NDC):
                bka, bku = hc % 2, 2 + hc % 2
                kb.wait_edge("tensor", kb.bank_free[bka])
                for dc in range(NDC):
                    kb.mm(lambda e, sl=sl, dc=dc, hc=hc, tg=tg, bka=bka: e.matmul(
                        kb.psum[bka][:, :], gw[sl][:, dc * D + hc * 128: dc * D + (hc + 1) * 128],
                        xin[:, dc * S + tg * TG: dc * S + (tg + 1) * TG],
                        start=(dc == 0), stop=False))
                ae = kb.mm(lambda e, tg=tg, bka=bka: e.matmul(
                    kb.psum[bka][:, :], onesrow_b, nr_sb[:, tg * TG:(tg + 1) * TG],
                    start=False, stop=True), inc=True)
                kb.wait_edge("tensor", kb.bank_free[bku])
                for dc in range(NDC):
                    ue = kb.mm(lambda e, sl=sl, dc=dc, hc=hc, tg=tg, bku=bku: e.matmul(
                        kb.psum[bku][:, :], uw[sl][:, dc * D + hc * 128: dc * D + (hc + 1) * 128],
                        xin[:, dc * S + tg * TG: dc * S + (tg + 1) * TG],
                        start=(dc == 0), stop=(dc == NDC - 1)), inc=(dc == NDC - 1))
                kb.wait_edge("scalar", ae)
                kb.wait("scalar", "sd_vector", kb.cnt["sd_vector"])
                se = kb.op("scalar", lambda e, bka=bka: e.activation(tmp[:, :], kb.psum[bka][:, :], AF.Silu))
                kb.bank_free[bka] = se
                hb = (e_ * NTG + tg) % 2
                kb.wait_edge("vector", se)
                kb.wait_edge("vector", ue)
                if hc == 0:
                    kb.wait_edge("vector", h_store[hb])
                ve = kb.op("vector", lambda e, hb=hb, hc=hc, bku=bku: e.tensor_mul(
                    h_sb[hb][:, hc * TG:(hc + 1) * TG], tmp[:, :], kb.psum[bku][:, :]))
                kb.bank_free[bku] = ve
            kb.wait_edge("sync", ve)
            h_store[hb] = kb.dma("sync", f"d_{tag}h{hb}", lambda e, e_=e_, tg=tg, hb=hb:
                                 e.dma_start(out=g["h0"][e_ * NDC:(e_ + 1) * NDC, :,
                                                         tg * TG:(tg + 1) * TG].rearrange(
                                     "hc p t -> p hc t"),
                                     in_=h_sb[hb][:, :].rearrange("p (hc t) -> p hc t", hc=NDC)))
            slab_last[sl] = ue
        nre = ae


def emit_ffn0b(kb, g, pctx):
    nc = kb.nc
    E = pctx.enter_context
    tag = "f0b"
    dw = E(nc.sbuf_tensor(f"{tag}_dw", [128, NEXP * NDC * 512], BF16))
    h_sb = [E(nc.sbuf_tensor(f"{tag}_h{i}", [128, NDC * TG], BF16)) for i in range(2)]
    s0t = E(nc.sbuf_tensor(f"{tag}_s0", [128, TG], BF16))
    xt = E(nc.sbuf_tensor(f"{tag}_x", [128, TG], F32))
    gt = E(nc.sbuf_tensor(f"{tag}_g", [128, TG], F32))
    pt = E(nc.sbuf_tensor(f"{tag}_p", [128, TG], F32))
    x1t = [E(nc.sbuf_tensor(f"{tag}_x1{i}", [128, TG], F32)) for i in range(2)]
    we = None
    x1_store = [None, None]
    for sw in range(2):
        kb.wait("gpsimd", "sd_pe", kb.cnt.get("sd_pe", 0) if sw > 0 else 0)
        for e_ in range(NEXP):
            we = kb.dma("gpsimd", f"d_{tag}w{sw}", lambda e, e_=e_, sw=sw: e.dma_start(
                out=dw[:, e_ * NDC * 512: (e_ + 1) * NDC * 512].rearrange(
                    "p (hc o) -> p hc o", hc=NDC),
                in_=g["dbank"][e_].rearrange("(hc p) o -> p hc o", p=128)[:, :, sw * 512:(sw + 1) * 512]))
        kb.wait_edge("tensor", we)
        slot_mm = [None, None]
        for tg in range(NTG):
            for e_ in range(NEXP):
                sl = e_ % 2
                kb.wait_edge("sync", slot_mm[sl])
                lde = kb.dma("sync", f"d_{tag}h{sl}", lambda e, e_=e_, tg=tg, sl=sl:
                             e.dma_start(out=h_sb[sl][:, :].rearrange("p (hc t) -> p hc t", hc=NDC),
                                         in_=g["h0"][e_ * NDC:(e_ + 1) * NDC, :,
                                                     tg * TG:(tg + 1) * TG].rearrange(
                                             "hc p t -> p hc t")))
                kb.wait_edge("tensor", lde)
                if e_ == 0:
                    for b in range(4):
                        kb.wait_edge("tensor", kb.bank_free[b])
                for do in range(4):
                    for hc in range(NDC):
                        mme = kb.mm(lambda e, e_=e_, do=do, hc=hc, sl=sl, sw=sw: e.matmul(
                            kb.psum[do][:, :],
                            dw[:, e_ * NDC * 512 + hc * 512 + do * 128:
                               e_ * NDC * 512 + hc * 512 + (do + 1) * 128],
                            h_sb[sl][:, hc * TG:(hc + 1) * TG],
                            start=(e_ == 0 and hc == 0), stop=(e_ == NEXP - 1 and hc == NDC - 1)),
                            inc=(hc == NDC - 1))
                slot_mm[sl] = mme
                h_last = mme
            for do in range(4):
                dc = sw * 4 + do
                kb.wait("sync", "sd_vector", kb.cnt["sd_vector"])
                l1 = kb.dma("sync", f"d_{tag}e", lambda e, dc=dc, tg=tg: e.dma_start(
                    out=s0t[:, :], in_=g["s0T"][dc, :, tg * TG:(tg + 1) * TG]))
                l2 = kb.dma("sync", f"d_{tag}e", lambda e, dc=dc, tg=tg: e.dma_start(
                    out=xt[:, :], in_=g["x_fm"][dc, :, tg * TG:(tg + 1) * TG]))
                kb.wait_edge("scalar", h_last)
                kb.wait("scalar", "sd_vector", kb.cnt["sd_vector"])
                se = kb.op("scalar", lambda e, do=do: e.activation(gt[:, :], kb.psum[do][:, :],
                                                                   AF.Silu))
                kb.bank_free[do] = se
                kb.wait_edge("vector", se)
                kb.wait_edge("vector", l2)
                sx = tg % 2
                kb.wait_edge("vector", x1_store[sx])
                kb.op("vector", lambda e: e.tensor_mul(pt[:, :], gt[:, :], s0t[:, :]))
                ve = kb.op("vector", lambda e, sx=sx: e.tensor_add(x1t[sx][:, :], xt[:, :], pt[:, :]))
                kb.wait_edge("sync", ve)
                x1_store[sx] = kb.dma("sync", f"d_{tag}x1{sx}", lambda e, dc=dc, tg=tg, sx=sx:
                                      e.dma_start(out=g["x1"][dc, :, tg * TG:(tg + 1) * TG],
                                                  in_=x1t[sx][:, :]))


def emit_qkv1(kb, g, pctx):
    nc = kb.nc
    E = pctx.enter_context
    tag = "qkv1"
    cos_sb = E(nc.sbuf_tensor(f"{tag}_cos", [128, S], F32))
    sin_sb = E(nc.sbuf_tensor(f"{tag}_sin", [128, S], F32))
    qsb = E(nc.sbuf_tensor(f"{tag}_qsb", [128, TG], BF16))
    t1 = E(nc.sbuf_tensor(f"{tag}_t1", [128, TG], F32))
    t2 = E(nc.sbuf_tensor(f"{tag}_t2", [128, TG], F32))
    ro = [E(nc.sbuf_tensor(f"{tag}_ro{i}", [128, TG], BF16)) for i in range(4)]
    ro_store = [None] * 4
    kb.dma("sync", f"d_{tag}cs", lambda e: e.dma_start(out=cos_sb[:, :], in_=g["c_cos"]))
    ce = kb.dma("sync", f"d_{tag}cs", lambda e: e.dma_start(out=sin_sb[:, :], in_=g["c_sin"]))
    kb.wait_edge("vector", ce)
    ptrot = kb.consts["ptrot"]
    ri = [0]

    def rope_epi(oc, tg, bk, mme):
        ten = g["q1T"] if oc < NDC else g["k1T"]
        occ = oc % NDC
        kb.wait_edge("vector", mme)
        kb.wait("vector", "sd_pe", kb.cnt["sd_pe"])
        ve = kb.op("vector", lambda e: e.tensor_copy(qsb[:, :], kb.psum[bk][:, :]))
        kb.bank_free[bk] = ve
        rb = 4 + ri[0] % 2
        ri[0] += 1
        kb.wait_edge("tensor", ve)
        kb.wait_edge("tensor", kb.bank_free[rb])
        rme = kb.mm(lambda e, rb=rb: e.matmul(kb.psum[rb][:, :], ptrot[:, :], qsb[:, :],
                                              start=True, stop=True), inc=True)
        ob = ri[0] % 4
        kb.wait_edge("vector", rme)
        kb.wait_edge("vector", ro_store[ob])
        kb.op("vector", lambda e, tg=tg: e.tensor_mul(t1[:, :], qsb[:, :],
                                                      cos_sb[:, tg * TG:(tg + 1) * TG]))
        kb.op("vector", lambda e, tg=tg, rb=rb: e.tensor_mul(t2[:, :], kb.psum[rb][:, :],
                                                             sin_sb[:, tg * TG:(tg + 1) * TG]))
        ve = kb.op("vector", lambda e, ob=ob: e.tensor_add(ro[ob][:, :], t1[:, :], t2[:, :]))
        kb.bank_free[rb] = ve
        kb.wait_edge("sync", ve)
        ro_store[ob] = kb.dma("sync", f"d_{tag}r{ob}", lambda e, ten=ten, occ=occ, tg=tg, ob=ob:
                              e.dma_start(out=ten[occ, :, tg * TG:(tg + 1) * TG], in_=ro[ob][:, :]))

    xin = emit_linear(kb, g, pctx, tag, g["wqk1T"], 2 * D, lambda oc: ((None, oc)),
                      in_hbm=g["xh1"], banks=(0, 1, 2, 3), epi=rope_epi)
    emit_linear_tm(kb, g, pctx, tag + "v", g["wv1T"], xin, g["v1tm"], banks=(5, 6))


def emit_att1(kb, g, pctx):
    nc = kb.nc
    E = pctx.enter_context
    tag = "a1"
    NKC = S // 128
    kt = [E(nc.sbuf_tensor(f"{tag}_kt{i}", [128, S], BF16)) for i in range(2)]
    qt = [E(nc.sbuf_tensor(f"{tag}_qt{i}", [128, S], BF16)) for i in range(2)]
    vt = [E(nc.sbuf_tensor(f"{tag}_vt{i}", [128, NKC * 130], BF16)) for i in range(2)]
    e_sb = [E(nc.sbuf_tensor(f"{tag}_e{i}", [128, NKC * TG], BF16)) for i in range(2)]
    rs = E(nc.sbuf_tensor(f"{tag}_rs", [1, TG], F32))
    rb = E(nc.sbuf_tensor(f"{tag}_rb", [128, TG], F32))
    o_sb = [E(nc.sbuf_tensor(f"{tag}_o{i}", [64, TG], BF16)) for i in range(2)]
    onesrow_f = kb.consts["ones_f"][:, 0:128]
    tri_le = kb.consts["tri_le"]
    o_store = [None, None]
    chunk_last_pv = [None, None]
    items = [(c, hh, qg) for c in range(NDC) for hh in range(2) for qg in range(NTG)]

    def emit_chunk_loads(c):
        slc = c % 2
        kb.wait_edge("sync", chunk_last_pv[slc])
        kb.wait_edge("gpsimd", chunk_last_pv[slc])
        lde = kb.dma("sync", f"d_{tag}l{slc}", lambda e, c=c, slc=slc: e.dma_start(
            out=kt[slc][:, :], in_=g["k1T"][c]))
        lde = kb.dma("sync", f"d_{tag}l{slc}", lambda e, c=c, slc=slc: e.dma_start(
            out=qt[slc][:, :], in_=g["q1T"][c]))
        me = kb.op("gpsimd", lambda e, slc=slc: e.memset(vt[slc][:, :], 1.0))
        kb.wait_edge("gpsimd", me)
        lde2 = None
        for hh in range(2):
            lde2 = kb.dma("gpsimd", f"d_{tag}v{slc}", lambda e, c=c, hh=hh, slc=slc: e.dma_start(
                out=vt[slc][:, :].rearrange("p (kc z) -> p kc z", z=130)[:, :, hh * 65: hh * 65 + 64],
                in_=g["v1tm"].rearrange("(kc p) ch -> p kc ch", p=128)[
                    :, :, c * 128 + hh * 64: c * 128 + hh * 64 + 64]))
        return (lde, lde2)

    prev_exp = None

    def emit_pv_state(i):
        c, hh, qg = items[i]
        slc = c % 2
        par = i % 2
        nkc = 4 * qg + 4
        pb_ = kb.psum[4 + par]
        state = {}

        def first():
            kb.wait_edge("tensor", state["exp_edge"])
            kb.wait_edge("tensor", state["ve_edge"])
            kb.wait_edge("tensor", kb.bank_free[4 + par])

        mms = []
        for kc in range(nkc):
            def mk(kc):
                def f():
                    if kc == 0:
                        first()
                    return kb.mm(lambda e, kc=kc: e.matmul(
                        pb_[0:65, :], vt[slc][:, kc * 130 + hh * 65: kc * 130 + hh * 65 + 65],
                        e_sb[par][:, kc * TG:(kc + 1) * TG],
                        start=(kc == 0), stop=(kc == nkc - 1)), inc=(kc == nkc - 1))
                return f
            mms.append(mk(kc))

        def finish(pve):
            nonlocal_state = state
            kb.wait_edge("vector", pve)
            kb.wait("vector", "sd_scalar", kb.cnt["sd_scalar"])
            re = kb.op("vector", lambda e: e.reciprocal(rs[:, :], pb_[64:65, :]))
            kb.wait_edge("tensor", re)
            kb.wait_edge("tensor", kb.bank_free[6])
            bce = kb.mm(lambda e: e.matmul(kb.psum[6][:, :], onesrow_f, rs[:, :],
                                           start=True, stop=True), inc=True)
            kb.wait_edge("scalar", bce)
            kb.wait("scalar", "sd_vector", kb.cnt["sd_vector"])
            ce = kb.op("scalar", lambda e: e.copy(rb[:, :], kb.psum[6][:, :]))
            kb.bank_free[6] = ce
            ob = i % 2
            kb.wait_edge("vector", ce)
            kb.wait_edge("vector", o_store[ob])
            ve = kb.op("vector", lambda e, ob=ob: e.tensor_mul(
                o_sb[ob][:, :], pb_[0:64, :], rb[0:64, :]))
            kb.bank_free[4 + par] = ve
            kb.wait_edge("sync", ve)
            o_store[ob] = kb.dma("sync", f"d_{tag}o{ob}", lambda e, c=c, hh=hh, qg=qg, ob=ob:
                                 e.dma_start(out=g["o1"][c, hh * 64:(hh + 1) * 64,
                                                         qg * TG:(qg + 1) * TG],
                                             in_=o_sb[ob][:, :]))
            chunk_last_pv[slc] = pve
        return state, mms, finish

    pend = None
    ld_edges = {0: emit_chunk_loads(0)}

    def drain_pv(n):
        nonlocal pend
        if pend is None:
            return
        state, mms, finish, hold = pend
        while mms and n != 0:
            hold[0] = mms.pop(0)()
            n -= 1
        if not mms:
            finish(hold[0])
            pend = None

    for i, (c, hh, qg) in enumerate(items):
        slc = c % 2
        par = i % 2
        hb = hh * 64
        nkc = 4 * qg + 4
        if hh == 0 and qg == 0:
            le = ld_edges[c]
            kb.wait_edge("tensor", le[0])
            kb.wait_edge("tensor", le[1])
        last_ve = None
        exp_edge = None
        for grp in range(nkc // 4):
            kb.wait_edge("tensor", prev_exp)
            for j in range(4):
                kc = grp * 4 + j
                a_ = max(0, (kc - 4 * qg) * 128)
                kb.mm(lambda e, kc=kc, j=j, a_=a_, hb=hb, qg=qg, slc=slc: e.matmul(
                    kb.psum[j][:, a_:TG], kt[slc][hb:hb + 64, kc * 128:(kc + 1) * 128],
                    qt[slc][hb:hb + 64, qg * TG + a_:(qg + 1) * TG], start=True, stop=True),
                    inc=(j == 3))
            gme = ("sd_pe", kb.cnt["sd_pe"])
            kb.wait_edge("scalar", gme)
            exp_edge = kb.op("scalar", lambda e, grp=grp, par=par: e.activation(
                e_sb[par][:, grp * 4 * TG:(grp + 1) * 4 * TG], kb.psum_quad[:, :], AF.Exp,
                scale=SCALE0))
            prev_exp = exp_edge
            for j in range(4):
                kb.bank_free[j] = exp_edge
            for j in range(4):
                kc = grp * 4 + j
                a_ = max(0, (kc - 4 * qg) * 128)
                if a_ > 0:
                    kb.wait_edge("vector", exp_edge)
                    last_ve = kb.op("vector", lambda e, kc=kc, a_=a_, par=par: e.memset(
                        e_sb[par][:, kc * TG: kc * TG + a_], 0.0))
                sd = kc - 4 * qg
                if 0 <= sd <= 3:
                    off = sd * 128
                    kb.wait_edge("vector", exp_edge)
                    last_ve = kb.op("vector", lambda e, kc=kc, off=off, par=par: e.tensor_mul(
                        e_sb[par][:, kc * TG + off: kc * TG + off + 128],
                        e_sb[par][:, kc * TG + off: kc * TG + off + 128], tri_le[:, :]))
            drain_pv(8)
        drain_pv(-1)
        st, mms, fin = emit_pv_state(i)
        st["exp_edge"] = exp_edge
        st["ve_edge"] = last_ve if last_ve is not None else exp_edge
        pend = (st, mms, fin, [None])
        if hh == 1 and qg == NTG - 1 and c + 1 < NDC:
            ld_edges[c + 1] = emit_chunk_loads(c + 1)
    drain_pv(-1)


def emit_ffn1a(kb, g, pctx):
    nc = kb.nc
    E = pctx.enter_context
    tag = "f1a"
    emit_router(kb, g, pctx, tag + "r", g["xh1"], g["xl1"], "wg1T", g["r1lg"])
    lrow = E(nc.sbuf_tensor(f"{tag}_lr", [32, NEXP * 128], F32))
    prow = E(nc.sbuf_tensor(f"{tag}_pr", [32, NEXP * 128], F32))
    mx = E(nc.sbuf_tensor(f"{tag}_mx", [32, 128], F32))
    sm = E(nc.sbuf_tensor(f"{tag}_sm", [32, 128], F32))
    t0 = E(nc.sbuf_tensor(f"{tag}_t0", [32, 128], F32))
    wrow = E(nc.sbuf_tensor(f"{tag}_wr", [32, 128], F32))
    nrow = E(nc.sbuf_tensor(f"{tag}_nr", [32, 128], BF16))
    arow = E(nc.sbuf_tensor(f"{tag}_ar", [32, 128], BF16))
    kb.wait("sync", f"d_{tag}rs", kb.cnt.get(f"d_{tag}rs", 0))
    lde = None
    for e_ in range(NEXP):
        lde = kb.dma("sync", f"d_{tag}lg", lambda e, e_=e_: e.dma_start(
            out=lrow[:, e_ * 128:(e_ + 1) * 128],
            in_=g["r1lg"][e_].rearrange("(a b) -> a b", b=128)))
    kb.wait_edge("vector", lde)
    sl_ = lambda t, e_: t[:, e_ * 128:(e_ + 1) * 128]
    kb.op("vector", lambda e: e.tensor_copy(mx[:, :], sl_(lrow, 0)))
    for e_ in range(1, NEXP):
        kb.op("vector", lambda e, e_=e_: e.tensor_max(mx[:, :], mx[:, :], sl_(lrow, e_)))
    for e_ in range(NEXP):
        ve = kb.op("vector", lambda e, e_=e_: e.tensor_sub(sl_(prow, e_), sl_(lrow, e_), mx[:, :]))
    kb.wait_edge("scalar", ve)
    for e_ in range(NEXP):
        se = kb.op("scalar", lambda e, e_=e_: e.activation(sl_(prow, e_), sl_(prow, e_), AF.Exp))
    kb.wait_edge("vector", se)
    kb.op("vector", lambda e: e.tensor_copy(sm[:, :], sl_(prow, 0)))
    for e_ in range(1, NEXP):
        kb.op("vector", lambda e, e_=e_: e.tensor_add(sm[:, :], sm[:, :], sl_(prow, e_)))
    kb.op("vector", lambda e: e.reciprocal(sm[:, :], sm[:, :]))
    for e_ in range(NEXP):
        kb.op("vector", lambda e, e_=e_: e.tensor_mul(sl_(prow, e_), sl_(prow, e_), sm[:, :]))
    kb.op("vector", lambda e: e.tensor_copy(mx[:, :], sl_(prow, 0)))
    for e_ in range(1, NEXP):
        kb.op("vector", lambda e, e_=e_: e.tensor_max(mx[:, :], mx[:, :], sl_(prow, e_)))
    first = True
    for e_ in range(NEXP):
        kb.op("vector", lambda e, e_=e_: e.tensor_tensor(t0[:, :], sl_(prow, e_), mx[:, :], ALU.is_ge))
        kb.op("vector", lambda e, e_=e_: e.tensor_scalar(t0[:, :], t0[:, :], -2.0, None, ALU.mult))
        kb.op("vector", lambda e, e_=e_: e.tensor_add(t0[:, :], t0[:, :], sl_(prow, e_)))
        if first:
            kb.op("vector", lambda e: e.tensor_copy(sm[:, :], t0[:, :]))
            first = False
        else:
            kb.op("vector", lambda e: e.tensor_max(sm[:, :], sm[:, :], t0[:, :]))
    for e_ in range(NEXP):
        kb.op("vector", lambda e, e_=e_: e.tensor_tensor(t0[:, :], sl_(prow, e_), sm[:, :], ALU.is_ge))
        ae = kb.op("vector", lambda e, e_=e_: e.tensor_mul(wrow[:, :], sl_(prow, e_), t0[:, :]))
        a2 = kb.op("vector", lambda e: e.tensor_copy(arow[:, :], t0[:, :]))
        kb.op("vector", lambda e: e.tensor_scalar(t0[:, :], t0[:, :], -1.0, -NEG, ALU.add, ALU.mult))
        ne = kb.op("vector", lambda e: e.tensor_copy(nrow[:, :], t0[:, :]))
        kb.wait_edge("sync", ne)
        kb.dma("sync", f"d_{tag}rw", lambda e, e_=e_: e.dma_start(
            out=g["r1w"][e_].rearrange("(a b) -> a b", b=128), in_=wrow[:, :]))
        kb.dma("sync", f"d_{tag}rw", lambda e, e_=e_: e.dma_start(
            out=g["r1act"][e_].rearrange("(a b) -> a b", b=128), in_=arow[:, :]))
        kb.dma("sync", f"d_{tag}rw", lambda e, e_=e_: e.dma_start(
            out=g["r1neg"][e_].rearrange("(a b) -> a b", b=128), in_=nrow[:, :]))
        kb.wait("vector", f"d_{tag}rw", kb.cnt[f"d_{tag}rw"])
    xin, lde = load_xin(kb, pctx, tag, g["xh1"])
    kb.wait_edge("tensor", lde)
    w1s = [E(nc.sbuf_tensor(f"{tag}_w1{i}", [128, NDC * D], BF16)) for i in range(2)]
    w3s = [E(nc.sbuf_tensor(f"{tag}_w3{i}", [128, NDC * D], BF16)) for i in range(2)]
    b1c = E(nc.sbuf_tensor(f"{tag}_b1c", [128, NEXP * NEXP], F32))
    b3c = E(nc.sbuf_tensor(f"{tag}_b3c", [128, NEXP * NEXP], F32))
    wr_sb = E(nc.sbuf_tensor(f"{tag}_wrs", [1, S], F32))
    nr_sb = E(nc.sbuf_tensor(f"{tag}_nrs", [1, S], BF16))
    wb_sb = E(nc.sbuf_tensor(f"{tag}_wb", [128, TG], F32))
    nb_sb = E(nc.sbuf_tensor(f"{tag}_nb", [128, TG], F32))
    tmp = E(nc.sbuf_tensor(f"{tag}_tmp", [128, TG], F32))
    tmp2 = E(nc.sbuf_tensor(f"{tag}_tmp2", [128, TG], F32))
    h_sb = [E(nc.sbuf_tensor(f"{tag}_h{i}", [128, NDC * TG], BF16)) for i in range(2)]
    onesrow_f = kb.consts["ones_f"][:, 0:128]
    be = kb.dma("sync", f"d_{tag}bc", lambda e: e.dma_start(
        out=b1c[:, :], in_=g["b11"].rearrange("p a b -> p (a b)")))
    be = kb.dma("sync", f"d_{tag}bc", lambda e: e.dma_start(
        out=b3c[:, :], in_=g["b31"].rearrange("p a b -> p (a b)")))
    kb.wait_edge("scalar", be)
    kb.wait_edge("vector", be)
    slab_last = [None, None]
    h_store = [None] * 2
    row_last = None
    for e_ in range(NEXP):
        sl = e_ % 2
        kb.wait_edge("gpsimd", slab_last[sl])
        for wsl, bank in ((w1s, "w11T"), (w3s, "w31T")):
            we = kb.dma("gpsimd", f"d_{tag}w{sl}", lambda e, e_=e_, sl=sl, wsl=wsl, bank=bank:
                        e.dma_start(out=wsl[sl][:, :].rearrange("p (dc o) -> p dc o", dc=NDC),
                                    in_=g[bank][e_].rearrange("(dc p) o -> p dc o", p=128)))
        kb.wait_edge("sync", row_last)
        rle = kb.dma("sync", f"d_{tag}rl", lambda e, e_=e_: e.dma_start(
            out=wr_sb[:, :], in_=g["r1w"][e_:e_ + 1, :]))
        rle = kb.dma("sync", f"d_{tag}rl", lambda e, e_=e_: e.dma_start(
            out=nr_sb[:, :], in_=g["r1neg"][e_:e_ + 1, :]))
        kb.wait_edge("tensor", we)
        kb.wait_edge("tensor", rle)
        for tg in range(NTG):
            kb.wait_edge("tensor", kb.bank_free[4])
            kb.wait_edge("tensor", kb.bank_free[5])
            kb.mm(lambda e, tg=tg: e.matmul(kb.psum[4][:, :], onesrow_f,
                                            wr_sb[:, tg * TG:(tg + 1) * TG], start=True, stop=True))
            bce = kb.mm(lambda e, tg=tg: e.matmul(kb.psum[5][:, :], kb.consts["ones_bf"][:, 0:128],
                                                  nr_sb[:, tg * TG:(tg + 1) * TG],
                                                  start=True, stop=True), inc=True)
            kb.wait_edge("scalar", bce)
            kb.wait("scalar", "sd_vector", kb.cnt["sd_vector"])
            ce1 = kb.op("scalar", lambda e: e.copy(wb_sb[:, :], kb.psum[4][:, :]))
            ce2 = kb.op("scalar", lambda e: e.copy(nb_sb[:, :], kb.psum[5][:, :]))
            kb.bank_free[4] = ce1
            kb.bank_free[5] = ce2
            kb.wait_edge("vector", ce2)
            for hc in range(NDC):
                bk = hc % 2
                kb.wait_edge("tensor", kb.bank_free[bk])
                for dc in range(NDC):
                    z1e = kb.mm(lambda e, sl=sl, dc=dc, hc=hc, tg=tg, bk=bk: e.matmul(
                        kb.psum[bk][:, :], w1s[sl][:, dc * D + hc * 128: dc * D + (hc + 1) * 128],
                        xin[:, dc * S + tg * TG: dc * S + (tg + 1) * TG],
                        start=(dc == 0), stop=(dc == NDC - 1)), inc=(dc == NDC - 1))
                kb.wait_edge("vector", z1e)
                kb.wait("vector", "sd_scalar", kb.cnt["sd_scalar"])
                kb.op("vector", lambda e, bk=bk: e.tensor_mul(tmp[:, :], kb.psum[bk][:, :], wb_sb[:, :]))
                ve = kb.op("vector", lambda e: e.tensor_add(tmp2[:, :], tmp[:, :], nb_sb[:, :]))
                kb.bank_free[bk] = ve
                hb = (e_ * NTG + tg) % 2
                kb.wait_edge("scalar", ve)
                if hc == 0:
                    kb.wait_edge("scalar", h_store[hb])
                se = kb.op("scalar", lambda e, e_=e_, hc=hc, hb=hb: e.activation(
                    h_sb[hb][:, hc * TG:(hc + 1) * TG], tmp2[:, :], AF.Silu,
                    bias=b1c[:, e_ * NEXP + hc: e_ * NEXP + hc + 1]))
            kb.wait_edge("sync", se)
            h_store[hb] = kb.dma("sync", f"d_{tag}hs{hb}", lambda e, e_=e_, tg=tg, hb=hb:
                                 e.dma_start(out=g["h1"][e_ * NDC:(e_ + 1) * NDC, :,
                                                         tg * TG:(tg + 1) * TG].rearrange(
                                     "hc p t -> p hc t"),
                                     in_=h_sb[hb][:, :].rearrange("p (hc t) -> p hc t", hc=NDC)))
            for dc in range(NDC):
                bk = 2 + dc % 2
                kb.wait_edge("tensor", kb.bank_free[bk])
                for dci in range(NDC):
                    z3e = kb.mm(lambda e, sl=sl, dci=dci, dc=dc, tg=tg, bk=bk: e.matmul(
                        kb.psum[bk][:, :], w3s[sl][:, dci * D + dc * 128: dci * D + (dc + 1) * 128],
                        xin[:, dci * S + tg * TG: dci * S + (tg + 1) * TG],
                        start=(dci == 0), stop=(dci == NDC - 1)), inc=(dci == NDC - 1))
                kb.wait_edge("vector", z3e)
                kb.wait("vector", "sd_scalar", kb.cnt["sd_scalar"])
                kb.op("vector", lambda e, bk=bk: e.tensor_mul(tmp[:, :], kb.psum[bk][:, :], wb_sb[:, :]))
                hb = (e_ * NTG + tg) % 2
                if dc == 0:
                    kb.wait_edge("vector", h_store[hb])
                ve = kb.op("vector", lambda e, e_=e_, dc=dc, hb=hb: e.tensor_scalar(
                    h_sb[hb][:, dc * TG:(dc + 1) * TG], tmp[:, :],
                    b3c[:, e_ * NEXP + dc: e_ * NEXP + dc + 1], None, ALU.add))
                kb.bank_free[bk] = ve
            kb.wait_edge("sync", ve)
            h_store[hb] = kb.dma("sync", f"d_{tag}hs{hb}", lambda e, e_=e_, tg=tg, hb=hb:
                                 e.dma_start(out=g["x31"][e_ * NDC:(e_ + 1) * NDC, :,
                                                          tg * TG:(tg + 1) * TG].rearrange(
                                     "dc p t -> p dc t"),
                                     in_=h_sb[hb][:, :].rearrange("p (dc t) -> p dc t", dc=NDC)))
            slab_last[sl] = z3e
        row_last = z3e


def emit_ffn1b(kb, g, pctx):
    nc = kb.nc
    E = pctx.enter_context
    tag = "f1b"
    w2s = E(nc.sbuf_tensor(f"{tag}_w2", [128, NEXP * NDC * D], BF16))
    h_sb = [E(nc.sbuf_tensor(f"{tag}_h{i}", [128, NDC * TG], BF16)) for i in range(2)]
    x3_sb = [E(nc.sbuf_tensor(f"{tag}_x30", [128, NDC * TG], BF16))]
    b2r = E(nc.sbuf_tensor(f"{tag}_b2f", [1, D], F32))
    b2b = E(nc.sbuf_tensor(f"{tag}_b2b", [1, NEXP * D], BF16))
    ar_sb = E(nc.sbuf_tensor(f"{tag}_ar", [1, S], BF16))
    pt = E(nc.sbuf_tensor(f"{tag}_p", [128, TG], F32))
    y_sb = E(nc.sbuf_tensor(f"{tag}_y", [128, NDC * TG], F32))
    yo = [E(nc.sbuf_tensor(f"{tag}_yo{i}", [128, TG], F32)) for i in range(2)]
    we = None
    for e_ in range(NEXP):
        we = kb.dma("gpsimd", f"d_{tag}w", lambda e, e_=e_: e.dma_start(
            out=w2s[:, e_ * NDC * D:(e_ + 1) * NDC * D].rearrange("p (hc o) -> p hc o", hc=NDC),
            in_=g["w21T"][e_].rearrange("(hc p) o -> p hc o", p=128)))
    for e_ in range(NEXP):
        kb.wait("sync", "sd_vector", kb.cnt["sd_vector"])
        be = kb.dma("sync", f"d_{tag}b", lambda e, e_=e_: e.dma_start(
            out=b2r[:, :], in_=g["b21"][e_:e_ + 1, :]))
        kb.wait_edge("vector", be)
        bce = kb.op("vector", lambda e, e_=e_: e.tensor_copy(
            b2b[:, e_ * D:(e_ + 1) * D], b2r[:, :]))
    kb.wait_edge("tensor", bce)
    kb.wait_edge("tensor", we)
    onesrow_b = kb.consts["ones_bf"][:, 0:128]
    y_store = [None, None]
    ar_last = None
    slot_mm = [None, None]
    x3_edge = [None, None]
    for tg in range(NTG):
        for e_ in range(NEXP):
            sl = e_ % 2
            kb.wait_edge("sync", slot_mm[sl])
            lde = kb.dma("sync", f"d_{tag}h{sl}", lambda e, e_=e_, tg=tg, sl=sl:
                         e.dma_start(out=h_sb[sl][:, :].rearrange("p (hc t) -> p hc t", hc=NDC),
                                     in_=g["h1"][e_ * NDC:(e_ + 1) * NDC, :,
                                                 tg * TG:(tg + 1) * TG].rearrange("hc p t -> p hc t")))
            kb.wait_edge("sync", ar_last)
            are = kb.dma("sync", f"d_{tag}a", lambda e, e_=e_: e.dma_start(
                out=ar_sb[:, :], in_=g["r1act"][e_:e_ + 1, :]))
            kb.wait_edge("tensor", lde)
            kb.wait_edge("tensor", are)
            sx = 0
            kb.wait_edge("sync", x3_edge[sx])
            x3e = kb.dma("sync", f"d_{tag}x{sx}", lambda e, e_=e_, tg=tg, sx=sx:
                         e.dma_start(out=x3_sb[sx][:, :].rearrange("p (dc t) -> p dc t", dc=NDC),
                                     in_=g["x31"][e_ * NDC:(e_ + 1) * NDC, :,
                                                  tg * TG:(tg + 1) * TG].rearrange("dc p t -> p dc t")))
            for dc in range(NDC):
                bk = dc % 4
                kb.wait_edge("tensor", kb.bank_free[bk])
                for hc in range(NDC):
                    kb.mm(lambda e, e_=e_, dc=dc, hc=hc, sl=sl, bk=bk: e.matmul(
                        kb.psum[bk][:, :],
                        w2s[:, e_ * NDC * D + hc * D + dc * 128: e_ * NDC * D + hc * D + (dc + 1) * 128],
                        h_sb[sl][:, hc * TG:(hc + 1) * TG],
                        start=(hc == 0), stop=False))
                mme = kb.mm(lambda e, e_=e_, dc=dc, tg=tg, bk=bk: e.matmul(
                    kb.psum[bk][:, :], b2b[0:1, e_ * D + dc * 128: e_ * D + (dc + 1) * 128],
                    ar_sb[:, tg * TG:(tg + 1) * TG], start=False, stop=True), inc=True)
                ar_last = mme
                slot_mm[sl] = mme
                kb.wait_edge("vector", mme)
                kb.wait_edge("vector", x3e)
                ve = kb.op("vector", lambda e, bk=bk, sx=sx, dc=dc: e.tensor_mul(
                    pt[:, :], kb.psum[bk][:, :], x3_sb[sx][:, dc * TG:(dc + 1) * TG]))
                kb.bank_free[bk] = ve
                x3_edge[sx] = ve
                if e_ == 0:
                    ve = kb.op("vector", lambda e, dc=dc: e.tensor_copy(
                        y_sb[:, dc * TG:(dc + 1) * TG], pt[:, :]))
                else:
                    ve = kb.op("vector", lambda e, dc=dc: e.tensor_add(
                        y_sb[:, dc * TG:(dc + 1) * TG], y_sb[:, dc * TG:(dc + 1) * TG], pt[:, :]))
        for dc in range(NDC):
            yb = dc % 2
            kb.wait("vector", f"d_{tag}y{yb}", kb.cnt.get(f"d_{tag}y{yb}", 0))
            ve2 = kb.op("vector", lambda e, dc=dc, yb=yb: e.tensor_copy(
                yo[yb][:, :], y_sb[:, dc * TG:(dc + 1) * TG]))
            kb.wait_edge("sync", ve2)
            kb.dma("sync", f"d_{tag}y{yb}", lambda e, dc=dc, tg=tg, yb=yb: e.dma_start(
                out=g["y1T"][dc, :, tg * TG:(tg + 1) * TG], in_=yo[yb][:, :]))


def emit_shfin1(kb, g, pctx):
    nc = kb.nc
    E = pctx.enter_context
    tag = "sf1"
    xin, lde = load_xin(kb, pctx, tag, g["xh1"])
    kb.wait_edge("tensor", lde)
    NIC = 16
    sw1 = E(nc.sbuf_tensor(f"{tag}_sw1", [128, NDC * 2 * D], BF16))
    sw2 = E(nc.sbuf_tensor(f"{tag}_sw2", [128, NDC * 2 * D], BF16))
    sw3 = E(nc.sbuf_tensor(f"{tag}_sw3", [128, NIC * D], BF16))
    zh = E(nc.sbuf_tensor(f"{tag}_zh", [128, NIC * TG], BF16))
    tmp = E(nc.sbuf_tensor(f"{tag}_tmp", [128, TG], F32))
    yt = E(nc.sbuf_tensor(f"{tag}_y", [128, TG], F32))
    s1t = E(nc.sbuf_tensor(f"{tag}_s1", [128, TG], BF16))
    x1t = E(nc.sbuf_tensor(f"{tag}_x1", [128, TG], F32))
    gt = E(nc.sbuf_tensor(f"{tag}_g", [128, TG], F32))
    ot = [E(nc.sbuf_tensor(f"{tag}_o{i}", [128, TG], F32)) for i in range(2)]
    for nm, t in (("sw1T", sw1), ("sw2T", sw2)):
        we = kb.dma("gpsimd", f"d_{tag}w", lambda e, nm=nm, t=t: e.dma_start(
            out=t[:, :].rearrange("p (dc o) -> p dc o", dc=NDC),
            in_=g[nm].rearrange("(dc p) o -> p dc o", p=128)))
    we = kb.dma("gpsimd", f"d_{tag}w", lambda e: e.dma_start(
        out=sw3[:, :].rearrange("p (ic o) -> p ic o", ic=NIC),
        in_=g["sw3T"].rearrange("(ic p) o -> p ic o", p=128)))
    kb.wait_edge("tensor", we)
    o_store = [None, None]
    for tg in range(NTG):
        for ic in range(NIC):
            bka, bku = ic % 2, 2 + ic % 2
            kb.wait_edge("tensor", kb.bank_free[bka])
            for dc in range(NDC):
                kb.mm(lambda e, dc=dc, ic=ic, tg=tg, bka=bka: e.matmul(
                    kb.psum[bka][:, :], sw1[:, dc * 2 * D + ic * 128: dc * 2 * D + (ic + 1) * 128],
                    xin[:, dc * S + tg * TG: dc * S + (tg + 1) * TG],
                    start=(dc == 0), stop=(dc == NDC - 1)), inc=(dc == NDC - 1))
            kb.wait_edge("tensor", kb.bank_free[bku])
            for dc in range(NDC):
                ue = kb.mm(lambda e, dc=dc, ic=ic, tg=tg, bku=bku: e.matmul(
                    kb.psum[bku][:, :], sw2[:, dc * 2 * D + ic * 128: dc * 2 * D + (ic + 1) * 128],
                    xin[:, dc * S + tg * TG: dc * S + (tg + 1) * TG],
                    start=(dc == 0), stop=(dc == NDC - 1)), inc=(dc == NDC - 1))
            kb.wait("scalar", "sd_pe", kb.cnt["sd_pe"])
            kb.wait("scalar", "sd_vector", kb.cnt["sd_vector"])
            se = kb.op("scalar", lambda e, bka=bka: e.activation(tmp[:, :], kb.psum[bka][:, :], AF.Silu))
            kb.bank_free[bka] = se
            kb.wait_edge("vector", se)
            kb.wait_edge("vector", ue)
            ve = kb.op("vector", lambda e, ic=ic, bku=bku: e.tensor_mul(
                zh[:, ic * TG:(ic + 1) * TG], tmp[:, :], kb.psum[bku][:, :]))
            kb.bank_free[bku] = ve
        kb.wait_edge("tensor", ve)
        for dc in range(NDC):
            bk = 4 + dc % 2
            kb.wait_edge("tensor", kb.bank_free[bk])
            for ic in range(NIC):
                zme = kb.mm(lambda e, ic=ic, dc=dc, bk=bk: e.matmul(
                    kb.psum[bk][:, :], sw3[:, ic * D + dc * 128: ic * D + (dc + 1) * 128],
                    zh[:, ic * TG:(ic + 1) * TG],
                    start=(ic == 0), stop=(ic == NIC - 1)), inc=(ic == NIC - 1))
            ob = dc % 2
            kb.wait("sync", "sd_vector", kb.cnt["sd_vector"])
            l1 = kb.dma("sync", f"d_{tag}e", lambda e, dc=dc, tg=tg: e.dma_start(
                out=yt[:, :], in_=g["y1T"][dc, :, tg * TG:(tg + 1) * TG]))
            l1 = kb.dma("sync", f"d_{tag}e", lambda e, dc=dc, tg=tg: e.dma_start(
                out=s1t[:, :], in_=g["s1T"][dc, :, tg * TG:(tg + 1) * TG]))
            l1 = kb.dma("sync", f"d_{tag}e", lambda e, dc=dc, tg=tg: e.dma_start(
                out=x1t[:, :], in_=g["x1"][dc, :, tg * TG:(tg + 1) * TG]))
            kb.wait_edge("vector", zme)
            kb.wait_edge("vector", l1)
            kb.wait("vector", "sd_scalar", kb.cnt["sd_scalar"])
            ve2 = kb.op("vector", lambda e, bk=bk: e.tensor_add(tmp[:, :], kb.psum[bk][:, :], yt[:, :]))
            kb.bank_free[bk] = ve2
            kb.wait_edge("scalar", ve2)
            se2 = kb.op("scalar", lambda e: e.activation(gt[:, :], tmp[:, :], AF.Silu))
            kb.wait_edge("vector", se2)
            kb.wait_edge("vector", o_store[ob])
            kb.op("vector", lambda e: e.tensor_mul(gt[:, :], gt[:, :], s1t[:, :]))
            ve3 = kb.op("vector", lambda e, ob=ob: e.tensor_add(ot[ob][:, :], x1t[:, :], gt[:, :]))
            kb.wait_edge("sync", ve3)
            o_store[ob] = kb.dma("sync", f"d_{tag}o{ob}", lambda e, dc=dc, tg=tg, ob=ob: e.dma_start(
                out=g["out_fm"][dc, :, tg * TG:(tg + 1) * TG], in_=ot[ob][:, :]))


def emit_outstub(kb, g, pctx):
    nc = kb.nc
    E = pctx.enter_context
    t = E(nc.sbuf_tensor("stub_t", [128, S], F32))
    for c in range(NDC):
        e1 = kb.dma("sync", "d_stub", lambda e, c=c: e.dma_start(out=t[:, :], in_=g["x_fm"][c]))
        kb.wait_edge("sync", e1)
        e2 = kb.dma("sync", "d_stub", lambda e, c=c: e.dma_start(out=g["out_fm"][c], in_=t[:, :]))
        kb.wait_edge("sync", e2)



def kernel(**inputs):
    maps = host_prep(inputs)
    nc = build()
    res = run_bass_kernel_spmd(nc, maps, core_ids=list(range(8)))
    out = np.empty((8, S, D), dtype=np.float32)
    for b in range(8):
        out[b] = res.results[b]["out_fm"].reshape(D, S).T
    return out


# version 06
# speedup vs baseline: 1.0662x; 1.0662x over previous
import contextlib
import numpy as np

import concourse.bass as bass
import concourse.mybir as mybir
from concourse.bass_utils import run_bass_kernel_spmd

F32 = mybir.dt.float32
BF16 = mybir.dt.bfloat16
I32 = mybir.dt.int32
U32 = mybir.dt.uint32
AF = mybir.ActivationFunctionType
ALU = mybir.AluOpType
AX = mybir.AxisListType

S = 4096
D = 1024
NDC = 8
TG = 512
NTG = S // TG
NT128 = S // 128
H0 = 16
WIN = 256
CBS = 32
NB = S // CBS
SBS = 64
TOPK = 16
NSB = S // SBS
NSEL = NSB * TOPK // 64 * 64
NEXP = 8
NEG = -100.0
SCALE0 = 0.125

ENGS = ("sync", "tensor", "vector", "scalar", "gpsimd")


class _Cnt(dict):

    def __init__(self, kb):
        super().__init__()
        self.kb = kb

    def _r(self, name):
        return self.kb.phase_map.get(name, name)

    def __getitem__(self, name):
        return super().get(self._r(name), 0)

    def get(self, name, default=0):
        return super().get(self._r(name), default)

    def bump(self, name, by):
        r = self._r(name)
        super().__setitem__(r, super().get(r, 0) + by)
        return super().__getitem__(r)


class KB:

    def __init__(self, nc, ctx):
        self.nc = nc
        self.ctx = ctx
        self.q = {e: [] for e in ENGS}
        self.sems = {}
        self.cnt = _Cnt(self)
        self.phase_map = {}
        self.phase_used = {"hw": 0, "sw": 0}

    def new_phase(self):
        self.phase_map.clear()
        self.phase_used = {"hw": 0, "sw": 0}

    def sem(self, name):
        name = self.phase_map.get(name, name)
        if name not in self.sems:
            self.sems[name] = self.ctx.enter_context(self.nc.semaphore(name))
            if name not in self.cnt:
                dict.__setitem__(self.cnt, name, 0)
        return self.sems[name]

    def _pooled(self, eng, name):
        if name in self.phase_map:
            return self.phase_map[name]
        qt = "sw" if eng == "gpsimd" else "hw"
        i = self.phase_used[qt]
        self.phase_used[qt] += 1
        pname = f"dp_{qt}{i}"
        self.phase_map[name] = pname
        return pname

    def raw(self, eng, fn):
        self.q[eng].append(fn)

    def wait(self, eng, name, val):
        if val <= 0:
            return
        s = self.sem(name)
        self.raw(eng, lambda e, s=s, v=val: e.wait_ge(s, v))

    def wait_edge(self, eng, edge):
        if edge is not None:
            self.wait(eng, edge[0], edge[1])

    def op(self, eng, fn, serialize=True):
        name = "sd_" + eng
        s = self.sem(name)
        n = self.cnt[name]
        if serialize and n > 0:
            self.raw(eng, lambda e, s=s, n=n: e.wait_ge(s, n))
        self.raw(eng, lambda e, s=s, fn=fn: fn(e).then_inc(s, 1))
        dict.__setitem__(self.cnt, name, n + 1)
        return (name, n + 1)

    def mm(self, fn, inc=False):
        if inc:
            s = self.sem("sd_pe")
            self.raw("tensor", lambda e, s=s, fn=fn: fn(e).then_inc(s, 1))
            v = self.cnt.bump("sd_pe", 1)
            return ("sd_pe", v)
        self.raw("tensor", lambda e, fn=fn: fn(e))
        return None

    def dma(self, eng, name, fn):
        assert name.startswith("d_"), name
        pname = self._pooled(eng, name)
        s = self.sem(pname)
        self.raw(eng, lambda e, s=s, fn=fn: fn(e).then_inc(s, 16))
        v = self.cnt.bump(pname, 16)
        return (pname, v)

    def barrier(self):
        snap = [(n, v) for n, v in dict.items(self.cnt) if v > 0]
        for eng in ENGS:
            for n, v in snap:
                self.wait(eng, n, v)

    def finalize(self):
        with self.nc.Block() as block:
            for eng in ENGS:
                def runner(e, eng=eng):
                    for f in self.q[eng]:
                        f(e)
                getattr(block, eng)(runner)



def _chunked_T(w_t):
    din, m = w_t.shape
    return np.ascontiguousarray(w_t.reshape(din // 128, 128, m).transpose(1, 0, 2))


def host_prep(inp):
    d = {}
    perm = np.concatenate([
        np.array([h * 192 + c * 64 + dd for h in range(16) for dd in range(64)])
        for c in range(3)])
    wqkv_t = np.ascontiguousarray(inp["L0_attn_w_qkv"].T[:, perm])
    d["wqkvT"] = wqkv_t
    d["bqkv"] = np.ascontiguousarray(inp["L0_attn_b_qkv"][perm])[None, :]
    d["wcompT"] = np.ascontiguousarray(
        inp["L0_attn_w_comp"].reshape(D, CBS, D).transpose(1, 2, 0))
    d["bcomp"] = inp["L0_attn_b_comp"][None, :]
    d["wscoreT"] = np.ascontiguousarray(
        inp["L0_attn_w_score"].T.reshape(NDC, 128, 1).transpose(1, 0, 2))
    pk = perm[D:2 * D]
    pv = perm[2 * D:3 * D]
    d["wkrows"] = np.ascontiguousarray(inp["L0_attn_w_qkv"][pk])
    d["wvrows"] = np.ascontiguousarray(inp["L0_attn_w_qkv"][pv])
    d["wsdo"] = np.ascontiguousarray(inp["L0_attn_w_score"][0].reshape(NDC, 128).T)
    d["bscore"] = inp["L0_attn_b_score"][None, :]
    d["wgateT"] = np.ascontiguousarray(
        inp["L0_attn_w_gate"].T.reshape(NDC, 128, 3).transpose(1, 0, 2))
    d["bgate"] = inp["L0_attn_b_gate"][None, :]
    d["woT"] = np.ascontiguousarray(inp["L0_attn_w_o"].T)
    d["bo"] = inp["L0_attn_b_o"][None, :]
    d["wg0T"] = np.ascontiguousarray(
        inp["L0_ffn_Wg"].T.reshape(NDC, 128, NEXP).transpose(1, 0, 2))
    d["gbank"] = inp["L0_ffn_gate_bank"]
    d["ubank"] = inp["L0_ffn_up_bank"]
    d["dbank"] = inp["L0_ffn_down_bank"]
    d["wqk1T"] = np.ascontiguousarray(
        np.concatenate([inp["L1_attn_Wq"].T, inp["L1_attn_Wk"].T], axis=1))
    d["wv1T"] = np.ascontiguousarray(inp["L1_attn_Wv"].T)
    d["wo1T"] = np.ascontiguousarray(inp["L1_attn_Wo"].T)
    d["wg1T"] = np.ascontiguousarray(
        inp["L1_ffn_gate_w"].T.reshape(NDC, 128, NEXP).transpose(1, 0, 2))
    d["w11T"] = np.ascontiguousarray(inp["L1_ffn_w1"].transpose(0, 2, 1))
    d["w21T"] = np.ascontiguousarray(inp["L1_ffn_w2"].transpose(0, 2, 1))
    d["w31T"] = np.ascontiguousarray(inp["L1_ffn_w3"].transpose(0, 2, 1))
    d["b11"] = np.ascontiguousarray(
        inp["L1_ffn_b1"].reshape(NEXP, 8, 128).transpose(2, 0, 1))
    d["b31"] = np.ascontiguousarray(
        inp["L1_ffn_b3"].reshape(NEXP, 8, 128).transpose(2, 0, 1))
    d["b21"] = inp["L1_ffn_b2"]
    d["sw1T"] = np.ascontiguousarray(inp["L1_ffn_sw1"].T)
    d["sw2T"] = np.ascontiguousarray(inp["L1_ffn_sw2"].T)
    d["sw3T"] = np.ascontiguousarray(inp["L1_ffn_sw3"].T)

    ii = np.arange(128)
    d["c_tri_ge"] = (ii[:, None] >= ii[None, :]).astype(np.float32)
    d["c_tri_le"] = (ii[:, None] <= ii[None, :]).astype(np.float32)
    d["c_ident"] = np.eye(128, dtype=np.float32)
    hd = 64
    inv = 1.0 / (10000.0 ** (np.arange(0, hd, 2, dtype=np.float32) / hd))
    pos = np.arange(S, dtype=np.float32)
    ang = pos[None, :] * np.concatenate([inv, inv])[:, None]
    cos = np.cos(ang).astype(np.float32)
    sin = np.sin(ang).astype(np.float32)
    d["c_cos"] = np.ascontiguousarray(np.concatenate([cos, cos], axis=0))
    d["c_sin"] = np.ascontiguousarray(np.concatenate([sin, sin], axis=0))
    P = np.zeros((128, 128), dtype=np.float32)
    for base in (0, 64):
        for r in range(32):
            P[base + r, base + r + 32] = -1.0
            P[base + 32 + r, base + r] = 1.0
    d["c_PT"] = np.ascontiguousarray(P.T)

    x = np.asarray(inp["L0_attn_x"])
    maps = []
    for b in range(8):
        m = dict(d)
        m["x_fm"] = np.ascontiguousarray(x[b].T).reshape(NDC, 128, S)
        maps.append(m)
    return maps



def build(dbg=(), upto=None, only=None):
    nc = bass.Bass("TRN2", target_bir_lowering=False, debug=False)
    dbg = set(dbg)

    def inp(name, shape, dtype=F32):
        return nc.dram_tensor(name, list(shape), dtype, kind="ExternalInput").ap()

    def scr(name, shape, dtype):
        kind = "ExternalOutput" if name in dbg else "Internal"
        return nc.dram_tensor(name, list(shape), dtype, kind=kind).ap()

    g = {}
    g["x_fm"] = inp("x_fm", (NDC, 128, S))
    g["wqkvT"] = inp("wqkvT", (D, 3 * D))
    g["bqkv"] = inp("bqkv", (1, 3 * D))
    g["wcompT"] = inp("wcompT", (CBS, D, D))
    g["bcomp"] = inp("bcomp", (1, D))
    g["wscoreT"] = inp("wscoreT", (128, NDC, 1))
    g["wkrows"] = inp("wkrows", (D, D))
    g["wvrows"] = inp("wvrows", (D, D))
    g["wsdo"] = inp("wsdo", (128, NDC))
    g["bscore"] = inp("bscore", (1, 1))
    g["wgateT"] = inp("wgateT", (128, NDC, 3))
    g["bgate"] = inp("bgate", (1, 3))
    g["woT"] = inp("woT", (D, D))
    g["bo"] = inp("bo", (1, D))
    g["wg0T"] = inp("wg0T", (128, NDC, NEXP))
    g["gbank"] = inp("gbank", (NEXP, D, D))
    g["ubank"] = inp("ubank", (NEXP, D, D))
    g["dbank"] = inp("dbank", (NEXP, D, D))
    g["wqk1T"] = inp("wqk1T", (D, 2 * D))
    g["wv1T"] = inp("wv1T", (D, D))
    g["wo1T"] = inp("wo1T", (D, D))
    g["wg1T"] = inp("wg1T", (128, NDC, NEXP))
    g["w11T"] = inp("w11T", (NEXP, D, D))
    g["w21T"] = inp("w21T", (NEXP, D, D))
    g["w31T"] = inp("w31T", (NEXP, D, D))
    g["b11"] = inp("b11", (128, NEXP, NEXP))
    g["b31"] = inp("b31", (128, NEXP, NEXP))
    g["b21"] = inp("b21", (NEXP, D))
    g["sw1T"] = inp("sw1T", (D, 2 * D))
    g["sw2T"] = inp("sw2T", (D, 2 * D))
    g["sw3T"] = inp("sw3T", (2 * D, D))
    for cn, shp in (("c_tri_ge", (128, 128)), ("c_tri_le", (128, 128)),
                    ("c_ident", (128, 128)), ("c_cos", (128, S)), ("c_sin", (128, S)),
                    ("c_PT", (128, 128))):
        g[cn] = inp(cn, shp)

    for ln in (0, 1):
        g[f"xh{ln}"] = scr(f"xh{ln}", (NDC, 128, S), BF16)
        g[f"xl{ln}"] = scr(f"xl{ln}", (NDC, 128, S), BF16)
    g["q0T"] = scr("q0T", (NDC, 128, S), BF16)
    g["k0T"] = scr("k0T", (NDC, 128, S), BF16)
    g["v0T"] = scr("v0T", (NDC, 128, S), BF16)
    g["k0tm"] = scr("k0tm", (S, D), BF16)
    g["v0tm"] = scr("v0tm", (S, D), BF16)
    g["scrow"] = scr("scrow", (2, S), F32)
    g["scb"] = scr("scb", (2, NSB, SBS), F32)
    g["selidx"] = scr("selidx", (2, NSEL), I32)
    g["sk0tm"] = scr("sk0tm", (NSEL, D), BF16)
    g["sv0tm"] = scr("sv0tm", (NSEL, D), BF16)
    g["sk0T"] = scr("sk0T", (NDC, 128, NSEL), BF16)
    g["ck0T"] = scr("ck0T", (NDC, 128, NB), BF16)
    g["cv0T"] = scr("cv0T", (NDC, 128, NB), BF16)
    g["ck0tm"] = scr("ck0tm", (NB, D), BF16)
    g["cv0tm"] = scr("cv0tm", (NB, D), BF16)
    g["negm"] = scr("negm", (4, S), BF16)
    g["mrow"] = scr("mrow", (1, S), F32)
    g["qnorm"] = scr("qnorm", (1, S), F32)
    g["oc0T"] = scr("oc0T", (NDC, 128, 384), BF16)
    g["os0T"] = scr("os0T", (NDC, 128, 1280), BF16)
    g["ow0T"] = scr("ow0T", (NDC, 128, S), BF16)
    g["s0T"] = scr("s0T", (NDC, 128, S), BF16)
    g["r0lg"] = scr("r0lg", (NEXP, S), F32)
    g["r0neg"] = scr("r0neg", (NEXP, S), BF16)
    g["h0"] = scr("h0", (NEXP * NDC, 128, S), BF16)
    g["x1"] = scr("x1", (NDC, 128, S), F32)
    g["q1T"] = scr("q1T", (NDC, 128, S), BF16)
    g["k1T"] = scr("k1T", (NDC, 128, S), BF16)
    g["v1tm"] = scr("v1tm", (S, D), BF16)
    g["o1"] = scr("o1", (NDC, 128, S), BF16)
    g["o1v"] = g["o1"]
    g["s1T"] = scr("s1T", (NDC, 128, S), BF16)
    g["r1lg"] = scr("r1lg", (NEXP, S), F32)
    g["r1w"] = scr("r1w", (NEXP, S), F32)
    g["r1neg"] = scr("r1neg", (NEXP, S), BF16)
    g["r1act"] = scr("r1act", (NEXP, S), BF16)
    g["h1"] = scr("h1", (NEXP * NDC, 128, S), BF16)
    g["x31"] = scr("x31", (NEXP * NDC, 128, S), BF16)
    g["y1T"] = scr("y1T", (NDC, 128, S), F32)
    g["out_fm"] = nc.dram_tensor("out_fm", [NDC, 128, S], F32, kind="ExternalOutput").ap()

    with contextlib.ExitStack() as ctx:
        kb = KB(nc, ctx)
        phases = _phases(kb, g, ctx)
        for name, fn in phases:
            if only is not None and name not in ("INIT",) + tuple(only):
                continue
            kb.new_phase()
            with contextlib.ExitStack() as pctx:
                fn(pctx)
            kb.barrier()
            if upto is not None and name == upto:
                break
        kb.finalize()
    return nc


def _phases(kb, g, ctx):
    nc = kb.nc

    ones_bf = ctx.enter_context(nc.sbuf_tensor("ones_bf", [1, TG], BF16))
    ones_f = ctx.enter_context(nc.sbuf_tensor("ones_f", [1, TG], F32))
    onescol_bf = ctx.enter_context(nc.sbuf_tensor("onescol_bf", [128, 1], BF16))
    twoscol_bf = ctx.enter_context(nc.sbuf_tensor("twoscol_bf", [128, 1], BF16))
    onescol_f = ctx.enter_context(nc.sbuf_tensor("onescol_f", [128, 1], F32))
    tri_ge = ctx.enter_context(nc.sbuf_tensor("tri_ge", [128, 128], BF16))
    tri_le = ctx.enter_context(nc.sbuf_tensor("tri_le", [128, 128], BF16))
    ident = ctx.enter_context(nc.sbuf_tensor("ident", [128, 128], BF16))
    ptrot = ctx.enter_context(nc.sbuf_tensor("ptrot", [128, 128], BF16))
    kb.consts = dict(ones_bf=ones_bf, ones_f=ones_f, onescol_bf=onescol_bf,
                     twoscol_bf=twoscol_bf, onescol_f=onescol_f, tri_ge=tri_ge,
                     tri_le=tri_le, ident=ident, ptrot=ptrot)
    pbE = ctx.enter_context(nc.psum_tensor("pbE", [128, 4 * TG], F32))
    kb.psum = [pbE[:, i * TG:(i + 1) * TG] for i in range(4)]
    kb.psum += [ctx.enter_context(nc.psum_tensor(f"pb{i}", [128, TG], F32))[:, :]
                for i in range(4, 7)]
    kb.psum_quad = pbE
    kb.psum_bf = ctx.enter_context(nc.psum_tensor("pbbf", [128, 1024], BF16))
    kb.bank_free = [None] * 8

    CF_VALS = (1e-5, 0.5, -0.5, 1.0, -1.0)
    cf = ctx.enter_context(nc.sbuf_tensor("cf", [128, len(CF_VALS)], F32))
    for j, v in enumerate(CF_VALS):
        nc.const_aps.register(F32, v, cf[:, j:j + 1])

    def init(pctx):
        for j, v in enumerate(CF_VALS):
            kb.op("gpsimd", lambda e, j=j, v=v: e.memset(cf[:, j:j + 1], v))
        kb.op("gpsimd", lambda e: e.memset(ones_bf[:, :], 1.0))
        kb.op("gpsimd", lambda e: e.memset(ones_f[:, :], 1.0))
        kb.op("gpsimd", lambda e: e.memset(onescol_bf[:, :], 1.0))
        kb.op("gpsimd", lambda e: e.memset(twoscol_bf[:, :], 2.0))
        kb.op("gpsimd", lambda e: e.memset(onescol_f[:, :], 1.0))
        kb.dma("gpsimd", "d_init", lambda e: e.dma_start(out=tri_ge[:, :], in_=g["c_tri_ge"]))
        kb.dma("gpsimd", "d_init", lambda e: e.dma_start(out=tri_le[:, :], in_=g["c_tri_le"]))
        kb.dma("gpsimd", "d_init", lambda e: e.dma_start(out=ident[:, :], in_=g["c_ident"]))
        kb.dma("gpsimd", "d_init", lambda e: e.dma_start(out=ptrot[:, :], in_=g["c_PT"]))

    out = [("INIT", init)]
    out.append(("LN0", lambda pctx: emit_ln(kb, g, pctx, g["x_fm"], g["xh0"], g["xl0"])))
    out.append(("QKV0", lambda pctx: emit_qkv0(kb, g, pctx)))
    out.append(("SEL0", lambda pctx: emit_sel0(kb, g, pctx)))
    out.append(("COMP0", lambda pctx: emit_comp0(kb, g, pctx)))
    out.append(("OCOMP0", lambda pctx: emit_banded(kb, g, pctx, "oc0", 384, 1,
                g["q0T"], g["ck0T"], g["cv0tm"], 1, g["oc0T"])))
    out.append(("OSEL0", lambda pctx: emit_banded(kb, g, pctx, "os0", 1280, NSEL // 128,
                g["q0T"], g["sk0T"], g["sv0tm"], 0, g["os0T"])))
    out.append(("OWIN0", lambda pctx: emit_banded(kb, g, pctx, "ow0", S, S // 128,
                g["q0T"], g["k0T"], g["v0tm"], 0, g["ow0T"])))
    out.append(("GCOMB0", lambda pctx: emit_gcomb0(kb, g, pctx)))
    out.append(("FFN0A", lambda pctx: emit_ffn0a(kb, g, pctx)))
    out.append(("FFN0B", lambda pctx: emit_ffn0b(kb, g, pctx)))
    out.append(("LN1", lambda pctx: emit_ln(kb, g, pctx, g["x1"], g["xh1"], g["xl1"])))
    out.append(("QKV1", lambda pctx: emit_qkv1(kb, g, pctx)))
    out.append(("ATT1", lambda pctx: emit_att1(kb, g, pctx)))
    out.append(("WO1", lambda pctx: emit_linear(kb, g, pctx, "wo1", g["wo1T"], D,
                lambda oc: (g["s1T"], oc), in_hbm=g["o1v"])))
    out.append(("FFN1A", lambda pctx: emit_ffn1a(kb, g, pctx)))
    out.append(("FFN1B", lambda pctx: emit_ffn1b(kb, g, pctx)))
    out.append(("SHFIN1", lambda pctx: emit_shfin1(kb, g, pctx)))
    return out


def load_xin(kb, pctx, tag, in_hbm, cols=S):
    nc = kb.nc
    xin = pctx.enter_context(nc.sbuf_tensor(f"{tag}_xin", [128, NDC * cols], BF16))
    lde = None
    for c in range(NDC):
        lde = kb.dma("sync", f"d_{tag}x", lambda e, c=c: e.dma_start(
            out=xin[:, c * cols:(c + 1) * cols], in_=in_hbm[c, :, 0:cols]))
    return xin, lde


def emit_linear(kb, g, pctx, tag, wT_hbm, nout, out_map, in_hbm=None, xin=None,
                bias_hbm=None, banks=(0, 1, 2, 3, 4, 5), epi=None):
    nc = kb.nc
    E = pctx.enter_context
    OS = 512
    nos = nout // OS
    if xin is None:
        xin, lde = load_xin(kb, pctx, tag, in_hbm)
        kb.wait_edge("tensor", lde)
    wsl = [E(nc.sbuf_tensor(f"{tag}_w{i}", [128, NDC * OS], BF16)) for i in range(2)]
    ones = kb.consts["ones_bf"]
    brow = None
    if bias_hbm is not None:
        brow_f = E(nc.sbuf_tensor(f"{tag}_bf", [1, nout], F32))
        brow = E(nc.sbuf_tensor(f"{tag}_bb", [1, nout], BF16))
        be = kb.dma("sync", f"d_{tag}b", lambda e: e.dma_start(out=brow_f[:, :], in_=bias_hbm))
        kb.wait_edge("vector", be)
        bce = kb.op("vector", lambda e: e.tensor_copy(brow[:, :], brow_f[:, :]))
        kb.wait_edge("tensor", bce)
    obuf = [E(nc.sbuf_tensor(f"{tag}_o{i}", [128, 4 * TG], BF16)) for i in range(2)]
    obuf_edge = [None] * 2
    slab_last = [None, None]
    nb = len(banks)
    ngrp = 0
    for osl in range(nos):
        sl = osl % 2
        kb.wait_edge("gpsimd", slab_last[sl])
        we = kb.dma("gpsimd", f"d_{tag}w{sl}", lambda e, osl=osl, sl=sl: e.dma_start(
            out=wsl[sl][:, :].rearrange("p (dc o) -> p dc o", dc=NDC),
            in_=wT_hbm.rearrange("(dc p) o -> p dc o", p=128)[:, :, osl * OS:(osl + 1) * OS]))
        kb.wait_edge("tensor", we)
        for tg in range(NTG):
            ob = (osl * NTG + tg) % 2
            oms = []
            for o4 in range(OS // 128):
                oc = osl * 4 + o4
                om = out_map(oc)
                if om is None:
                    continue
                bk = banks[ngrp % nb]
                ngrp += 1
                kb.wait_edge("tensor", kb.bank_free[bk])
                nmm = NDC + (1 if brow is not None else 0)
                for dc in range(NDC):
                    last = (dc == nmm - 1)
                    mme = kb.mm(lambda e, sl=sl, dc=dc, o4=o4, tg=tg, bk=bk, last=last: e.matmul(
                        kb.psum[bk][:, :],
                        wsl[sl][:, dc * OS + o4 * 128: dc * OS + (o4 + 1) * 128],
                        xin[:, dc * S + tg * TG: dc * S + (tg + 1) * TG],
                        start=(dc == 0), stop=last), inc=last)
                if brow is not None:
                    mme = kb.mm(lambda e, oc=oc, bk=bk: e.matmul(
                        kb.psum[bk][:, :], brow[0:1, oc * 128:(oc + 1) * 128],
                        ones[:, :], start=False, stop=True), inc=True)
                slab_last[sl] = mme
                if epi is not None:
                    epi(oc, tg, bk, mme)
                else:
                    oms.append(om)
                    if o4 == 0:
                        kb.wait_edge("vector", obuf_edge[ob])
                    kb.wait_edge("vector", mme)
                    ve = kb.op("vector", lambda e, ob=ob, bk=bk, o4=o4: e.tensor_copy(
                        obuf[ob][:, o4 * TG:(o4 + 1) * TG], kb.psum[bk][:, :]))
                    kb.bank_free[bk] = ve
            if epi is None and oms:
                assert len(oms) == 4 and all(t.tensor is oms[0][0].tensor for t, _ in oms) \
                    and [ocl for _, ocl in oms] == list(range(oms[0][1], oms[0][1] + 4)), oms
                ohbm, ocl0 = oms[0]
                kb.wait_edge("sync", ve)
                obuf_edge[ob] = kb.dma("sync", f"d_{tag}s{ob}", lambda e, ob=ob, ohbm=ohbm,
                                       ocl0=ocl0, tg=tg: e.dma_start(
                    out=ohbm[ocl0:ocl0 + 4, :, tg * TG:(tg + 1) * TG].rearrange("c p t -> p c t"),
                    in_=obuf[ob][:, :].rearrange("p (c t) -> p c t", c=4)))
    return xin


def emit_linear_tm(kb, g, pctx, tag, wT_hbm, xin, out_tm_hbm, bias_row=None,
                   banks=(5, 6)):
    nc = kb.nc
    E = pctx.enter_context
    wsb = E(nc.sbuf_tensor(f"{tag}_wtm", [128, NDC * D], BF16))
    we = kb.dma("gpsimd", f"d_{tag}wt", lambda e: e.dma_start(
        out=wsb[:, :].rearrange("p (dc o) -> p dc o", dc=NDC),
        in_=wT_hbm.rearrange("(dc p) o -> p dc o", p=128)))
    kb.wait_edge("tensor", we)
    obuf = [E(nc.sbuf_tensor(f"{tag}_to{i}", [128, TG], BF16)) for i in range(4)]
    obuf_edge = [None] * 4
    onesr = kb.consts["ones_bf"][:, 0:128]
    ngrp = 0
    for t1 in range(NT128):
        for half in range(2):
            bk = banks[ngrp % len(banks)]
            kb.wait_edge("tensor", kb.bank_free[bk])
            for dc in range(NDC):
                last = (dc == NDC - 1 and bias_row is None)
                mme = kb.mm(lambda e, dc=dc, t1=t1, half=half, bk=bk, last=last: e.matmul(
                    kb.psum[bk][:, :],
                    xin[:, dc * S + t1 * 128: dc * S + (t1 + 1) * 128],
                    wsb[:, dc * D + half * TG: dc * D + (half + 1) * TG],
                    start=(dc == 0), stop=last), inc=last)
            if bias_row is not None:
                mme = kb.mm(lambda e, half=half, bk=bk: e.matmul(
                    kb.psum[bk][:, :], onesr, bias_row[0:1, half * TG:(half + 1) * TG],
                    start=False, stop=True), inc=True)
            ob = ngrp % 4
            ngrp += 1
            kb.wait_edge("vector", obuf_edge[ob])
            kb.wait_edge("vector", mme)
            ve = kb.op("vector", lambda e, ob=ob, bk=bk: e.tensor_copy(
                obuf[ob][:, :], kb.psum[bk][:, :]))
            kb.bank_free[bk] = ve
            kb.wait_edge("sync", ve)
            obuf_edge[ob] = kb.dma("sync", f"d_{tag}ts{ob}", lambda e, ob=ob, t1=t1,
                                   half=half: e.dma_start(
                out=out_tm_hbm[t1 * 128:(t1 + 1) * 128, half * TG:(half + 1) * TG],
                in_=obuf[ob][:, :]))


def emit_qkv0(kb, g, pctx):
    def omap(oc):
        if oc < 8:
            return (g["q0T"], oc)
        if oc < 16:
            return (g["k0T"], oc - 8)
        return (g["v0T"], oc - 16)
    xin = emit_linear(kb, g, pctx, "qkv0", g["wqkvT"], 3 * D, omap,
                      in_hbm=g["xh0"], bias_hbm=g["bqkv"])
    nc = kb.nc
    E = pctx.enter_context
    brow_f = E(nc.sbuf_tensor("qkv0tm_bf", [1, 2 * D], F32))
    brow = E(nc.sbuf_tensor("qkv0tm_bb", [1, 2 * D], BF16))
    be = kb.dma("sync", "d_qkv0tmb", lambda e: e.dma_start(
        out=brow_f[:, :], in_=g["bqkv"][:, D:3 * D]))
    kb.wait_edge("vector", be)
    bce = kb.op("vector", lambda e: e.tensor_copy(brow[:, :], brow_f[:, :]))
    kb.wait_edge("tensor", bce)
    kv = g["wqkvT"]
    emit_linear_tm(kb, g, pctx, "k0tm", kv[:, D:2 * D], xin, g["k0tm"],
                   bias_row=brow[:, 0:D])
    emit_linear_tm(kb, g, pctx, "v0tm", kv[:, 2 * D:3 * D], xin, g["v0tm"],
                   bias_row=brow[:, D:2 * D])


SEL0_PARTS = "ABC"


def emit_sel0(kb, g, pctx):
    nc = kb.nc
    E = pctx.enter_context
    tag = "sel0"
    wrow = E(nc.sbuf_tensor(f"{tag}_wrow", [128, NDC * D], F32))
    wsdo = E(nc.sbuf_tensor(f"{tag}_wsdo", [128, NDC], F32))
    weff = E(nc.sbuf_tensor(f"{tag}_weff", [128, 2 * NDC], F32))
    wh = E(nc.sbuf_tensor(f"{tag}_wh", [128, 2 * NDC], BF16))
    wl = E(nc.sbuf_tensor(f"{tag}_wl", [128, 2 * NDC], BF16))
    de = kb.dma("sync", f"d_{tag}ws", lambda e: e.dma_start(out=wsdo[:, :], in_=g["wsdo"]))
    kb.wait_edge("tensor", de)
    for w, src_w in enumerate(("wkrows", "wvrows")):
        kb.wait_edge("sync", kb.bank_free[5])
        lde = None
        for doc in range(NDC):
            lde = kb.dma("sync", f"d_{tag}wr{w}", lambda e, doc=doc, src_w=src_w: e.dma_start(
                out=wrow[:, doc * D:(doc + 1) * D],
                in_=g[src_w].rearrange("(doc p) di -> doc p di", p=128)[doc]))
        kb.wait_edge("tensor", lde)
        ve_prev = kb.bank_free[5]
        for di in range(NDC):
            kb.wait_edge("tensor", ve_prev)
            for doc in range(NDC):
                mme = kb.mm(lambda e, di=di, doc=doc: e.matmul(
                    kb.psum[5][:, 0:1],
                    wrow[:, doc * D + di * 128: doc * D + (di + 1) * 128],
                    wsdo[:, doc:doc + 1], start=(doc == 0), stop=(doc == NDC - 1)),
                    inc=(doc == NDC - 1))
            kb.wait_edge("vector", mme)
            ve_prev = kb.op("vector", lambda e, w=w, di=di: e.tensor_copy(
                weff[:, w * NDC + di: w * NDC + di + 1], kb.psum[5][:, 0:1]))
        kb.bank_free[5] = ve_prev
    he = kb.op("vector", lambda e: e.tensor_copy(wh[:, :], weff[:, :]))
    le = kb.op("vector", lambda e: e.tensor_sub(wl[:, :], weff[:, :], wh[:, :]))
    kb.wait_edge("tensor", le)

    xh = [E(nc.sbuf_tensor(f"{tag}_xh{i}", [128, NDC * TG], BF16)) for i in range(2)]
    xl = [E(nc.sbuf_tensor(f"{tag}_xl{i}", [128, NDC * TG], BF16)) for i in range(2)]
    kt = [E(nc.sbuf_tensor(f"{tag}_kt{i}", [128, NDC * TG], BF16)) for i in range(2)]
    qt = [E(nc.sbuf_tensor(f"{tag}_qt{i}", [128, NDC * TG], BF16)) for i in range(2)]
    sqq = [E(nc.sbuf_tensor(f"{tag}_sqq{i}", [128, TG], BF16)) for i in range(2)]
    sqk = [E(nc.sbuf_tensor(f"{tag}_sqk{i}", [128, TG], BF16)) for i in range(2)]
    scrk = E(nc.sbuf_tensor(f"{tag}_scrk", [1, S], F32))
    scrv = E(nc.sbuf_tensor(f"{tag}_scrv", [1, S], F32))
    qn = E(nc.sbuf_tensor(f"{tag}_qn", [1, S], F32))
    k2 = E(nc.sbuf_tensor(f"{tag}_k2", [1, S], F32))
    kmax = E(nc.sbuf_tensor(f"{tag}_kmax", [1, 2], F32))
    nm = E(nc.sbuf_tensor(f"{tag}_nm", [1, S], BF16))
    onesc = kb.consts["onescol_bf"]

    slot_pe = [None, None]
    sq_ve = [None, None]
    for tg in range(NTG):
        sl = tg % 2
        kb.wait_edge("sync", slot_pe[sl])
        de = None
        for t, hb in ((xh, "xh0"), (xl, "xl0"), (kt, "k0T"), (qt, "q0T")):
            de = kb.dma("sync", f"d_{tag}l{sl}", lambda e, t=t, hb=hb, tg=tg, sl=sl:
                        e.dma_start(out=t[sl][:, :].rearrange("p (dc t) -> p dc t", dc=NDC),
                                    in_=g[hb][:, :, tg * TG:(tg + 1) * TG].rearrange(
                                        "dc p t -> p dc t")))
        kb.wait_edge("vector", de)
        kb.wait_edge("tensor", de)
        for b in range(4):
            kb.wait_edge("tensor", kb.bank_free[b])
        for dc in range(NDC):
            sv = dc % 2
            kb.wait_edge("vector", sq_ve[sv])
            v1 = kb.op("vector", lambda e, sl=sl, dc=dc, sv=sv: e.tensor_mul(
                sqq[sv][:, :], qt[sl][:, dc * TG:(dc + 1) * TG], qt[sl][:, dc * TG:(dc + 1) * TG]))
            v2 = kb.op("vector", lambda e, sl=sl, dc=dc, sv=sv: e.tensor_mul(
                sqk[sv][:, :], kt[sl][:, dc * TG:(dc + 1) * TG], kt[sl][:, dc * TG:(dc + 1) * TG]))
            kb.wait_edge("tensor", v2)
            for w in range(2):
                for ti, (wt_, xt_) in enumerate(((wh, xh), (wh, xl), (wl, xh))):
                    kb.mm(lambda e, sl=sl, dc=dc, w=w, wt_=wt_, xt_=xt_, ti=ti: e.matmul(
                        kb.psum[w][0:1, :], wt_[:, w * NDC + dc:w * NDC + dc + 1],
                        xt_[sl][:, dc * TG:(dc + 1) * TG],
                        start=(dc == 0 and ti == 0), stop=(dc == NDC - 1 and ti == 2)))
            kb.mm(lambda e, sv=sv, dc=dc: e.matmul(kb.psum[2][0:1, :], onesc[:, :],
                                                   sqq[sv][:, :], start=(dc == 0), stop=(dc == NDC - 1)))
            mme = kb.mm(lambda e, sv=sv, dc=dc: e.matmul(kb.psum[3][0:1, :], onesc[:, :],
                                                         sqk[sv][:, :], start=(dc == 0),
                                                         stop=(dc == NDC - 1)), inc=True)
            sq_ve[sv] = mme
            slot_pe[sl] = mme
        kb.wait_edge("vector", mme)
        kb.wait_edge("scalar", mme)
        e0 = kb.op("vector", lambda e, tg=tg: e.tensor_copy(scrk[:, tg * TG:(tg + 1) * TG], kb.psum[0][0:1, :]))
        e1 = kb.op("vector", lambda e, tg=tg: e.tensor_copy(scrv[:, tg * TG:(tg + 1) * TG], kb.psum[1][0:1, :]))
        e2 = kb.op("scalar", lambda e, tg=tg: e.activation(qn[:, tg * TG:(tg + 1) * TG],
                                                           kb.psum[2][0:1, :], AF.Sqrt))
        e3 = kb.op("vector", lambda e, tg=tg: e.tensor_copy(k2[:, tg * TG:(tg + 1) * TG], kb.psum[3][0:1, :]))
        kb.bank_free[0], kb.bank_free[1], kb.bank_free[2], kb.bank_free[3] = e0, e1, e2, e3

    ve = kb.op("vector", lambda e: e.reduce_max(kmax[:, 0:1], k2[:, :], axis=AX.X))
    kb.wait_edge("scalar", ve)
    se = kb.op("scalar", lambda e: e.activation(kmax[:, 1:2], kmax[:, 0:1], AF.Sqrt))
    kb.wait_edge("vector", se)
    kb.wait_edge("vector", e2)
    ve = kb.op("vector", lambda e: e.tensor_scalar(nm[:, :], qn[:, :], kmax[:, 1:2], -1.0,
                                                   ALU.mult, ALU.mult))
    kb.wait_edge("sync", ve)
    kb.dma("sync", "d_selr", lambda e: e.dma_start(out=g["negm"][0:1, :], in_=nm[:, :]))
    kb.wait_edge("sync", e2)
    kb.dma("sync", "d_selr", lambda e: e.dma_start(out=g["qnorm"][0:1, :], in_=qn[:, :]))
    kb.wait_edge("sync", e1)
    kb.dma("sync", "d_selr", lambda e: e.dma_start(
        out=g["scb"][0:1, :, :].rearrange("a n j -> a (n j)"), in_=scrk[:, :]))
    kb.dma("sync", "d_selr", lambda e: e.dma_start(
        out=g["scb"][1:2, :, :].rearrange("a n j -> a (n j)"), in_=scrv[:, :]))
    dsel = kb.cnt["d_selr"]

    if "B" not in SEL0_PARTS:
        return
    sc64 = E(nc.sbuf_tensor(f"{tag}_sc64", [NSB, SBS], F32))
    sc64r = E(nc.sbuf_tensor(f"{tag}_sc64r", [NSB, SBS], F32))
    v8 = E(nc.sbuf_tensor(f"{tag}_v8", [NSB, 8], F32))
    idx16 = E(nc.sbuf_tensor(f"{tag}_idx16", [NSB, TOPK], U32))
    iot = E(nc.sbuf_tensor(f"{tag}_iot", [NSB, TOPK], I32))
    idxg = E(nc.sbuf_tensor(f"{tag}_idxg", [NSB, TOPK], I32))
    ge = kb.op("gpsimd", lambda e: e.iota(iot[:, :], pattern=[[0, TOPK]], base=0,
                                          channel_multiplier=SBS))
    for w in range(2):
        kb.wait("sync", "d_selr", dsel)
        kb.wait("sync", "sd_vector", kb.cnt["sd_vector"])
        le = kb.dma("sync", f"d_{tag}t{w}", lambda e, w=w: e.dma_start(out=sc64[:, :], in_=g["scb"][w]))
        kb.wait_edge("vector", le)
        kb.op("vector", lambda e: e.max(out=v8[:, :], in_=sc64[:, :]))
        kb.op("vector", lambda e: e.max_index(out=idx16[:, 0:8], in_max=v8[:, :], in_values=sc64[:, :]))
        kb.op("vector", lambda e: e.match_replace(out=sc64r[:, :], in_to_replace=v8[:, :],
                                                  in_values=sc64[:, :], imm_value=-1e30))
        kb.op("vector", lambda e: e.max(out=v8[:, :], in_=sc64r[:, :]))
        ie = kb.op("vector", lambda e: e.max_index(out=idx16[:, 8:16], in_max=v8[:, :], in_values=sc64r[:, :]))
        kb.wait_edge("vector", ge)
        ae = kb.op("vector", lambda e: e.tensor_tensor(idxg[:, :], idx16[:, :], iot[:, :], ALU.add))
        kb.wait_edge("sync", ae)
        kb.dma("sync", f"d_{tag}t{w}", lambda e, w=w: e.dma_start(
            out=g["selidx"][w].rearrange("(n r) -> n r", r=TOPK), in_=idxg[:, :]))
        kb.wait("vector", f"d_{tag}t{w}", kb.cnt[f"d_{tag}t{w}"])

    if "C" not in SEL0_PARTS:
        return
    idxs = E(nc.sbuf_tensor(f"{tag}_idxs", [128, NSEL // 128], I32))
    gat = E(nc.sbuf_tensor(f"{tag}_gat", [128, D], BF16))
    tsb = E(nc.sbuf_tensor(f"{tag}_tsb", [128, 128], BF16))
    ident = kb.consts["ident"]
    for w, (src_tm, dst_tm) in enumerate((("k0tm", "sk0tm"), ("v0tm", "sv0tm"))):
        kb.wait("gpsimd", f"d_{tag}t{w}", kb.cnt[f"d_{tag}t{w}"])
        ie = None
        for i in range(NSEL // 128):
            ie = kb.dma("gpsimd", f"d_{tag}gi{w}", lambda e, w=w, i=i: e.dma_start(
                out=idxs[:, i:i + 1],
                in_=g["selidx"][w].rearrange("(i p) -> i p", p=128)[i][:, None]))
        for i in range(NSEL // 128):
            kb.wait_edge("gpsimd", ie)
            ge = kb.dma("gpsimd", f"d_{tag}g{w}", lambda e, i=i, src_tm=src_tm: e.indirect_dma_start(
                out=gat[:, :], out_offset=None, in_=g[src_tm][:, :],
                in_offset=bass.IndirectOffsetOnAxis(ap=idxs[:, i:i + 1], axis=0)))
            kb.wait_edge("sync", ge)
            ste = kb.dma("sync", f"d_{tag}g{w}s", lambda e, i=i, dst_tm=dst_tm: e.dma_start(
                out=g[dst_tm][i * 128:(i + 1) * 128, :], in_=gat[:, :]))
            if w == 0:
                for dc in range(NDC):
                    kb.wait_edge("tensor", ge)
                    kb.wait_edge("tensor", kb.bank_free[7])
                    te = kb.mm(lambda e, dc=dc: e.transpose(
                        kb.psum_bf[:, 0:128], gat[:, dc * 128:(dc + 1) * 128], ident[:, :]),
                        inc=True)
                    kb.wait_edge("vector", te)
                    kb.wait("vector", f"d_{tag}g{w}s", kb.cnt[f"d_{tag}g{w}s"])
                    ve = kb.op("vector", lambda e: e.tensor_copy(tsb[:, :], kb.psum_bf[:, 0:128]))
                    kb.bank_free[7] = ve
                    kb.wait_edge("sync", ve)
                    kb.dma("sync", f"d_{tag}g{w}s", lambda e, dc=dc, i=i: e.dma_start(
                        out=g["sk0T"][dc, :, i * 128:(i + 1) * 128], in_=tsb[:, :]))
            kb.wait("gpsimd", f"d_{tag}g{w}s", kb.cnt[f"d_{tag}g{w}s"])
            if w == 0:
                kb.wait_edge("gpsimd", kb.bank_free[7])


def emit_comp0(kb, g, pctx):
    nc = kb.nc
    E = pctx.enter_context
    tag = "cmp0"
    stg = [E(nc.sbuf_tensor(f"{tag}_st{i}", [128, S], BF16)) for i in range(2)]
    rkv = E(nc.sbuf_tensor(f"{tag}_rkv", [128, NDC * 2 * S], BF16))
    wsl = [E(nc.sbuf_tensor(f"{tag}_w{i}", [128, NDC * TG], BF16)) for i in range(2)]
    bcf = E(nc.sbuf_tensor(f"{tag}_bcf", [1, D], F32))
    bcb = E(nc.sbuf_tensor(f"{tag}_bcb", [1, D], BF16))
    csb = E(nc.sbuf_tensor(f"{tag}_csb", [128, 2 * NB], BF16))
    tsb = E(nc.sbuf_tensor(f"{tag}_tsb", [128, 128], BF16))
    sqt = E(nc.sbuf_tensor(f"{tag}_sqt", [128, NB], BF16))
    ck2 = E(nc.sbuf_tensor(f"{tag}_ck2", [1, NB + 2], F32))
    qnr = E(nc.sbuf_tensor(f"{tag}_qnr", [1, S], F32))
    nm2 = E(nc.sbuf_tensor(f"{tag}_nm2", [1, S], BF16))
    ones = kb.consts["ones_bf"]
    onesc = kb.consts["onescol_bf"]
    ident = kb.consts["ident"]

    be = kb.dma("sync", f"d_{tag}b", lambda e: e.dma_start(out=bcf[:, :], in_=g["bcomp"]))
    kb.wait_edge("vector", be)
    bce = kb.op("vector", lambda e: e.tensor_copy(bcb[:, :], bcf[:, :]))
    qle = kb.dma("sync", f"d_{tag}q", lambda e: e.dma_start(out=qnr[:, :], in_=g["qnorm"][0:1, :]))

    ve = None
    for dc in range(NDC):
        for w, hb in enumerate(("k0T", "v0T")):
            sl = (dc * 2 + w) % 2
            kb.wait_edge("sync", ve)
            le = kb.dma("sync", f"d_{tag}l{sl}", lambda e, hb=hb, dc=dc, sl=sl: e.dma_start(
                out=stg[sl][:, :], in_=g[hb][dc]))
            kb.wait_edge("vector", le)
            ve = kb.op("vector", lambda e, dc=dc, w=w, sl=sl: e.tensor_copy(
                rkv[:, :].rearrange("p (dc j two n) -> p dc j two n", dc=NDC, j=CBS, two=2)[:, dc, :, w, :],
                stg[sl][:, :].rearrange("p (n j) -> p j n", j=CBS)))

    kb.wait_edge("tensor", ve)
    slab_last = [None, None]
    for sw in range(2):
        for b in range(4):
            kb.wait_edge("tensor", kb.bank_free[b])
        for j in range(CBS):
            sl = j % 2
            kb.wait_edge("gpsimd", slab_last[sl])
            we = kb.dma("gpsimd", f"d_{tag}w{sl}", lambda e, j=j, sw=sw, sl=sl: e.dma_start(
                out=wsl[sl][:, :].rearrange("p (dc o) -> p dc o", dc=NDC),
                in_=g["wcompT"][j].rearrange("(dc p) o -> p dc o", p=128)[:, :, sw * TG:(sw + 1) * TG]))
            kb.wait_edge("tensor", we)
            for b in range(4):
                oc = sw * 4 + b
                for dic in range(NDC):
                    mme = kb.mm(lambda e, sl=sl, b=b, dic=dic, j=j: e.matmul(
                        kb.psum[b][:, 0:2 * NB],
                        wsl[sl][:, dic * TG + b * 128:(dic * TG + (b + 1) * 128)],
                        rkv[:, :].rearrange("p (dc r) -> p dc r", dc=NDC)[:, dic,
                            j * 2 * NB:(j + 1) * 2 * NB],
                        start=(j == 0 and dic == 0), stop=False),
                        inc=(dic == NDC - 1))
            slab_last[sl] = mme
        for b in range(4):
            oc = sw * 4 + b
            kb.wait_edge("tensor", bce)
            mme = kb.mm(lambda e, b=b, oc=oc: e.matmul(
                kb.psum[b][:, 0:2 * NB], bcb[0:1, oc * 128:(oc + 1) * 128], ones[:, 0:2 * NB],
                start=False, stop=True), inc=True)
            kb.wait_edge("vector", mme)
            ve = kb.op("vector", lambda e, b=b: e.tensor_copy(csb[:, :], kb.psum[b][:, 0:2 * NB]))
            kb.bank_free[b] = ve
            kb.wait_edge("sync", ve)
            kb.dma("sync", f"d_{tag}s", lambda e, oc=oc: e.dma_start(
                out=g["ck0T"][oc], in_=csb[:, 0:NB]))
            kb.dma("sync", f"d_{tag}s", lambda e, oc=oc: e.dma_start(
                out=g["cv0T"][oc], in_=csb[:, NB:2 * NB]))
            kb.wait_edge("scalar", ve)
            sqe = kb.op("scalar", lambda e: e.activation(sqt[:, :], csb[:, 0:NB], AF.Square))
            kb.wait_edge("tensor", sqe)
            if oc == 0:
                kb.wait_edge("tensor", kb.bank_free[4])
            sme = kb.mm(lambda e, oc=oc: e.matmul(kb.psum[4][0:1, 0:NB], onesc[:, :], sqt[:, :],
                                                  start=(oc == 0), stop=(oc == 7)),
                        inc=(oc == 7))
            for w, dst in enumerate(("ck0tm", "cv0tm")):
                kb.wait_edge("tensor", kb.bank_free[7])
                te = kb.mm(lambda e, w=w: e.transpose(
                    kb.psum_bf[:, 0:128], csb[:, w * NB:(w + 1) * NB], ident[:, :]), inc=True)
                kb.wait_edge("vector", te)
                kb.wait("vector", f"d_{tag}s", kb.cnt[f"d_{tag}s"])
                ve2 = kb.op("vector", lambda e: e.tensor_copy(tsb[:, :], kb.psum_bf[:, 0:128]))
                kb.bank_free[7] = ve2
                kb.wait_edge("sync", ve2)
                kb.dma("sync", f"d_{tag}s", lambda e, dst=dst, oc=oc: e.dma_start(
                    out=g[dst][:, oc * 128:(oc + 1) * 128], in_=tsb[:, :]))
            kb.wait("vector", f"d_{tag}s", kb.cnt[f"d_{tag}s"])
            kb.wait_edge("vector", sqe)
            kb.wait_edge("vector", ("sd_pe", kb.cnt["sd_pe"]))
    kb.wait_edge("vector", sme)
    ve = kb.op("vector", lambda e: e.reduce_max(ck2[:, NB:NB + 1], kb.psum[4][0:1, 0:NB], axis=AX.X))
    kb.bank_free[4] = ve
    kb.wait_edge("scalar", ve)
    se = kb.op("scalar", lambda e: e.activation(ck2[:, NB + 1:NB + 2], ck2[:, NB:NB + 1], AF.Sqrt))
    kb.wait_edge("vector", se)
    kb.wait("vector", f"d_{tag}q", kb.cnt[f"d_{tag}q"])
    ve = kb.op("vector", lambda e: e.tensor_scalar(nm2[:, :], qnr[:, :], ck2[:, NB + 1:NB + 2], -1.0,
                                                   ALU.mult, ALU.mult))
    kb.wait_edge("sync", ve)
    kb.dma("sync", f"d_{tag}s", lambda e: e.dma_start(out=g["negm"][1:2, :], in_=nm2[:, :]))


def emit_ln(kb, g, pctx, x_hbm, xh_hbm, xl_hbm):
    nc = kb.nc
    E = pctx.enter_context
    ph = "ln_" + xh_hbm.tensor.name
    xt = [E(nc.sbuf_tensor(f"{ph}_x{i}", [128, NDC * TG], F32)) for i in range(2)]
    xb = E(nc.sbuf_tensor(f"{ph}_xb", [128, NDC * TG], BF16))
    xlo = E(nc.sbuf_tensor(f"{ph}_xlo", [128, NDC * TG], BF16))
    sq = E(nc.sbuf_tensor(f"{ph}_sq", [128, NDC * TG], BF16))
    xp = E(nc.sbuf_tensor(f"{ph}_xp", [128, NDC * TG], BF16))
    rows = E(nc.sbuf_tensor(f"{ph}_rows", [1, 6 * TG], F32))
    xnf = E(nc.sbuf_tensor(f"{ph}_xnf", [128, NDC * TG], F32))
    xh = E(nc.sbuf_tensor(f"{ph}_xh", [128, NDC * TG], BF16))
    xl = E(nc.sbuf_tensor(f"{ph}_xl", [128, NDC * TG], BF16))
    onesc = kb.consts["onescol_bf"]
    twosc = kb.consts["twoscol_bf"]
    onesrow_f = kb.consts["ones_f"][:, 0:128]
    mean = rows[:, 0 * TG:1 * TG]
    m2 = rows[:, 1 * TG:2 * TG]
    var = rows[:, 2 * TG:3 * TG]
    sd = rows[:, 3 * TG:4 * TG]
    rcp = rows[:, 4 * TG:5 * TG]
    mrs = rows[:, 5 * TG:6 * TG]

    prev_mm = None
    st_edges = []
    for tg in range(NTG):
        sl = tg % 2
        kb.wait_edge("sync", st_edges[tg - 2][1] if tg >= 2 else None)
        ld = kb.dma("sync", f"d_{ph}{sl}", lambda e, tg=tg, sl=sl: e.dma_start(
            out=xt[sl][:, :].rearrange("p (c t) -> p c t", c=NDC),
            in_=x_hbm[:, :, tg * TG:(tg + 1) * TG].rearrange("c p t -> p c t")))
        kb.wait_edge("vector", ld)
        kb.wait_edge("vector", prev_mm)
        kb.wait_edge("scalar", prev_mm)
        ce = se = None
        for c in range(NDC):
            ce0 = kb.op("vector", lambda e, c=c, sl=sl: e.tensor_copy(
                xb[:, c * TG:(c + 1) * TG], xt[sl][:, c * TG:(c + 1) * TG]))
            kb.op("vector", lambda e, c=c, sl=sl: e.tensor_sub(
                xlo[:, c * TG:(c + 1) * TG], xt[sl][:, c * TG:(c + 1) * TG],
                xb[:, c * TG:(c + 1) * TG]))
            ce = kb.op("vector", lambda e, c=c: e.tensor_mul(
                xp[:, c * TG:(c + 1) * TG], xb[:, c * TG:(c + 1) * TG],
                xlo[:, c * TG:(c + 1) * TG]))
            kb.wait_edge("scalar", ce0)
            se = kb.op("scalar", lambda e, c=c: e.activation(
                sq[:, c * TG:(c + 1) * TG], xb[:, c * TG:(c + 1) * TG], AF.Square))
        kb.wait_edge("tensor", se)
        kb.wait_edge("tensor", ce)
        kb.wait_edge("tensor", kb.bank_free[0])
        kb.wait_edge("tensor", kb.bank_free[1])
        for c in range(NDC):
            kb.mm(lambda e, c=c: e.matmul(kb.psum[0][0:1, :], onesc[:, :],
                                          xb[:, c * TG:(c + 1) * TG],
                                          start=(c == 0), stop=False))
            kb.mm(lambda e, c=c: e.matmul(kb.psum[0][0:1, :], onesc[:, :],
                                          xlo[:, c * TG:(c + 1) * TG],
                                          start=False, stop=(c == NDC - 1)))
            kb.mm(lambda e, c=c: e.matmul(kb.psum[1][0:1, :], onesc[:, :],
                                          sq[:, c * TG:(c + 1) * TG],
                                          start=(c == 0), stop=False))
            prev_mm = kb.mm(lambda e, c=c: e.matmul(kb.psum[1][0:1, :], twosc[:, :],
                                                    xp[:, c * TG:(c + 1) * TG],
                                                    start=False, stop=(c == NDC - 1)),
                            inc=(c == NDC - 1))
        kb.wait_edge("vector", prev_mm)
        kb.op("vector", lambda e: e.tensor_scalar_mul(mean, kb.psum[0][0:1, :], 1.0 / D))
        kb.op("vector", lambda e: e.tensor_scalar_mul(m2, kb.psum[1][0:1, :], 1.0 / D))
        kb.op("vector", lambda e: e.tensor_tensor(var, mean, mean, ALU.mult))
        ve = kb.op("vector", lambda e: e.tensor_sub(var, m2, var))
        kb.bank_free[0] = kb.bank_free[1] = ve
        kb.wait_edge("scalar", ve)
        se = kb.op("scalar", lambda e: e.activation(sd, var, AF.Sqrt, bias=1e-5))
        kb.wait_edge("vector", se)
        kb.op("vector", lambda e: e.reciprocal(rcp, sd))
        ve = kb.op("vector", lambda e: e.tensor_mul(mrs, mean, rcp))
        kb.wait_edge("tensor", ve)
        kb.wait_edge("tensor", kb.bank_free[2])
        kb.wait_edge("tensor", kb.bank_free[3])
        kb.mm(lambda e: e.matmul(kb.psum[2][:, :], onesrow_f, rcp, start=True, stop=True))
        bce = kb.mm(lambda e: e.matmul(kb.psum[3][:, :], onesrow_f, mrs, start=True, stop=True),
                    inc=True)
        kb.wait_edge("vector", bce)
        kb.wait("vector", "sd_scalar", kb.cnt["sd_scalar"])
        kb.wait("vector", "sd_gpsimd", kb.cnt["sd_gpsimd"])
        kb.wait("scalar", f"d_{ph}s", kb.cnt.get(f"d_{ph}s", 0))
        kb.wait("gpsimd", f"d_{ph}s", kb.cnt.get(f"d_{ph}s", 0))
        sub_e = []
        for c in range(NDC):
            kb.op("vector", lambda e, c=c, sl=sl: e.tensor_mul(
                xnf[:, c * TG:(c + 1) * TG], xt[sl][:, c * TG:(c + 1) * TG], kb.psum[2][:, :]))
            sub_e.append(kb.op("vector", lambda e, c=c: e.tensor_sub(
                xnf[:, c * TG:(c + 1) * TG], xnf[:, c * TG:(c + 1) * TG], kb.psum[3][:, :])))
        he_l = []
        for c in range(NDC):
            kb.wait_edge("scalar", sub_e[c])
            he = kb.op("scalar", lambda e, c=c: e.copy(
                xh[:, c * TG:(c + 1) * TG], xnf[:, c * TG:(c + 1) * TG]))
            he_l.append(he)
        for c in range(NDC):
            kb.wait_edge("gpsimd", he_l[c])
            le = kb.op("gpsimd", lambda e, c=c: e.tensor_sub(
                xl[:, c * TG:(c + 1) * TG], xnf[:, c * TG:(c + 1) * TG],
                xh[:, c * TG:(c + 1) * TG]))
        kb.wait_edge("sync", he_l[-1])
        kb.wait_edge("sync", le)
        kb.dma("sync", f"d_{ph}s", lambda e, tg=tg: e.dma_start(
            out=xh_hbm[:, :, tg * TG:(tg + 1) * TG].rearrange("c p t -> p c t"),
            in_=xh[:, :].rearrange("p (c t) -> p c t", c=NDC)))
        kb.dma("sync", f"d_{ph}s", lambda e, tg=tg: e.dma_start(
            out=xl_hbm[:, :, tg * TG:(tg + 1) * TG].rearrange("c p t -> p c t"),
            in_=xl[:, :].rearrange("p (c t) -> p c t", c=NDC)))
        st_edges.append(((f"d_{ph}s", kb.cnt[f"d_{ph}s"]), sub_e[-1]))
        kb.bank_free[2] = kb.bank_free[3] = sub_e[-1]


def emit_banded(kb, g, pctx, tag, nq, kv_nc, qT_hbm, kT_hbm, vtm_hbm, negm_row, out_hbm):
    nc = kb.nc
    E = pctx.enter_context
    MAXKC = min(kv_nc, 8)
    q_sb = [E(nc.sbuf_tensor(f"{tag}_q{i}", [128, NDC * TG], BF16)) for i in range(2)]
    k_sb = [E(nc.sbuf_tensor(f"{tag}_k{i}", [128, NDC * MAXKC * 128], BF16)) for i in range(2)]
    v_sb = [E(nc.sbuf_tensor(f"{tag}_v{i}", [128, MAXKC * D], BF16)) for i in range(2)]
    e_sb = E(nc.sbuf_tensor(f"{tag}_e", [128, MAXKC * TG], BF16))
    mcol = E(nc.sbuf_tensor(f"{tag}_mc", [128, 8], F32))
    nmr = E(nc.sbuf_tensor(f"{tag}_nmr", [1, TG], BF16))
    mrow_f = E(nc.sbuf_tensor(f"{tag}_mrf", [1, TG], F32))
    rs_sb = E(nc.sbuf_tensor(f"{tag}_rs", [1, TG], F32))
    rb_sb = E(nc.sbuf_tensor(f"{tag}_rb", [128, TG], F32))
    o_sb = [E(nc.sbuf_tensor(f"{tag}_o{i}", [128, TG], BF16)) for i in range(4)]
    onesc = kb.consts["onescol_bf"]
    onesrow_b = kb.consts["ones_bf"][:, 0:128]
    onesrow_f = kb.consts["ones_f"][:, 0:128]
    tri_ge, tri_le = kb.consts["tri_ge"], kb.consts["tri_le"]

    ngr = (nq + TG - 1) // TG
    slot_done = [None, None]
    o_store = [None] * 4
    for qg in range(ngr):
        qlo = qg * TG
        W = min(TG, nq - qlo)
        ns = W // 128
        s_first, s_last = qlo // 128, (qlo + W - 1) // 128
        kclo = max(0, s_first - 2)
        kchi = min(kv_nc - 1, s_last + 2)
        nkc = kchi - kclo + 1
        sl = qg % 2
        kb.wait_edge("sync", slot_done[sl])
        lde = kb.dma("sync", f"d_{tag}q{sl}", lambda e, qlo=qlo, W=W, sl=sl: e.dma_start(
            out=q_sb[sl][:, :].rearrange("p (dc t) -> p dc t", t=TG)[:, :, 0:W],
            in_=qT_hbm[:, :, qlo:qlo + W].rearrange("dc p t -> p dc t")))
        lde = kb.dma("sync", f"d_{tag}q{sl}", lambda e, kclo=kclo, nkc=nkc, sl=sl: e.dma_start(
            out=k_sb[sl][:, :].rearrange("p (dc t) -> p dc t", t=MAXKC * 128)[:, :, 0:nkc * 128],
            in_=kT_hbm[:, :, kclo * 128:(kclo + nkc) * 128].rearrange("dc p t -> p dc t")))
        lde = kb.dma("sync", f"d_{tag}q{sl}", lambda e, kclo=kclo, nkc=nkc, sl=sl: e.dma_start(
            out=v_sb[sl][:, 0:nkc * D].rearrange("p (kc d) -> p kc d", d=D),
            in_=vtm_hbm[kclo * 128:(kclo + nkc) * 128, :].rearrange("(kc p) d -> p kc d", p=128)))
        kb.wait_edge("tensor", lde)

        for si in range(ns):
            s = s_first + si
            kcs = [kc for kc in range(kclo, kchi + 1) if abs(kc - s) <= 2]
            for ki, kc in enumerate(kcs):
                bk = 3 + (ki % 2)
                kb.wait_edge("tensor", kb.bank_free[bk])
                for dc in range(NDC):
                    mqe = kb.mm(lambda e, dc=dc, s=s, kc=kc, sl=sl, bk=bk, qlo=qlo, kclo=kclo: e.matmul(
                        kb.psum[bk][:, 0:128],
                        q_sb[sl][:, dc * TG + s * 128 - qlo: dc * TG + (s + 1) * 128 - qlo],
                        k_sb[sl][:, dc * MAXKC * 128 + (kc - kclo) * 128:
                                 dc * MAXKC * 128 + (kc - kclo + 1) * 128],
                        start=(dc == 0), stop=(dc == NDC - 1)), inc=(dc == NDC - 1))
                kb.wait_edge("vector", mqe)
                if ki == 0:
                    ve = kb.op("vector", lambda e, si=si, bk=bk: e.reduce_max(
                        mcol[:, si:si + 1], kb.psum[bk][:, 0:128], axis=AX.X))
                else:
                    ve = kb.op("vector", lambda e, bk=bk: e.reduce_max(
                        mcol[:, 7:8], kb.psum[bk][:, 0:128], axis=AX.X))
                    ve = kb.op("vector", lambda e, si=si: e.tensor_max(
                        mcol[:, si:si + 1], mcol[:, si:si + 1], mcol[:, 7:8]))
                kb.bank_free[bk] = ve
            kb.wait_edge("sync", ve)
            kb.dma("sync", f"d_{tag}mc", lambda e, si=si, s=s: e.dma_start(
                out=g["mrow"][0:1, s * 128:(s + 1) * 128],
                in_=mcol[:, si:si + 1]))
        kb.wait("sync", f"d_{tag}mc", kb.cnt[f"d_{tag}mc"])
        mre = kb.dma("sync", f"d_{tag}mr", lambda e, qlo=qlo, W=W: e.dma_start(
            out=mrow_f[:, 0:W], in_=g["mrow"][0:1, qlo:qlo + W]))
        kb.wait_edge("vector", mre)
        nve = kb.op("vector", lambda e, W=W: e.tensor_scalar_mul(nmr[:, 0:W], mrow_f[:, 0:W], -1.0))
        kb.wait_edge("tensor", nve)

        exp_edges = []
        for kci in range(nkc):
            kc = kclo + kci
            a = max(0, (kc - 2) * 128 - qlo)
            b = min(W, (kc + 3) * 128 - qlo)
            eb = kb.psum[kci % 2]
            if a > 0 or b < W:
                me = kb.op("vector", lambda e, kci=kci, W=W: e.memset(e_sb[:, kci * TG: kci * TG + W], 0.0))
            kb.wait_edge("tensor", kb.bank_free[kci % 2])
            for dc in range(NDC):
                kb.mm(lambda e, dc=dc, kci=kci, a=a, b=b, sl=sl, eb=eb: e.matmul(
                    eb[:, 0:b - a],
                    k_sb[sl][:, dc * MAXKC * 128 + kci * 128: dc * MAXKC * 128 + (kci + 1) * 128],
                    q_sb[sl][:, dc * TG + a: dc * TG + b], start=(dc == 0), stop=False))
            mme = kb.mm(lambda e, a=a, b=b, eb=eb: e.matmul(
                eb[:, 0:b - a], onesrow_b, nmr[:, a:b],
                start=False, stop=True), inc=True)
            kb.wait_edge("scalar", mme)
            if a > 0 or b < W:
                kb.wait_edge("scalar", me)
            xe = kb.op("scalar", lambda e, kci=kci, a=a, b=b, eb=eb: e.activation(
                e_sb[:, kci * TG + a: kci * TG + b], eb[:, 0:b - a], AF.Exp, scale=SCALE0))
            kb.bank_free[kci % 2] = xe
            exp_edges.append(xe)
            for s_m, tri in ((kc - 2, tri_le), (kc + 2, tri_ge)):
                if s_m < s_first or s_m > s_last:
                    continue
                off = s_m * 128 - qlo
                kb.wait_edge("vector", xe)
                kb.op("vector", lambda e, kci=kci, off=off, tri=tri: e.tensor_mul(
                    e_sb[:, kci * TG + off: kci * TG + off + 128],
                    e_sb[:, kci * TG + off: kci * TG + off + 128], tri[:, :]))
        kb.wait("tensor", "sd_scalar", kb.cnt["sd_scalar"])
        kb.wait("tensor", "sd_vector", kb.cnt["sd_vector"])
        kb.wait_edge("tensor", kb.bank_free[2])
        for kci in range(nkc):
            sme = kb.mm(lambda e, kci=kci, W=W, nkc=nkc: e.matmul(
                kb.psum[2][0:1, 0:W], onesc[:, :], e_sb[:, kci * TG: kci * TG + W],
                start=(kci == 0), stop=(kci == nkc - 1)), inc=(kci == nkc - 1))
        kb.wait_edge("vector", sme)
        re = kb.op("vector", lambda e, W=W: e.reciprocal(rs_sb[:, 0:W], kb.psum[2][0:1, 0:W]))
        kb.bank_free[2] = re
        kb.wait_edge("tensor", re)
        kb.wait_edge("tensor", kb.bank_free[0])
        bce = kb.mm(lambda e, W=W: e.matmul(kb.psum[0][:, 0:W], onesrow_f, rs_sb[:, 0:W],
                                            start=True, stop=True), inc=True)
        kb.wait_edge("scalar", bce)
        ce = kb.op("scalar", lambda e, W=W: e.copy(rb_sb[:, 0:W], kb.psum[0][:, 0:W]))
        kb.bank_free[0] = ce
        pe_last = None
        for half in range(2):
            for eo in range(4):
                ec = half * 4 + eo
                bk = 3 + eo
                kb.wait_edge("tensor", kb.bank_free[bk])
                for kci in range(nkc):
                    pe_last = kb.mm(lambda e, kci=kci, ec=ec, sl=sl, W=W, bk=bk, nkc=nkc: e.matmul(
                        kb.psum[bk][:, 0:W], v_sb[sl][:, kci * D + ec * 128: kci * D + (ec + 1) * 128],
                        e_sb[:, kci * TG: kci * TG + W],
                        start=(kci == 0), stop=(kci == nkc - 1)), inc=(kci == nkc - 1))
                ob = eo if half == 0 else eo
                ob = (half * 4 + eo) % 4
                kb.wait_edge("vector", pe_last)
                kb.wait_edge("vector", ce)
                kb.wait_edge("vector", o_store[ob])
                ve = kb.op("vector", lambda e, bk=bk, ob=ob, W=W: e.tensor_mul(
                    o_sb[ob][:, 0:W], kb.psum[bk][:, 0:W], rb_sb[:, 0:W]))
                kb.bank_free[bk] = ve
                kb.wait_edge("sync", ve)
                o_store[ob] = kb.dma("sync", f"d_{tag}o{ob}", lambda e, ec=ec, ob=ob, qlo=qlo,
                                     W=W: e.dma_start(
                    out=out_hbm[ec, :, qlo:qlo + W], in_=o_sb[ob][:, 0:W]))
        slot_done[sl] = pe_last


def emit_gcomb0(kb, g, pctx):
    nc = kb.nc
    E = pctx.enter_context
    tag = "gc0"
    xin, lde = load_xin(kb, pctx, tag, g["xh0"])
    kb.wait_edge("tensor", lde)
    wg_f = E(nc.sbuf_tensor(f"{tag}_wgf", [128, NDC * 3], F32))
    wg_b = E(nc.sbuf_tensor(f"{tag}_wgb", [128, NDC * 3], BF16))
    bg_f = E(nc.sbuf_tensor(f"{tag}_bgf", [1, 3], F32))
    bg_b = E(nc.sbuf_tensor(f"{tag}_bgb", [1, 3], BF16))
    wo_sb = E(nc.sbuf_tensor(f"{tag}_wo", [128, NDC * D], BF16))
    bo_f = E(nc.sbuf_tensor(f"{tag}_bof", [1, D], F32))
    bo_b = E(nc.sbuf_tensor(f"{tag}_bob", [1, D], BF16))
    grow = [E(nc.sbuf_tensor(f"{tag}_gr{c}", [1, TG], F32)) for c in range(3)]
    gb = [E(nc.sbuf_tensor(f"{tag}_gb{c}", [128, TG], F32)) for c in range(3)]
    ot = [E(nc.sbuf_tensor(f"{tag}_ot{c}", [128, TG], BF16)) for c in range(3)]
    tt = [E(nc.sbuf_tensor(f"{tag}_tt{c}", [128, TG], F32)) for c in range(2)]
    comb = E(nc.sbuf_tensor(f"{tag}_cb", [128, NDC * TG], BF16))
    o_sb = [E(nc.sbuf_tensor(f"{tag}_o{i}", [128, 4 * TG], BF16)) for i in range(2)]
    ones = kb.consts["ones_bf"]
    onesrow_f = kb.consts["ones_f"][:, 0:128]
    e1 = kb.dma("gpsimd", f"d_{tag}w", lambda e: e.dma_start(
        out=wg_f[:, :], in_=g["wgateT"].rearrange("p dc c -> p (dc c)")))
    e2 = kb.dma("sync", f"d_{tag}b", lambda e: e.dma_start(out=bg_f[:, :], in_=g["bgate"]))
    e3 = kb.dma("sync", f"d_{tag}b", lambda e: e.dma_start(out=bo_f[:, :], in_=g["bo"]))
    kb.dma("gpsimd", f"d_{tag}w", lambda e: e.dma_start(
        out=wo_sb[:, :].rearrange("p (dc o) -> p dc o", dc=NDC),
        in_=g["woT"].rearrange("(dc p) o -> p dc o", p=128)))
    kb.wait("vector", f"d_{tag}w", kb.cnt[f"d_{tag}w"])
    kb.wait("vector", f"d_{tag}b", kb.cnt[f"d_{tag}b"])
    c1 = kb.op("vector", lambda e: e.tensor_copy(wg_b[:, :], wg_f[:, :]))
    c2 = kb.op("vector", lambda e: e.tensor_copy(bg_b[:, :], bg_f[:, :]))
    c3 = kb.op("vector", lambda e: e.tensor_copy(bo_b[:, :], bo_f[:, :]))
    kb.wait_edge("tensor", c3)

    o_store = [None] * 4
    oc_loads = None
    for tg in range(NTG):
        for c in range(3):
            kb.wait_edge("tensor", kb.bank_free[c])
            for dc in range(NDC):
                kb.mm(lambda e, c=c, dc=dc, tg=tg: e.matmul(
                    kb.psum[c][0:1, :], wg_b[:, dc * 3 + c: dc * 3 + c + 1],
                    xin[:, dc * S + tg * TG:(dc * S + (tg + 1) * TG)],
                    start=(dc == 0), stop=False))
            gme = kb.mm(lambda e, c=c: e.matmul(
                kb.psum[c][0:1, :], bg_b[0:1, c:c + 1], ones[:, :],
                start=False, stop=True), inc=True)
        kb.wait_edge("scalar", gme)
        for c in range(3):
            se = kb.op("scalar", lambda e, c=c: e.activation(grow[c][:, :], kb.psum[c][0:1, :],
                                                             AF.Sigmoid))
            kb.bank_free[c] = se
        kb.wait_edge("tensor", se)
        for c in range(3):
            kb.wait_edge("tensor", kb.bank_free[3 + c])
            be = kb.mm(lambda e, c=c: e.matmul(kb.psum[3 + c][:, :], onesrow_f, grow[c][:, :],
                                               start=True, stop=True), inc=True)
        kb.wait_edge("scalar", be)
        kb.wait("scalar", "sd_vector", kb.cnt["sd_vector"])
        for c in range(3):
            ce = kb.op("scalar", lambda e, c=c: e.copy(gb[c][:, :], kb.psum[3 + c][:, :]))
            kb.bank_free[3 + c] = ce
        nb_r = 384 - tg * TG
        ns_r = 1280 - tg * TG
        kb.wait("vector", "sd_pe", kb.cnt.get("sd_pe", 0))
        for dc in range(NDC):
            lw = kb.dma("sync", f"d_{tag}ow", lambda e, dc=dc, tg=tg: e.dma_start(
                out=ot[2][:, :], in_=g["ow0T"][dc, :, tg * TG:(tg + 1) * TG]))
            if ns_r > 0:
                lw = kb.dma("sync", f"d_{tag}ow", lambda e, dc=dc, tg=tg, w=min(TG, ns_r): e.dma_start(
                    out=ot[1][:, 0:w], in_=g["os0T"][dc, :, tg * TG: tg * TG + w]))
            if nb_r > 0:
                lw = kb.dma("sync", f"d_{tag}ow", lambda e, dc=dc, tg=tg, w=min(TG, nb_r): e.dma_start(
                    out=ot[0][:, 0:w], in_=g["oc0T"][dc, :, tg * TG: tg * TG + w]))
            kb.wait_edge("vector", lw)
            kb.wait_edge("vector", ce)
            ve = kb.op("vector", lambda e, dc=dc: e.tensor_mul(tt[0][:, :], ot[2][:, :], gb[2][:, :]))
            if ns_r > 0:
                w = min(TG, ns_r)
                kb.op("vector", lambda e, w=w: e.tensor_mul(tt[1][:, 0:w], ot[1][:, 0:w], gb[1][:, 0:w]))
                ve = kb.op("vector", lambda e, w=w: e.tensor_add(tt[0][:, 0:w], tt[0][:, 0:w], tt[1][:, 0:w]))
            if nb_r > 0:
                w = min(TG, nb_r)
                kb.op("vector", lambda e, w=w: e.tensor_mul(tt[1][:, 0:w], ot[0][:, 0:w], gb[0][:, 0:w]))
                ve = kb.op("vector", lambda e, w=w: e.tensor_add(tt[0][:, 0:w], tt[0][:, 0:w], tt[1][:, 0:w]))
            ve = kb.op("vector", lambda e, dc=dc: e.tensor_copy(comb[:, dc * TG:(dc + 1) * TG], tt[0][:, :]))
            kb.wait("sync", "sd_vector", kb.cnt["sd_vector"])
        kb.wait_edge("tensor", ve)
        for oc in range(NDC):
            bk = oc % 6
            kb.wait_edge("tensor", kb.bank_free[bk])
            for dc in range(NDC):
                kb.mm(lambda e, oc=oc, dc=dc, bk=bk: e.matmul(
                    kb.psum[bk][:, :], wo_sb[:, dc * D + oc * 128: dc * D + (oc + 1) * 128],
                    comb[:, dc * TG:(dc + 1) * TG], start=(dc == 0), stop=False))
            mme = kb.mm(lambda e, oc=oc, bk=bk: e.matmul(
                kb.psum[bk][:, :], bo_b[0:1, oc * 128:(oc + 1) * 128], ones[:, :],
                start=False, stop=True), inc=True)
            ob = (oc // 4) % 2
            kb.wait_edge("vector", mme)
            if oc % 4 == 0:
                kb.wait_edge("vector", o_store[ob])
            ve2 = kb.op("vector", lambda e, ob=ob, bk=bk, oc=oc: e.tensor_copy(
                o_sb[ob][:, (oc % 4) * TG:(oc % 4 + 1) * TG], kb.psum[bk][:, :]))
            kb.bank_free[bk] = ve2
            if oc % 4 == 3:
                kb.wait_edge("sync", ve2)
                o_store[ob] = kb.dma("sync", f"d_{tag}s{ob}", lambda e, oc=oc, ob=ob, tg=tg: e.dma_start(
                    out=g["s0T"][oc - 3:oc + 1, :, tg * TG:(tg + 1) * TG].rearrange("c p t -> p c t"),
                    in_=o_sb[ob][:, :].rearrange("p (c t) -> p c t", c=4)))


def emit_router(kb, g, pctx, tag, xh_hbm, xl_hbm, wgT_name, lg_hbm):
    nc = kb.nc
    E = pctx.enter_context
    wg_f = E(nc.sbuf_tensor(f"{tag}_wgf", [128, NDC * NEXP], F32))
    wg_h = E(nc.sbuf_tensor(f"{tag}_wgh", [128, NDC * NEXP], BF16))
    wg_l = E(nc.sbuf_tensor(f"{tag}_wgl", [128, NDC * NEXP], BF16))
    xh = [E(nc.sbuf_tensor(f"{tag}_xh{i}", [128, TG], BF16)) for i in range(2)]
    xl = [E(nc.sbuf_tensor(f"{tag}_xl{i}", [128, TG], BF16)) for i in range(2)]
    lg_sb = E(nc.sbuf_tensor(f"{tag}_lg", [NEXP, TG], F32))
    we = kb.dma("sync", f"d_{tag}w", lambda e: e.dma_start(
        out=wg_f[:, :], in_=g[wgT_name].rearrange("p dc c -> p (dc c)")))
    kb.wait_edge("vector", we)
    kb.op("vector", lambda e: e.tensor_copy(wg_h[:, :], wg_f[:, :]))
    ce = kb.op("vector", lambda e: e.tensor_sub(wg_l[:, :], wg_f[:, :], wg_h[:, :]))
    kb.wait_edge("tensor", ce)
    slot_pe = [None, None]
    for tg in range(NTG):
        for dc in range(NDC):
            sl = dc % 2
            kb.wait_edge("sync", slot_pe[sl])
            de = kb.dma("sync", f"d_{tag}l{sl}", lambda e, dc=dc, tg=tg, sl=sl: e.dma_start(
                out=xh[sl][:, :], in_=xh_hbm[dc, :, tg * TG:(tg + 1) * TG]))
            de = kb.dma("sync", f"d_{tag}l{sl}", lambda e, dc=dc, tg=tg, sl=sl: e.dma_start(
                out=xl[sl][:, :], in_=xl_hbm[dc, :, tg * TG:(tg + 1) * TG]))
            kb.wait_edge("tensor", de)
            if dc == 0:
                kb.wait_edge("tensor", kb.bank_free[6])
            for ti, (wt_, xt_) in enumerate(((wg_h, xh), (wg_h, xl), (wg_l, xh))):
                mme = kb.mm(lambda e, dc=dc, sl=sl, ti=ti, wt_=wt_, xt_=xt_: e.matmul(
                    kb.psum[6][0:NEXP, :], wt_[:, dc * NEXP:(dc + 1) * NEXP], xt_[sl][:, :],
                    start=(dc == 0 and ti == 0), stop=(dc == NDC - 1 and ti == 2)),
                    inc=(ti == 2))
            slot_pe[sl] = mme
        kb.wait_edge("vector", mme)
        ve = kb.op("vector", lambda e: e.tensor_copy(lg_sb[:, :], kb.psum[6][0:NEXP, :]))
        kb.bank_free[6] = ve
        kb.wait_edge("sync", ve)
        kb.dma("sync", f"d_{tag}s", lambda e, tg=tg: e.dma_start(
            out=lg_hbm[:, tg * TG:(tg + 1) * TG], in_=lg_sb[:, :]))
        kb.wait("vector", f"d_{tag}s", kb.cnt[f"d_{tag}s"])


def emit_ffn0a(kb, g, pctx):
    nc = kb.nc
    E = pctx.enter_context
    tag = "f0a"
    emit_router(kb, g, pctx, tag + "r", g["xh0"], g["xl0"], "wg0T", g["r0lg"])
    lrow = E(nc.sbuf_tensor(f"{tag}_lr", [32, NEXP * 128], F32))
    mx = E(nc.sbuf_tensor(f"{tag}_mx", [32, 128], F32))
    msk = E(nc.sbuf_tensor(f"{tag}_msk", [32, 128], F32))
    nrow = E(nc.sbuf_tensor(f"{tag}_nr", [32, 128], BF16))
    kb.wait("sync", f"d_{tag}rs", kb.cnt.get(f"d_{tag}rs", 0))
    lde = None
    for e_ in range(NEXP):
        lde = kb.dma("sync", f"d_{tag}lg", lambda e, e_=e_: e.dma_start(
            out=lrow[:, e_ * 128:(e_ + 1) * 128],
            in_=g["r0lg"][e_].rearrange("(a b) -> a b", b=128)))
    kb.wait_edge("vector", lde)
    ve = kb.op("vector", lambda e: e.tensor_copy(mx[:, :], lrow[:, 0:128]))
    for e_ in range(1, NEXP):
        ve = kb.op("vector", lambda e, e_=e_: e.tensor_max(mx[:, :], mx[:, :],
                                                           lrow[:, e_ * 128:(e_ + 1) * 128]))
    for e_ in range(NEXP):
        kb.op("vector", lambda e, e_=e_: e.tensor_tensor(msk[:, :], lrow[:, e_ * 128:(e_ + 1) * 128],
                                                         mx[:, :], ALU.is_lt))
        ne = kb.op("vector", lambda e: e.tensor_scalar_mul(nrow[:, :], msk[:, :], NEG))
        kb.wait_edge("sync", ne)
        kb.dma("sync", f"d_{tag}nr", lambda e, e_=e_: e.dma_start(
            out=g["r0neg"][e_].rearrange("(a b) -> a b", b=128), in_=nrow[:, :]))
        kb.wait("vector", f"d_{tag}nr", kb.cnt[f"d_{tag}nr"])
    xin, lde = load_xin(kb, pctx, tag, g["xh0"])
    kb.wait_edge("tensor", lde)
    gw = [E(nc.sbuf_tensor(f"{tag}_gw{i}", [128, NDC * D], BF16)) for i in range(2)]
    uw = [E(nc.sbuf_tensor(f"{tag}_uw{i}", [128, NDC * D], BF16)) for i in range(2)]
    nr_sb = E(nc.sbuf_tensor(f"{tag}_nrs", [1, S], BF16))
    tmp = E(nc.sbuf_tensor(f"{tag}_tmp", [128, TG], F32))
    h_sb = [E(nc.sbuf_tensor(f"{tag}_h{i}", [128, NDC * TG], BF16)) for i in range(2)]
    onesrow_b = kb.consts["ones_bf"][:, 0:128]
    slab_last = [None, None]
    h_store = [None] * 2
    nre = None
    for e_ in range(NEXP):
        sl = e_ % 2
        kb.wait_edge("gpsimd", slab_last[sl])
        for wsl, bank in ((gw, "gbank"), (uw, "ubank")):
            we = kb.dma("gpsimd", f"d_{tag}w{sl}", lambda e, e_=e_, sl=sl, wsl=wsl, bank=bank:
                        e.dma_start(out=wsl[sl][:, :].rearrange("p (dc o) -> p dc o", dc=NDC),
                                    in_=g[bank][e_].rearrange("(dc p) o -> p dc o", p=128)))
        kb.wait_edge("sync", nre)
        nre_ld = kb.dma("sync", f"d_{tag}nl", lambda e, e_=e_: e.dma_start(
            out=nr_sb[:, :], in_=g["r0neg"][e_:e_ + 1, :]))
        kb.wait_edge("tensor", we)
        kb.wait_edge("tensor", nre_ld)
        for tg in range(NTG):
            for hc in range(NDC):
                bka, bku = hc % 2, 2 + hc % 2
                kb.wait_edge("tensor", kb.bank_free[bka])
                for dc in range(NDC):
                    kb.mm(lambda e, sl=sl, dc=dc, hc=hc, tg=tg, bka=bka: e.matmul(
                        kb.psum[bka][:, :], gw[sl][:, dc * D + hc * 128: dc * D + (hc + 1) * 128],
                        xin[:, dc * S + tg * TG: dc * S + (tg + 1) * TG],
                        start=(dc == 0), stop=False))
                ae = kb.mm(lambda e, tg=tg, bka=bka: e.matmul(
                    kb.psum[bka][:, :], onesrow_b, nr_sb[:, tg * TG:(tg + 1) * TG],
                    start=False, stop=True), inc=True)
                kb.wait_edge("tensor", kb.bank_free[bku])
                for dc in range(NDC):
                    ue = kb.mm(lambda e, sl=sl, dc=dc, hc=hc, tg=tg, bku=bku: e.matmul(
                        kb.psum[bku][:, :], uw[sl][:, dc * D + hc * 128: dc * D + (hc + 1) * 128],
                        xin[:, dc * S + tg * TG: dc * S + (tg + 1) * TG],
                        start=(dc == 0), stop=(dc == NDC - 1)), inc=(dc == NDC - 1))
                kb.wait_edge("scalar", ae)
                kb.wait("scalar", "sd_vector", kb.cnt["sd_vector"])
                se = kb.op("scalar", lambda e, bka=bka: e.activation(tmp[:, :], kb.psum[bka][:, :], AF.Silu))
                kb.bank_free[bka] = se
                hb = (e_ * NTG + tg) % 2
                kb.wait_edge("vector", se)
                kb.wait_edge("vector", ue)
                if hc == 0:
                    kb.wait_edge("vector", h_store[hb])
                ve = kb.op("vector", lambda e, hb=hb, hc=hc, bku=bku: e.tensor_mul(
                    h_sb[hb][:, hc * TG:(hc + 1) * TG], tmp[:, :], kb.psum[bku][:, :]))
                kb.bank_free[bku] = ve
            kb.wait_edge("sync", ve)
            h_store[hb] = kb.dma("sync", f"d_{tag}h{hb}", lambda e, e_=e_, tg=tg, hb=hb:
                                 e.dma_start(out=g["h0"][e_ * NDC:(e_ + 1) * NDC, :,
                                                         tg * TG:(tg + 1) * TG].rearrange(
                                     "hc p t -> p hc t"),
                                     in_=h_sb[hb][:, :].rearrange("p (hc t) -> p hc t", hc=NDC)))
            slab_last[sl] = ue
        nre = ae


def emit_ffn0b(kb, g, pctx):
    nc = kb.nc
    E = pctx.enter_context
    tag = "f0b"
    dw = E(nc.sbuf_tensor(f"{tag}_dw", [128, NEXP * NDC * 512], BF16))
    h_sb = [E(nc.sbuf_tensor(f"{tag}_h{i}", [128, NDC * TG], BF16)) for i in range(2)]
    s0t = E(nc.sbuf_tensor(f"{tag}_s0", [128, TG], BF16))
    xt = E(nc.sbuf_tensor(f"{tag}_x", [128, TG], F32))
    gt = E(nc.sbuf_tensor(f"{tag}_g", [128, TG], F32))
    pt = E(nc.sbuf_tensor(f"{tag}_p", [128, TG], F32))
    x1t = [E(nc.sbuf_tensor(f"{tag}_x1{i}", [128, 4 * TG], F32)) for i in range(2)]
    we = None
    x1_store = [None, None]
    for sw in range(2):
        kb.wait("gpsimd", "sd_pe", kb.cnt.get("sd_pe", 0) if sw > 0 else 0)
        for e_ in range(NEXP):
            we = kb.dma("gpsimd", f"d_{tag}w{sw}", lambda e, e_=e_, sw=sw: e.dma_start(
                out=dw[:, e_ * NDC * 512: (e_ + 1) * NDC * 512].rearrange(
                    "p (hc o) -> p hc o", hc=NDC),
                in_=g["dbank"][e_].rearrange("(hc p) o -> p hc o", p=128)[:, :, sw * 512:(sw + 1) * 512]))
        kb.wait_edge("tensor", we)
        slot_mm = [None, None]
        for tg in range(NTG):
            for e_ in range(NEXP):
                sl = e_ % 2
                kb.wait_edge("sync", slot_mm[sl])
                lde = kb.dma("sync", f"d_{tag}h{sl}", lambda e, e_=e_, tg=tg, sl=sl:
                             e.dma_start(out=h_sb[sl][:, :].rearrange("p (hc t) -> p hc t", hc=NDC),
                                         in_=g["h0"][e_ * NDC:(e_ + 1) * NDC, :,
                                                     tg * TG:(tg + 1) * TG].rearrange(
                                             "hc p t -> p hc t")))
                kb.wait_edge("tensor", lde)
                if e_ == 0:
                    for b in range(4):
                        kb.wait_edge("tensor", kb.bank_free[b])
                for do in range(4):
                    for hc in range(NDC):
                        mme = kb.mm(lambda e, e_=e_, do=do, hc=hc, sl=sl, sw=sw: e.matmul(
                            kb.psum[do][:, :],
                            dw[:, e_ * NDC * 512 + hc * 512 + do * 128:
                               e_ * NDC * 512 + hc * 512 + (do + 1) * 128],
                            h_sb[sl][:, hc * TG:(hc + 1) * TG],
                            start=(e_ == 0 and hc == 0), stop=(e_ == NEXP - 1 and hc == NDC - 1)),
                            inc=(hc == NDC - 1))
                slot_mm[sl] = mme
                h_last = mme
            for do in range(4):
                dc = sw * 4 + do
                kb.wait("sync", "sd_vector", kb.cnt["sd_vector"])
                l1 = kb.dma("sync", f"d_{tag}e", lambda e, dc=dc, tg=tg: e.dma_start(
                    out=s0t[:, :], in_=g["s0T"][dc, :, tg * TG:(tg + 1) * TG]))
                l2 = kb.dma("sync", f"d_{tag}e", lambda e, dc=dc, tg=tg: e.dma_start(
                    out=xt[:, :], in_=g["x_fm"][dc, :, tg * TG:(tg + 1) * TG]))
                kb.wait_edge("scalar", h_last)
                kb.wait("scalar", "sd_vector", kb.cnt["sd_vector"])
                se = kb.op("scalar", lambda e, do=do: e.activation(gt[:, :], kb.psum[do][:, :],
                                                                   AF.Silu))
                kb.bank_free[do] = se
                kb.wait_edge("vector", se)
                kb.wait_edge("vector", l2)
                sx = tg % 2
                if do == 0:
                    kb.wait_edge("vector", x1_store[sx])
                kb.op("vector", lambda e: e.tensor_mul(pt[:, :], gt[:, :], s0t[:, :]))
                ve = kb.op("vector", lambda e, sx=sx, do=do: e.tensor_add(
                    x1t[sx][:, do * TG:(do + 1) * TG], xt[:, :], pt[:, :]))
            kb.wait_edge("sync", ve)
            x1_store[sx] = kb.dma("sync", f"d_{tag}x1{sx}", lambda e, sw=sw, tg=tg, sx=sx:
                                  e.dma_start(out=g["x1"][sw * 4:(sw + 1) * 4, :,
                                                          tg * TG:(tg + 1) * TG].rearrange(
                                      "c p t -> p c t"),
                                      in_=x1t[sx][:, :].rearrange("p (c t) -> p c t", c=4)))


def emit_qkv1(kb, g, pctx):
    nc = kb.nc
    E = pctx.enter_context
    tag = "qkv1"
    cos_sb = E(nc.sbuf_tensor(f"{tag}_cos", [128, S], F32))
    sin_sb = E(nc.sbuf_tensor(f"{tag}_sin", [128, S], F32))
    qsb = E(nc.sbuf_tensor(f"{tag}_qsb", [128, TG], BF16))
    t1 = E(nc.sbuf_tensor(f"{tag}_t1", [128, TG], F32))
    t2 = E(nc.sbuf_tensor(f"{tag}_t2", [128, TG], F32))
    ro = [E(nc.sbuf_tensor(f"{tag}_ro{i}", [128, TG], BF16)) for i in range(4)]
    ro_store = [None] * 4
    kb.dma("sync", f"d_{tag}cs", lambda e: e.dma_start(out=cos_sb[:, :], in_=g["c_cos"]))
    ce = kb.dma("sync", f"d_{tag}cs", lambda e: e.dma_start(out=sin_sb[:, :], in_=g["c_sin"]))
    kb.wait_edge("vector", ce)
    ptrot = kb.consts["ptrot"]
    ri = [0]

    def rope_epi(oc, tg, bk, mme):
        ten = g["q1T"] if oc < NDC else g["k1T"]
        occ = oc % NDC
        kb.wait_edge("vector", mme)
        kb.wait("vector", "sd_pe", kb.cnt["sd_pe"])
        ve = kb.op("vector", lambda e: e.tensor_copy(qsb[:, :], kb.psum[bk][:, :]))
        kb.bank_free[bk] = ve
        rb = 4 + ri[0] % 2
        ri[0] += 1
        kb.wait_edge("tensor", ve)
        kb.wait_edge("tensor", kb.bank_free[rb])
        rme = kb.mm(lambda e, rb=rb: e.matmul(kb.psum[rb][:, :], ptrot[:, :], qsb[:, :],
                                              start=True, stop=True), inc=True)
        ob = ri[0] % 4
        kb.wait_edge("vector", rme)
        kb.wait_edge("vector", ro_store[ob])
        kb.op("vector", lambda e, tg=tg: e.tensor_mul(t1[:, :], qsb[:, :],
                                                      cos_sb[:, tg * TG:(tg + 1) * TG]))
        kb.op("vector", lambda e, tg=tg, rb=rb: e.tensor_mul(t2[:, :], kb.psum[rb][:, :],
                                                             sin_sb[:, tg * TG:(tg + 1) * TG]))
        ve = kb.op("vector", lambda e, ob=ob: e.tensor_add(ro[ob][:, :], t1[:, :], t2[:, :]))
        kb.bank_free[rb] = ve
        kb.wait_edge("sync", ve)
        ro_store[ob] = kb.dma("sync", f"d_{tag}r{ob}", lambda e, ten=ten, occ=occ, tg=tg, ob=ob:
                              e.dma_start(out=ten[occ, :, tg * TG:(tg + 1) * TG], in_=ro[ob][:, :]))

    xin = emit_linear(kb, g, pctx, tag, g["wqk1T"], 2 * D, lambda oc: ((None, oc)),
                      in_hbm=g["xh1"], banks=(0, 1, 2, 3), epi=rope_epi)
    emit_linear_tm(kb, g, pctx, tag + "v", g["wv1T"], xin, g["v1tm"], banks=(5, 6))


def emit_att1(kb, g, pctx):
    nc = kb.nc
    E = pctx.enter_context
    tag = "a1"
    NKC = S // 128
    kt = [E(nc.sbuf_tensor(f"{tag}_kt{i}", [128, S], BF16)) for i in range(2)]
    qt = [E(nc.sbuf_tensor(f"{tag}_qt{i}", [128, S], BF16)) for i in range(2)]
    vt = [E(nc.sbuf_tensor(f"{tag}_vt{i}", [128, NKC * 130], BF16)) for i in range(2)]
    e_sb = [E(nc.sbuf_tensor(f"{tag}_e{i}", [128, NKC * TG], BF16)) for i in range(2)]
    rs = E(nc.sbuf_tensor(f"{tag}_rs", [1, TG], F32))
    rb = E(nc.sbuf_tensor(f"{tag}_rb", [128, TG], F32))
    o_sb = [E(nc.sbuf_tensor(f"{tag}_o{i}", [64, TG], BF16)) for i in range(2)]
    onesrow_f = kb.consts["ones_f"][:, 0:128]
    tri_le = kb.consts["tri_le"]
    o_store = [None, None]
    chunk_last_pv = [None, None]
    items = [(c, hh, qg) for c in range(NDC) for hh in range(2) for qg in range(NTG)]

    def emit_chunk_loads(c):
        slc = c % 2
        kb.wait_edge("sync", chunk_last_pv[slc])
        kb.wait_edge("gpsimd", chunk_last_pv[slc])
        lde = kb.dma("sync", f"d_{tag}l{slc}", lambda e, c=c, slc=slc: e.dma_start(
            out=kt[slc][:, :], in_=g["k1T"][c]))
        lde = kb.dma("sync", f"d_{tag}l{slc}", lambda e, c=c, slc=slc: e.dma_start(
            out=qt[slc][:, :], in_=g["q1T"][c]))
        me = kb.op("gpsimd", lambda e, slc=slc: e.memset(vt[slc][:, :], 1.0))
        kb.wait_edge("gpsimd", me)
        lde2 = None
        for hh in range(2):
            lde2 = kb.dma("gpsimd", f"d_{tag}v{slc}", lambda e, c=c, hh=hh, slc=slc: e.dma_start(
                out=vt[slc][:, :].rearrange("p (kc z) -> p kc z", z=130)[:, :, hh * 65: hh * 65 + 64],
                in_=g["v1tm"].rearrange("(kc p) ch -> p kc ch", p=128)[
                    :, :, c * 128 + hh * 64: c * 128 + hh * 64 + 64]))
        return (lde, lde2)

    prev_exp = None

    def emit_pv_state(i):
        c, hh, qg = items[i]
        slc = c % 2
        par = i % 2
        nkc = 4 * qg + 4
        pb_ = kb.psum[4 + par]
        state = {}

        def first():
            kb.wait_edge("tensor", state["exp_edge"])
            kb.wait_edge("tensor", state["ve_edge"])
            kb.wait_edge("tensor", kb.bank_free[4 + par])

        mms = []
        for kc in range(nkc):
            def mk(kc):
                def f():
                    if kc == 0:
                        first()
                    return kb.mm(lambda e, kc=kc: e.matmul(
                        pb_[0:65, :], vt[slc][:, kc * 130 + hh * 65: kc * 130 + hh * 65 + 65],
                        e_sb[par][:, kc * TG:(kc + 1) * TG],
                        start=(kc == 0), stop=(kc == nkc - 1)), inc=(kc == nkc - 1))
                return f
            mms.append(mk(kc))

        def finish(pve):
            nonlocal_state = state
            kb.wait_edge("vector", pve)
            kb.wait("vector", "sd_scalar", kb.cnt["sd_scalar"])
            re = kb.op("vector", lambda e: e.reciprocal(rs[:, :], pb_[64:65, :]))
            kb.wait_edge("tensor", re)
            kb.wait_edge("tensor", kb.bank_free[6])
            bce = kb.mm(lambda e: e.matmul(kb.psum[6][:, :], onesrow_f, rs[:, :],
                                           start=True, stop=True), inc=True)
            kb.wait_edge("scalar", bce)
            kb.wait("scalar", "sd_vector", kb.cnt["sd_vector"])
            ce = kb.op("scalar", lambda e: e.copy(rb[:, :], kb.psum[6][:, :]))
            kb.bank_free[6] = ce
            ob = i % 2
            kb.wait_edge("vector", ce)
            kb.wait_edge("vector", o_store[ob])
            ve = kb.op("vector", lambda e, ob=ob: e.tensor_mul(
                o_sb[ob][:, :], pb_[0:64, :], rb[0:64, :]))
            kb.bank_free[4 + par] = ve
            kb.wait_edge("sync", ve)
            o_store[ob] = kb.dma("sync", f"d_{tag}o{ob}", lambda e, c=c, hh=hh, qg=qg, ob=ob:
                                 e.dma_start(out=g["o1"][c, hh * 64:(hh + 1) * 64,
                                                         qg * TG:(qg + 1) * TG],
                                             in_=o_sb[ob][:, :]))
            chunk_last_pv[slc] = pve
        return state, mms, finish

    pend = None
    ld_edges = {0: emit_chunk_loads(0)}

    def drain_pv(n):
        nonlocal pend
        if pend is None:
            return
        state, mms, finish, hold = pend
        while mms and n != 0:
            hold[0] = mms.pop(0)()
            n -= 1
        if not mms:
            finish(hold[0])
            pend = None

    for i, (c, hh, qg) in enumerate(items):
        slc = c % 2
        par = i % 2
        hb = hh * 64
        nkc = 4 * qg + 4
        if hh == 0 and qg == 0:
            le = ld_edges[c]
            kb.wait_edge("tensor", le[0])
            kb.wait_edge("tensor", le[1])
        last_ve = None
        exp_edge = None
        for grp in range(nkc // 2):
            pg = grp % 2
            for j in range(2):
                kc = grp * 2 + j
                bj = pg * 2 + j
                a_ = max(0, (kc - 4 * qg) * 128)
                kb.wait_edge("tensor", kb.bank_free[bj])
                kb.mm(lambda e, kc=kc, bj=bj, a_=a_, hb=hb, qg=qg, slc=slc: e.matmul(
                    kb.psum[bj][:, a_:TG], kt[slc][hb:hb + 64, kc * 128:(kc + 1) * 128],
                    qt[slc][hb:hb + 64, qg * TG + a_:(qg + 1) * TG], start=True, stop=True),
                    inc=(j == 1))
            gme = ("sd_pe", kb.cnt["sd_pe"])
            kb.wait_edge("scalar", gme)
            exp_edge = kb.op("scalar", lambda e, grp=grp, par=par, pg=pg: e.activation(
                e_sb[par][:, grp * 2 * TG:(grp + 1) * 2 * TG],
                kb.psum_quad[:, pg * 2 * TG:(pg + 1) * 2 * TG], AF.Exp,
                scale=SCALE0))
            prev_exp = exp_edge
            kb.bank_free[pg * 2] = kb.bank_free[pg * 2 + 1] = exp_edge
            for j in range(2):
                kc = grp * 2 + j
                a_ = max(0, (kc - 4 * qg) * 128)
                if a_ > 0:
                    kb.wait_edge("vector", exp_edge)
                    last_ve = kb.op("vector", lambda e, kc=kc, a_=a_, par=par: e.memset(
                        e_sb[par][:, kc * TG: kc * TG + a_], 0.0))
                sd = kc - 4 * qg
                if 0 <= sd <= 3:
                    off = sd * 128
                    kb.wait_edge("vector", exp_edge)
                    last_ve = kb.op("vector", lambda e, kc=kc, off=off, par=par: e.tensor_mul(
                        e_sb[par][:, kc * TG + off: kc * TG + off + 128],
                        e_sb[par][:, kc * TG + off: kc * TG + off + 128], tri_le[:, :]))
            drain_pv(4)
        drain_pv(-1)
        st, mms, fin = emit_pv_state(i)
        st["exp_edge"] = exp_edge
        st["ve_edge"] = last_ve if last_ve is not None else exp_edge
        pend = (st, mms, fin, [None])
        if hh == 1 and qg == NTG - 1 and c + 1 < NDC:
            ld_edges[c + 1] = emit_chunk_loads(c + 1)
    drain_pv(-1)


def emit_ffn1a(kb, g, pctx):
    nc = kb.nc
    E = pctx.enter_context
    tag = "f1a"
    emit_router(kb, g, pctx, tag + "r", g["xh1"], g["xl1"], "wg1T", g["r1lg"])
    lrow = E(nc.sbuf_tensor(f"{tag}_lr", [32, NEXP * 128], F32))
    prow = E(nc.sbuf_tensor(f"{tag}_pr", [32, NEXP * 128], F32))
    mx = E(nc.sbuf_tensor(f"{tag}_mx", [32, 128], F32))
    sm = E(nc.sbuf_tensor(f"{tag}_sm", [32, 128], F32))
    t0 = E(nc.sbuf_tensor(f"{tag}_t0", [32, 128], F32))
    wrow = E(nc.sbuf_tensor(f"{tag}_wr", [32, 128], F32))
    nrow = E(nc.sbuf_tensor(f"{tag}_nr", [32, 128], BF16))
    arow = E(nc.sbuf_tensor(f"{tag}_ar", [32, 128], BF16))
    kb.wait("sync", f"d_{tag}rs", kb.cnt.get(f"d_{tag}rs", 0))
    lde = None
    for e_ in range(NEXP):
        lde = kb.dma("sync", f"d_{tag}lg", lambda e, e_=e_: e.dma_start(
            out=lrow[:, e_ * 128:(e_ + 1) * 128],
            in_=g["r1lg"][e_].rearrange("(a b) -> a b", b=128)))
    kb.wait_edge("vector", lde)
    sl_ = lambda t, e_: t[:, e_ * 128:(e_ + 1) * 128]
    kb.op("vector", lambda e: e.tensor_copy(mx[:, :], sl_(lrow, 0)))
    for e_ in range(1, NEXP):
        kb.op("vector", lambda e, e_=e_: e.tensor_max(mx[:, :], mx[:, :], sl_(lrow, e_)))
    for e_ in range(NEXP):
        ve = kb.op("vector", lambda e, e_=e_: e.tensor_sub(sl_(prow, e_), sl_(lrow, e_), mx[:, :]))
    kb.wait_edge("scalar", ve)
    for e_ in range(NEXP):
        se = kb.op("scalar", lambda e, e_=e_: e.activation(sl_(prow, e_), sl_(prow, e_), AF.Exp))
    kb.wait_edge("vector", se)
    kb.op("vector", lambda e: e.tensor_copy(sm[:, :], sl_(prow, 0)))
    for e_ in range(1, NEXP):
        kb.op("vector", lambda e, e_=e_: e.tensor_add(sm[:, :], sm[:, :], sl_(prow, e_)))
    kb.op("vector", lambda e: e.reciprocal(sm[:, :], sm[:, :]))
    for e_ in range(NEXP):
        kb.op("vector", lambda e, e_=e_: e.tensor_mul(sl_(prow, e_), sl_(prow, e_), sm[:, :]))
    kb.op("vector", lambda e: e.tensor_copy(mx[:, :], sl_(prow, 0)))
    for e_ in range(1, NEXP):
        kb.op("vector", lambda e, e_=e_: e.tensor_max(mx[:, :], mx[:, :], sl_(prow, e_)))
    first = True
    for e_ in range(NEXP):
        kb.op("vector", lambda e, e_=e_: e.tensor_tensor(t0[:, :], sl_(prow, e_), mx[:, :], ALU.is_ge))
        kb.op("vector", lambda e, e_=e_: e.tensor_scalar(t0[:, :], t0[:, :], -2.0, None, ALU.mult))
        kb.op("vector", lambda e, e_=e_: e.tensor_add(t0[:, :], t0[:, :], sl_(prow, e_)))
        if first:
            kb.op("vector", lambda e: e.tensor_copy(sm[:, :], t0[:, :]))
            first = False
        else:
            kb.op("vector", lambda e: e.tensor_max(sm[:, :], sm[:, :], t0[:, :]))
    for e_ in range(NEXP):
        kb.op("vector", lambda e, e_=e_: e.tensor_tensor(t0[:, :], sl_(prow, e_), sm[:, :], ALU.is_ge))
        ae = kb.op("vector", lambda e, e_=e_: e.tensor_mul(wrow[:, :], sl_(prow, e_), t0[:, :]))
        a2 = kb.op("vector", lambda e: e.tensor_copy(arow[:, :], t0[:, :]))
        kb.op("vector", lambda e: e.tensor_scalar(t0[:, :], t0[:, :], -1.0, -NEG, ALU.add, ALU.mult))
        ne = kb.op("vector", lambda e: e.tensor_copy(nrow[:, :], t0[:, :]))
        kb.wait_edge("sync", ne)
        kb.dma("sync", f"d_{tag}rw", lambda e, e_=e_: e.dma_start(
            out=g["r1w"][e_].rearrange("(a b) -> a b", b=128), in_=wrow[:, :]))
        kb.dma("sync", f"d_{tag}rw", lambda e, e_=e_: e.dma_start(
            out=g["r1act"][e_].rearrange("(a b) -> a b", b=128), in_=arow[:, :]))
        kb.dma("sync", f"d_{tag}rw", lambda e, e_=e_: e.dma_start(
            out=g["r1neg"][e_].rearrange("(a b) -> a b", b=128), in_=nrow[:, :]))
        kb.wait("vector", f"d_{tag}rw", kb.cnt[f"d_{tag}rw"])
    xin, lde = load_xin(kb, pctx, tag, g["xh1"])
    kb.wait_edge("tensor", lde)
    w1s = [E(nc.sbuf_tensor(f"{tag}_w1{i}", [128, NDC * D], BF16)) for i in range(2)]
    w3s = [E(nc.sbuf_tensor(f"{tag}_w3{i}", [128, NDC * D], BF16)) for i in range(2)]
    b1c = E(nc.sbuf_tensor(f"{tag}_b1c", [128, NEXP * NEXP], F32))
    b3c = E(nc.sbuf_tensor(f"{tag}_b3c", [128, NEXP * NEXP], F32))
    wr_sb = E(nc.sbuf_tensor(f"{tag}_wrs", [1, S], F32))
    nr_sb = E(nc.sbuf_tensor(f"{tag}_nrs", [1, S], BF16))
    wb_sb = E(nc.sbuf_tensor(f"{tag}_wb", [128, TG], F32))
    nb_sb = E(nc.sbuf_tensor(f"{tag}_nb", [128, TG], F32))
    tmp = [E(nc.sbuf_tensor(f"{tag}_tmp{i}", [128, TG], F32)) for i in range(2)]
    tmp2 = [E(nc.sbuf_tensor(f"{tag}_tmp2{i}", [128, TG], F32)) for i in range(2)]
    h_sb = [E(nc.sbuf_tensor(f"{tag}_h{i}", [128, NDC * TG], BF16)) for i in range(2)]
    onesrow_f = kb.consts["ones_f"][:, 0:128]
    be = kb.dma("sync", f"d_{tag}bc", lambda e: e.dma_start(
        out=b1c[:, :], in_=g["b11"].rearrange("p a b -> p (a b)")))
    be = kb.dma("sync", f"d_{tag}bc", lambda e: e.dma_start(
        out=b3c[:, :], in_=g["b31"].rearrange("p a b -> p (a b)")))
    kb.wait_edge("scalar", be)
    kb.wait_edge("vector", be)
    slab_last = [None, None]
    h_store = [None] * 2
    silu_e = [None, None]
    row_last = None
    for e_ in range(NEXP):
        sl = e_ % 2
        kb.wait_edge("gpsimd", slab_last[sl])
        for wsl, bank in ((w1s, "w11T"), (w3s, "w31T")):
            we = kb.dma("gpsimd", f"d_{tag}w{sl}", lambda e, e_=e_, sl=sl, wsl=wsl, bank=bank:
                        e.dma_start(out=wsl[sl][:, :].rearrange("p (dc o) -> p dc o", dc=NDC),
                                    in_=g[bank][e_].rearrange("(dc p) o -> p dc o", p=128)))
        kb.wait_edge("sync", row_last)
        rle = kb.dma("sync", f"d_{tag}rl", lambda e, e_=e_: e.dma_start(
            out=wr_sb[:, :], in_=g["r1w"][e_:e_ + 1, :]))
        rle = kb.dma("sync", f"d_{tag}rl", lambda e, e_=e_: e.dma_start(
            out=nr_sb[:, :], in_=g["r1neg"][e_:e_ + 1, :]))
        kb.wait_edge("tensor", we)
        kb.wait_edge("tensor", rle)
        for tg in range(NTG):
            kb.wait_edge("tensor", kb.bank_free[4])
            kb.wait_edge("tensor", kb.bank_free[5])
            kb.mm(lambda e, tg=tg: e.matmul(kb.psum[4][:, :], onesrow_f,
                                            wr_sb[:, tg * TG:(tg + 1) * TG], start=True, stop=True))
            bce = kb.mm(lambda e, tg=tg: e.matmul(kb.psum[5][:, :], kb.consts["ones_bf"][:, 0:128],
                                                  nr_sb[:, tg * TG:(tg + 1) * TG],
                                                  start=True, stop=True), inc=True)
            kb.wait_edge("scalar", bce)
            kb.wait("scalar", "sd_vector", kb.cnt["sd_vector"])
            ce1 = kb.op("scalar", lambda e: e.copy(wb_sb[:, :], kb.psum[4][:, :]))
            ce2 = kb.op("scalar", lambda e: e.copy(nb_sb[:, :], kb.psum[5][:, :]))
            kb.bank_free[4] = ce1
            kb.bank_free[5] = ce2
            kb.wait_edge("vector", ce2)
            for hc in range(NDC):
                bk = hc % 2
                kb.wait_edge("tensor", kb.bank_free[bk])
                for dc in range(NDC):
                    z1e = kb.mm(lambda e, sl=sl, dc=dc, hc=hc, tg=tg, bk=bk: e.matmul(
                        kb.psum[bk][:, :], w1s[sl][:, dc * D + hc * 128: dc * D + (hc + 1) * 128],
                        xin[:, dc * S + tg * TG: dc * S + (tg + 1) * TG],
                        start=(dc == 0), stop=(dc == NDC - 1)), inc=(dc == NDC - 1))
                tp = hc % 2
                kb.wait_edge("vector", z1e)
                kb.wait_edge("vector", silu_e[tp])
                mve = kb.op("vector", lambda e, bk=bk, tp=tp: e.tensor_mul(
                    tmp[tp][:, :], kb.psum[bk][:, :], wb_sb[:, :]))
                kb.bank_free[bk] = mve
                ve = kb.op("vector", lambda e, tp=tp: e.tensor_add(
                    tmp2[tp][:, :], tmp[tp][:, :], nb_sb[:, :]))
                hb = (e_ * NTG + tg) % 2
                kb.wait_edge("scalar", ve)
                if hc == 0:
                    kb.wait_edge("scalar", h_store[hb])
                se = kb.op("scalar", lambda e, e_=e_, hc=hc, hb=hb, tp=tp: e.activation(
                    h_sb[hb][:, hc * TG:(hc + 1) * TG], tmp2[tp][:, :], AF.Silu,
                    bias=b1c[:, e_ * NEXP + hc: e_ * NEXP + hc + 1]))
                silu_e[tp] = se
            kb.wait_edge("sync", se)
            h_store[hb] = kb.dma("sync", f"d_{tag}hs{hb}", lambda e, e_=e_, tg=tg, hb=hb:
                                 e.dma_start(out=g["h1"][e_ * NDC:(e_ + 1) * NDC, :,
                                                         tg * TG:(tg + 1) * TG].rearrange(
                                     "hc p t -> p hc t"),
                                     in_=h_sb[hb][:, :].rearrange("p (hc t) -> p hc t", hc=NDC)))
            for dc in range(NDC):
                bk = 2 + dc % 2
                kb.wait_edge("tensor", kb.bank_free[bk])
                for dci in range(NDC):
                    z3e = kb.mm(lambda e, sl=sl, dci=dci, dc=dc, tg=tg, bk=bk: e.matmul(
                        kb.psum[bk][:, :], w3s[sl][:, dci * D + dc * 128: dci * D + (dc + 1) * 128],
                        xin[:, dci * S + tg * TG: dci * S + (tg + 1) * TG],
                        start=(dci == 0), stop=(dci == NDC - 1)), inc=(dci == NDC - 1))
                tp = dc % 2
                kb.wait_edge("vector", z3e)
                if dc < 2:
                    kb.wait("vector", "sd_scalar", kb.cnt["sd_scalar"])
                mve = kb.op("vector", lambda e, bk=bk, tp=tp: e.tensor_mul(
                    tmp[tp][:, :], kb.psum[bk][:, :], wb_sb[:, :]))
                kb.bank_free[bk] = mve
                hb = (e_ * NTG + tg) % 2
                if dc == 0:
                    kb.wait_edge("vector", h_store[hb])
                ve = kb.op("vector", lambda e, e_=e_, dc=dc, hb=hb, tp=tp: e.tensor_scalar(
                    h_sb[hb][:, dc * TG:(dc + 1) * TG], tmp[tp][:, :],
                    b3c[:, e_ * NEXP + dc: e_ * NEXP + dc + 1], None, ALU.add))
            kb.wait_edge("sync", ve)
            h_store[hb] = kb.dma("sync", f"d_{tag}hs{hb}", lambda e, e_=e_, tg=tg, hb=hb:
                                 e.dma_start(out=g["x31"][e_ * NDC:(e_ + 1) * NDC, :,
                                                          tg * TG:(tg + 1) * TG].rearrange(
                                     "dc p t -> p dc t"),
                                     in_=h_sb[hb][:, :].rearrange("p (dc t) -> p dc t", dc=NDC)))
            slab_last[sl] = z3e
        row_last = z3e


def emit_ffn1b(kb, g, pctx):
    nc = kb.nc
    E = pctx.enter_context
    tag = "f1b"
    w2s = E(nc.sbuf_tensor(f"{tag}_w2", [128, NEXP * NDC * D], BF16))
    h_sb = [E(nc.sbuf_tensor(f"{tag}_h{i}", [128, NDC * TG], BF16)) for i in range(2)]
    x3_sb = [E(nc.sbuf_tensor(f"{tag}_x30", [128, NDC * TG], BF16))]
    b2r = E(nc.sbuf_tensor(f"{tag}_b2f", [1, D], F32))
    b2b = E(nc.sbuf_tensor(f"{tag}_b2b", [1, NEXP * D], BF16))
    ar_sb = E(nc.sbuf_tensor(f"{tag}_ar", [1, S], BF16))
    pt = E(nc.sbuf_tensor(f"{tag}_p", [128, TG], F32))
    y_sb = E(nc.sbuf_tensor(f"{tag}_y", [128, NDC * TG], F32))
    yo = [E(nc.sbuf_tensor(f"{tag}_yo{i}", [128, TG], F32)) for i in range(2)]
    we = None
    for e_ in range(NEXP):
        we = kb.dma("gpsimd", f"d_{tag}w", lambda e, e_=e_: e.dma_start(
            out=w2s[:, e_ * NDC * D:(e_ + 1) * NDC * D].rearrange("p (hc o) -> p hc o", hc=NDC),
            in_=g["w21T"][e_].rearrange("(hc p) o -> p hc o", p=128)))
    for e_ in range(NEXP):
        kb.wait("sync", "sd_vector", kb.cnt["sd_vector"])
        be = kb.dma("sync", f"d_{tag}b", lambda e, e_=e_: e.dma_start(
            out=b2r[:, :], in_=g["b21"][e_:e_ + 1, :]))
        kb.wait_edge("vector", be)
        bce = kb.op("vector", lambda e, e_=e_: e.tensor_copy(
            b2b[:, e_ * D:(e_ + 1) * D], b2r[:, :]))
    kb.wait_edge("tensor", bce)
    kb.wait_edge("tensor", we)
    onesrow_b = kb.consts["ones_bf"][:, 0:128]
    y_store = [None, None]
    ar_last = None
    slot_mm = [None, None]
    x3_edge = [None, None]
    for tg in range(NTG):
        for e_ in range(NEXP):
            sl = e_ % 2
            kb.wait_edge("sync", slot_mm[sl])
            lde = kb.dma("sync", f"d_{tag}h{sl}", lambda e, e_=e_, tg=tg, sl=sl:
                         e.dma_start(out=h_sb[sl][:, :].rearrange("p (hc t) -> p hc t", hc=NDC),
                                     in_=g["h1"][e_ * NDC:(e_ + 1) * NDC, :,
                                                 tg * TG:(tg + 1) * TG].rearrange("hc p t -> p hc t")))
            kb.wait_edge("sync", ar_last)
            are = kb.dma("sync", f"d_{tag}a", lambda e, e_=e_: e.dma_start(
                out=ar_sb[:, :], in_=g["r1act"][e_:e_ + 1, :]))
            kb.wait_edge("tensor", lde)
            kb.wait_edge("tensor", are)
            sx = 0
            kb.wait_edge("sync", x3_edge[sx])
            x3e = kb.dma("sync", f"d_{tag}x{sx}", lambda e, e_=e_, tg=tg, sx=sx:
                         e.dma_start(out=x3_sb[sx][:, :].rearrange("p (dc t) -> p dc t", dc=NDC),
                                     in_=g["x31"][e_ * NDC:(e_ + 1) * NDC, :,
                                                  tg * TG:(tg + 1) * TG].rearrange("dc p t -> p dc t")))
            for dc in range(NDC):
                bk = dc % 4
                kb.wait_edge("tensor", kb.bank_free[bk])
                for hc in range(NDC):
                    kb.mm(lambda e, e_=e_, dc=dc, hc=hc, sl=sl, bk=bk: e.matmul(
                        kb.psum[bk][:, :],
                        w2s[:, e_ * NDC * D + hc * D + dc * 128: e_ * NDC * D + hc * D + (dc + 1) * 128],
                        h_sb[sl][:, hc * TG:(hc + 1) * TG],
                        start=(hc == 0), stop=False))
                mme = kb.mm(lambda e, e_=e_, dc=dc, tg=tg, bk=bk: e.matmul(
                    kb.psum[bk][:, :], b2b[0:1, e_ * D + dc * 128: e_ * D + (dc + 1) * 128],
                    ar_sb[:, tg * TG:(tg + 1) * TG], start=False, stop=True), inc=True)
                ar_last = mme
                slot_mm[sl] = mme
                kb.wait_edge("vector", mme)
                kb.wait_edge("vector", x3e)
                ve = kb.op("vector", lambda e, bk=bk, sx=sx, dc=dc: e.tensor_mul(
                    pt[:, :], kb.psum[bk][:, :], x3_sb[sx][:, dc * TG:(dc + 1) * TG]))
                kb.bank_free[bk] = ve
                x3_edge[sx] = ve
                if e_ == 0:
                    ve = kb.op("vector", lambda e, dc=dc: e.tensor_copy(
                        y_sb[:, dc * TG:(dc + 1) * TG], pt[:, :]))
                else:
                    ve = kb.op("vector", lambda e, dc=dc: e.tensor_add(
                        y_sb[:, dc * TG:(dc + 1) * TG], y_sb[:, dc * TG:(dc + 1) * TG], pt[:, :]))
        for dc in range(NDC):
            yb = dc % 2
            kb.wait("vector", f"d_{tag}y{yb}", kb.cnt.get(f"d_{tag}y{yb}", 0))
            ve2 = kb.op("vector", lambda e, dc=dc, yb=yb: e.tensor_copy(
                yo[yb][:, :], y_sb[:, dc * TG:(dc + 1) * TG]))
            kb.wait_edge("sync", ve2)
            kb.dma("sync", f"d_{tag}y{yb}", lambda e, dc=dc, tg=tg, yb=yb: e.dma_start(
                out=g["y1T"][dc, :, tg * TG:(tg + 1) * TG], in_=yo[yb][:, :]))


def emit_shfin1(kb, g, pctx):
    nc = kb.nc
    E = pctx.enter_context
    tag = "sf1"
    xin, lde = load_xin(kb, pctx, tag, g["xh1"])
    kb.wait_edge("tensor", lde)
    NIC = 16
    sw1 = E(nc.sbuf_tensor(f"{tag}_sw1", [128, NDC * 2 * D], BF16))
    sw2 = E(nc.sbuf_tensor(f"{tag}_sw2", [128, NDC * 2 * D], BF16))
    sw3 = E(nc.sbuf_tensor(f"{tag}_sw3", [128, NIC * D], BF16))
    zh = E(nc.sbuf_tensor(f"{tag}_zh", [128, NIC * TG], BF16))
    tmp = E(nc.sbuf_tensor(f"{tag}_tmp", [128, TG], F32))
    yt = E(nc.sbuf_tensor(f"{tag}_y", [128, TG], F32))
    s1t = E(nc.sbuf_tensor(f"{tag}_s1", [128, TG], BF16))
    x1t = E(nc.sbuf_tensor(f"{tag}_x1", [128, TG], F32))
    gt = E(nc.sbuf_tensor(f"{tag}_g", [128, TG], F32))
    ot = [E(nc.sbuf_tensor(f"{tag}_o{i}", [128, 4 * TG], F32)) for i in range(2)]
    for nm, t in (("sw1T", sw1), ("sw2T", sw2)):
        we = kb.dma("gpsimd", f"d_{tag}w", lambda e, nm=nm, t=t: e.dma_start(
            out=t[:, :].rearrange("p (dc o) -> p dc o", dc=NDC),
            in_=g[nm].rearrange("(dc p) o -> p dc o", p=128)))
    we = kb.dma("gpsimd", f"d_{tag}w", lambda e: e.dma_start(
        out=sw3[:, :].rearrange("p (ic o) -> p ic o", ic=NIC),
        in_=g["sw3T"].rearrange("(ic p) o -> p ic o", p=128)))
    kb.wait_edge("tensor", we)
    o_store = [None, None]
    for tg in range(NTG):
        for ic in range(NIC):
            bka, bku = ic % 2, 2 + ic % 2
            kb.wait_edge("tensor", kb.bank_free[bka])
            for dc in range(NDC):
                kb.mm(lambda e, dc=dc, ic=ic, tg=tg, bka=bka: e.matmul(
                    kb.psum[bka][:, :], sw1[:, dc * 2 * D + ic * 128: dc * 2 * D + (ic + 1) * 128],
                    xin[:, dc * S + tg * TG: dc * S + (tg + 1) * TG],
                    start=(dc == 0), stop=(dc == NDC - 1)), inc=(dc == NDC - 1))
            kb.wait_edge("tensor", kb.bank_free[bku])
            for dc in range(NDC):
                ue = kb.mm(lambda e, dc=dc, ic=ic, tg=tg, bku=bku: e.matmul(
                    kb.psum[bku][:, :], sw2[:, dc * 2 * D + ic * 128: dc * 2 * D + (ic + 1) * 128],
                    xin[:, dc * S + tg * TG: dc * S + (tg + 1) * TG],
                    start=(dc == 0), stop=(dc == NDC - 1)), inc=(dc == NDC - 1))
            kb.wait("scalar", "sd_pe", kb.cnt["sd_pe"])
            kb.wait("scalar", "sd_vector", kb.cnt["sd_vector"])
            se = kb.op("scalar", lambda e, bka=bka: e.activation(tmp[:, :], kb.psum[bka][:, :], AF.Silu))
            kb.bank_free[bka] = se
            kb.wait_edge("vector", se)
            kb.wait_edge("vector", ue)
            ve = kb.op("vector", lambda e, ic=ic, bku=bku: e.tensor_mul(
                zh[:, ic * TG:(ic + 1) * TG], tmp[:, :], kb.psum[bku][:, :]))
            kb.bank_free[bku] = ve
        kb.wait_edge("tensor", ve)
        for dc in range(NDC):
            bk = 4 + dc % 2
            kb.wait_edge("tensor", kb.bank_free[bk])
            for ic in range(NIC):
                zme = kb.mm(lambda e, ic=ic, dc=dc, bk=bk: e.matmul(
                    kb.psum[bk][:, :], sw3[:, ic * D + dc * 128: ic * D + (dc + 1) * 128],
                    zh[:, ic * TG:(ic + 1) * TG],
                    start=(ic == 0), stop=(ic == NIC - 1)), inc=(ic == NIC - 1))
            ob = dc % 2
            kb.wait("sync", "sd_vector", kb.cnt["sd_vector"])
            l1 = kb.dma("sync", f"d_{tag}e", lambda e, dc=dc, tg=tg: e.dma_start(
                out=yt[:, :], in_=g["y1T"][dc, :, tg * TG:(tg + 1) * TG]))
            l1 = kb.dma("sync", f"d_{tag}e", lambda e, dc=dc, tg=tg: e.dma_start(
                out=s1t[:, :], in_=g["s1T"][dc, :, tg * TG:(tg + 1) * TG]))
            l1 = kb.dma("sync", f"d_{tag}e", lambda e, dc=dc, tg=tg: e.dma_start(
                out=x1t[:, :], in_=g["x1"][dc, :, tg * TG:(tg + 1) * TG]))
            kb.wait_edge("vector", zme)
            kb.wait_edge("vector", l1)
            kb.wait("vector", "sd_scalar", kb.cnt["sd_scalar"])
            ve2 = kb.op("vector", lambda e, bk=bk: e.tensor_add(tmp[:, :], kb.psum[bk][:, :], yt[:, :]))
            kb.bank_free[bk] = ve2
            kb.wait_edge("scalar", ve2)
            se2 = kb.op("scalar", lambda e: e.activation(gt[:, :], tmp[:, :], AF.Silu))
            ob = (dc // 4) % 2
            kb.wait_edge("vector", se2)
            if dc % 4 == 0:
                kb.wait_edge("vector", o_store[ob])
            kb.op("vector", lambda e: e.tensor_mul(gt[:, :], gt[:, :], s1t[:, :]))
            ve3 = kb.op("vector", lambda e, ob=ob, dc=dc: e.tensor_add(
                ot[ob][:, (dc % 4) * TG:(dc % 4 + 1) * TG], x1t[:, :], gt[:, :]))
            if dc % 4 == 3:
                kb.wait_edge("sync", ve3)
                o_store[ob] = kb.dma("sync", f"d_{tag}o{ob}", lambda e, dc=dc, tg=tg, ob=ob: e.dma_start(
                    out=g["out_fm"][dc - 3:dc + 1, :, tg * TG:(tg + 1) * TG].rearrange("c p t -> p c t"),
                    in_=ot[ob][:, :].rearrange("p (c t) -> p c t", c=4)))


def emit_outstub(kb, g, pctx):
    nc = kb.nc
    E = pctx.enter_context
    t = E(nc.sbuf_tensor("stub_t", [128, S], F32))
    for c in range(NDC):
        e1 = kb.dma("sync", "d_stub", lambda e, c=c: e.dma_start(out=t[:, :], in_=g["x_fm"][c]))
        kb.wait_edge("sync", e1)
        e2 = kb.dma("sync", "d_stub", lambda e, c=c: e.dma_start(out=g["out_fm"][c], in_=t[:, :]))
        kb.wait_edge("sync", e2)



def kernel(**inputs):
    maps = host_prep(inputs)
    nc = build()
    res = run_bass_kernel_spmd(nc, maps, core_ids=list(range(8)))
    out = np.empty((8, S, D), dtype=np.float32)
    for b in range(8):
        out[b] = res.results[b]["out_fm"].reshape(D, S).T
    return out


# version 07
# speedup vs baseline: 1.0756x; 1.0088x over previous
import contextlib
import numpy as np

import concourse.bass as bass
import concourse.mybir as mybir
from concourse.bass_utils import run_bass_kernel_spmd

F32 = mybir.dt.float32
BF16 = mybir.dt.bfloat16
I32 = mybir.dt.int32
U32 = mybir.dt.uint32
AF = mybir.ActivationFunctionType
ALU = mybir.AluOpType
AX = mybir.AxisListType

S = 4096
D = 1024
NDC = 8
TG = 512
NTG = S // TG
NT128 = S // 128
H0 = 16
WIN = 256
CBS = 32
NB = S // CBS
SBS = 64
TOPK = 16
NSB = S // SBS
NSEL = NSB * TOPK // 64 * 64
NEXP = 8
NEG = -100.0
SCALE0 = 0.125

ENGS = ("sync", "tensor", "vector", "scalar", "gpsimd")


class _Cnt(dict):

    def __init__(self, kb):
        super().__init__()
        self.kb = kb

    def _r(self, name):
        return self.kb.phase_map.get(name, name)

    def __getitem__(self, name):
        return super().get(self._r(name), 0)

    def get(self, name, default=0):
        return super().get(self._r(name), default)

    def bump(self, name, by):
        r = self._r(name)
        super().__setitem__(r, super().get(r, 0) + by)
        return super().__getitem__(r)


class KB:

    def __init__(self, nc, ctx):
        self.nc = nc
        self.ctx = ctx
        self.q = {e: [] for e in ENGS}
        self.sems = {}
        self.cnt = _Cnt(self)
        self.phase_map = {}
        self.phase_used = {"hw": 0, "sw": 0}

    def new_phase(self):
        self.phase_map.clear()
        self.phase_used = {"hw": 0, "sw": 0}

    def sem(self, name):
        name = self.phase_map.get(name, name)
        if name not in self.sems:
            self.sems[name] = self.ctx.enter_context(self.nc.semaphore(name))
            if name not in self.cnt:
                dict.__setitem__(self.cnt, name, 0)
        return self.sems[name]

    def _pooled(self, eng, name):
        if name in self.phase_map:
            return self.phase_map[name]
        qt = "sw" if eng == "gpsimd" else "hw"
        i = self.phase_used[qt]
        self.phase_used[qt] += 1
        pname = f"dp_{qt}{i}"
        self.phase_map[name] = pname
        return pname

    def raw(self, eng, fn):
        self.q[eng].append(fn)

    def wait(self, eng, name, val):
        if val <= 0:
            return
        s = self.sem(name)
        self.raw(eng, lambda e, s=s, v=val: e.wait_ge(s, v))

    def wait_edge(self, eng, edge):
        if edge is not None:
            self.wait(eng, edge[0], edge[1])

    def op(self, eng, fn, serialize=True):
        name = "sd_" + eng
        s = self.sem(name)
        n = self.cnt[name]
        if serialize and n > 0:
            self.raw(eng, lambda e, s=s, n=n: e.wait_ge(s, n))
        self.raw(eng, lambda e, s=s, fn=fn: fn(e).then_inc(s, 1))
        dict.__setitem__(self.cnt, name, n + 1)
        return (name, n + 1)

    def mm(self, fn, inc=False):
        if inc:
            s = self.sem("sd_pe")
            self.raw("tensor", lambda e, s=s, fn=fn: fn(e).then_inc(s, 1))
            v = self.cnt.bump("sd_pe", 1)
            return ("sd_pe", v)
        self.raw("tensor", lambda e, fn=fn: fn(e))
        return None

    def dma(self, eng, name, fn):
        assert name.startswith("d_"), name
        pname = self._pooled(eng, name)
        s = self.sem(pname)
        self.raw(eng, lambda e, s=s, fn=fn: fn(e).then_inc(s, 16))
        v = self.cnt.bump(pname, 16)
        return (pname, v)

    def barrier(self):
        snap = [(n, v) for n, v in dict.items(self.cnt) if v > 0]
        for eng in ENGS:
            for n, v in snap:
                self.wait(eng, n, v)

    def finalize(self):
        with self.nc.Block() as block:
            for eng in ENGS:
                def runner(e, eng=eng):
                    for f in self.q[eng]:
                        f(e)
                getattr(block, eng)(runner)



def _chunked_T(w_t):
    din, m = w_t.shape
    return np.ascontiguousarray(w_t.reshape(din // 128, 128, m).transpose(1, 0, 2))


def host_prep(inp):
    d = {}
    perm = np.concatenate([
        np.array([h * 192 + c * 64 + dd for h in range(16) for dd in range(64)])
        for c in range(3)])
    wqkv_t = np.ascontiguousarray(inp["L0_attn_w_qkv"].T[:, perm])
    d["wqkvT"] = wqkv_t
    d["bqkv"] = np.ascontiguousarray(inp["L0_attn_b_qkv"][perm])[None, :]
    d["wcompT"] = np.ascontiguousarray(
        inp["L0_attn_w_comp"].reshape(D, CBS, D).transpose(1, 2, 0))
    d["bcomp"] = inp["L0_attn_b_comp"][None, :]
    d["wscoreT"] = np.ascontiguousarray(
        inp["L0_attn_w_score"].T.reshape(NDC, 128, 1).transpose(1, 0, 2))
    pk = perm[D:2 * D]
    pv = perm[2 * D:3 * D]
    d["wkrows"] = np.ascontiguousarray(inp["L0_attn_w_qkv"][pk])
    d["wvrows"] = np.ascontiguousarray(inp["L0_attn_w_qkv"][pv])
    d["wsdo"] = np.ascontiguousarray(inp["L0_attn_w_score"][0].reshape(NDC, 128).T)
    d["bscore"] = inp["L0_attn_b_score"][None, :]
    d["wgateT"] = np.ascontiguousarray(
        inp["L0_attn_w_gate"].T.reshape(NDC, 128, 3).transpose(1, 0, 2))
    d["bgate"] = inp["L0_attn_b_gate"][None, :]
    d["woT"] = np.ascontiguousarray(inp["L0_attn_w_o"].T)
    d["bo"] = inp["L0_attn_b_o"][None, :]
    d["wg0T"] = np.ascontiguousarray(
        inp["L0_ffn_Wg"].T.reshape(NDC, 128, NEXP).transpose(1, 0, 2))
    d["gbank"] = inp["L0_ffn_gate_bank"]
    d["ubank"] = inp["L0_ffn_up_bank"]
    d["dbank"] = inp["L0_ffn_down_bank"]
    d["wqk1T"] = np.ascontiguousarray(
        np.concatenate([inp["L1_attn_Wq"].T, inp["L1_attn_Wk"].T], axis=1))
    d["wv1T"] = np.ascontiguousarray(inp["L1_attn_Wv"].T)
    d["wo1T"] = np.ascontiguousarray(inp["L1_attn_Wo"].T)
    d["wg1T"] = np.ascontiguousarray(
        inp["L1_ffn_gate_w"].T.reshape(NDC, 128, NEXP).transpose(1, 0, 2))
    d["w11T"] = np.ascontiguousarray(inp["L1_ffn_w1"].transpose(0, 2, 1))
    d["w21T"] = np.ascontiguousarray(inp["L1_ffn_w2"].transpose(0, 2, 1))
    d["w31T"] = np.ascontiguousarray(inp["L1_ffn_w3"].transpose(0, 2, 1))
    d["b11"] = np.ascontiguousarray(
        inp["L1_ffn_b1"].reshape(NEXP, 8, 128).transpose(2, 0, 1))
    d["b31"] = np.ascontiguousarray(
        inp["L1_ffn_b3"].reshape(NEXP, 8, 128).transpose(2, 0, 1))
    d["b21"] = inp["L1_ffn_b2"]
    d["sw1T"] = np.ascontiguousarray(inp["L1_ffn_sw1"].T)
    d["sw2T"] = np.ascontiguousarray(inp["L1_ffn_sw2"].T)
    d["sw3T"] = np.ascontiguousarray(inp["L1_ffn_sw3"].T)

    ii = np.arange(128)
    d["c_tri_ge"] = (ii[:, None] >= ii[None, :]).astype(np.float32)
    d["c_tri_le"] = (ii[:, None] <= ii[None, :]).astype(np.float32)
    d["c_ident"] = np.eye(128, dtype=np.float32)
    hd = 64
    inv = 1.0 / (10000.0 ** (np.arange(0, hd, 2, dtype=np.float32) / hd))
    pos = np.arange(S, dtype=np.float32)
    ang = pos[None, :] * np.concatenate([inv, inv])[:, None]
    cos = np.cos(ang).astype(np.float32)
    sin = np.sin(ang).astype(np.float32)
    d["c_cos"] = np.ascontiguousarray(np.concatenate([cos, cos], axis=0))
    d["c_sin"] = np.ascontiguousarray(np.concatenate([sin, sin], axis=0))
    P = np.zeros((128, 128), dtype=np.float32)
    for base in (0, 64):
        for r in range(32):
            P[base + r, base + r + 32] = -1.0
            P[base + 32 + r, base + r] = 1.0
    d["c_PT"] = np.ascontiguousarray(P.T)

    x = np.asarray(inp["L0_attn_x"])
    maps = []
    for b in range(8):
        m = dict(d)
        m["x_fm"] = np.ascontiguousarray(x[b].T).reshape(NDC, 128, S)
        maps.append(m)
    return maps



def build(dbg=(), upto=None, only=None):
    nc = bass.Bass("TRN2", target_bir_lowering=False, debug=False)
    dbg = set(dbg)

    def inp(name, shape, dtype=F32):
        return nc.dram_tensor(name, list(shape), dtype, kind="ExternalInput").ap()

    def scr(name, shape, dtype):
        kind = "ExternalOutput" if name in dbg else "Internal"
        return nc.dram_tensor(name, list(shape), dtype, kind=kind).ap()

    g = {}
    g["x_fm"] = inp("x_fm", (NDC, 128, S))
    g["wqkvT"] = inp("wqkvT", (D, 3 * D))
    g["bqkv"] = inp("bqkv", (1, 3 * D))
    g["wcompT"] = inp("wcompT", (CBS, D, D))
    g["bcomp"] = inp("bcomp", (1, D))
    g["wscoreT"] = inp("wscoreT", (128, NDC, 1))
    g["wkrows"] = inp("wkrows", (D, D))
    g["wvrows"] = inp("wvrows", (D, D))
    g["wsdo"] = inp("wsdo", (128, NDC))
    g["bscore"] = inp("bscore", (1, 1))
    g["wgateT"] = inp("wgateT", (128, NDC, 3))
    g["bgate"] = inp("bgate", (1, 3))
    g["woT"] = inp("woT", (D, D))
    g["bo"] = inp("bo", (1, D))
    g["wg0T"] = inp("wg0T", (128, NDC, NEXP))
    g["gbank"] = inp("gbank", (NEXP, D, D))
    g["ubank"] = inp("ubank", (NEXP, D, D))
    g["dbank"] = inp("dbank", (NEXP, D, D))
    g["wqk1T"] = inp("wqk1T", (D, 2 * D))
    g["wv1T"] = inp("wv1T", (D, D))
    g["wo1T"] = inp("wo1T", (D, D))
    g["wg1T"] = inp("wg1T", (128, NDC, NEXP))
    g["w11T"] = inp("w11T", (NEXP, D, D))
    g["w21T"] = inp("w21T", (NEXP, D, D))
    g["w31T"] = inp("w31T", (NEXP, D, D))
    g["b11"] = inp("b11", (128, NEXP, NEXP))
    g["b31"] = inp("b31", (128, NEXP, NEXP))
    g["b21"] = inp("b21", (NEXP, D))
    g["sw1T"] = inp("sw1T", (D, 2 * D))
    g["sw2T"] = inp("sw2T", (D, 2 * D))
    g["sw3T"] = inp("sw3T", (2 * D, D))
    for cn, shp in (("c_tri_ge", (128, 128)), ("c_tri_le", (128, 128)),
                    ("c_ident", (128, 128)), ("c_cos", (128, S)), ("c_sin", (128, S)),
                    ("c_PT", (128, 128))):
        g[cn] = inp(cn, shp)

    for ln in (0, 1):
        g[f"xh{ln}"] = scr(f"xh{ln}", (NDC, 128, S), BF16)
        g[f"xl{ln}"] = scr(f"xl{ln}", (NDC, 128, S), BF16)
    g["q0T"] = scr("q0T", (NDC, 128, S), BF16)
    g["k0T"] = scr("k0T", (NDC, 128, S), BF16)
    g["v0T"] = scr("v0T", (NDC, 128, S), BF16)
    g["k0tm"] = scr("k0tm", (S, D), BF16)
    g["v0tm"] = scr("v0tm", (S, D), BF16)
    g["scrow"] = scr("scrow", (2, S), F32)
    g["scb"] = scr("scb", (2, NSB, SBS), F32)
    g["selidx"] = scr("selidx", (2, NSEL), I32)
    g["sk0tm"] = scr("sk0tm", (NSEL, D), BF16)
    g["sv0tm"] = scr("sv0tm", (NSEL, D), BF16)
    g["sk0T"] = scr("sk0T", (NDC, 128, NSEL), BF16)
    g["ck0T"] = scr("ck0T", (NDC, 128, NB), BF16)
    g["cv0T"] = scr("cv0T", (NDC, 128, NB), BF16)
    g["ck0tm"] = scr("ck0tm", (NB, D), BF16)
    g["cv0tm"] = scr("cv0tm", (NB, D), BF16)
    g["negm"] = scr("negm", (4, S), BF16)
    g["mrow"] = scr("mrow", (1, S), F32)
    g["qnorm"] = scr("qnorm", (1, S), F32)
    g["oc0T"] = scr("oc0T", (NDC, 128, 384), BF16)
    g["os0T"] = scr("os0T", (NDC, 128, 1280), BF16)
    g["ow0T"] = scr("ow0T", (NDC, 128, S), BF16)
    g["s0T"] = scr("s0T", (NDC, 128, S), BF16)
    g["r0lg"] = scr("r0lg", (NEXP, S), F32)
    g["r0neg"] = scr("r0neg", (NEXP, S), BF16)
    g["h0"] = scr("h0", (NEXP * NDC, 128, S), BF16)
    g["x1"] = scr("x1", (NDC, 128, S), F32)
    g["q1T"] = scr("q1T", (NDC, 128, S), BF16)
    g["k1T"] = scr("k1T", (NDC, 128, S), BF16)
    g["v1tm"] = scr("v1tm", (S, D), BF16)
    g["o1"] = scr("o1", (NDC, 128, S), BF16)
    g["o1v"] = g["o1"]
    g["s1T"] = scr("s1T", (NDC, 128, S), BF16)
    g["r1lg"] = scr("r1lg", (NEXP, S), F32)
    g["r1w"] = scr("r1w", (NEXP, S), F32)
    g["r1neg"] = scr("r1neg", (NEXP, S), BF16)
    g["r1act"] = scr("r1act", (NEXP, S), BF16)
    g["h1"] = scr("h1", (NEXP * NDC, 128, S), BF16)
    g["x31"] = scr("x31", (NEXP * NDC, 128, S), BF16)
    g["y1T"] = scr("y1T", (NDC, 128, S), F32)
    g["out_fm"] = nc.dram_tensor("out_fm", [NDC, 128, S], F32, kind="ExternalOutput").ap()

    with contextlib.ExitStack() as ctx:
        kb = KB(nc, ctx)
        phases = _phases(kb, g, ctx)
        for name, fn in phases:
            if only is not None and name not in ("INIT",) + tuple(only):
                continue
            kb.new_phase()
            with contextlib.ExitStack() as pctx:
                fn(pctx)
            kb.barrier()
            if upto is not None and name == upto:
                break
        kb.finalize()
    return nc


def _phases(kb, g, ctx):
    nc = kb.nc

    ones_bf = ctx.enter_context(nc.sbuf_tensor("ones_bf", [1, TG], BF16))
    ones_f = ctx.enter_context(nc.sbuf_tensor("ones_f", [1, TG], F32))
    onescol_bf = ctx.enter_context(nc.sbuf_tensor("onescol_bf", [128, 1], BF16))
    twoscol_bf = ctx.enter_context(nc.sbuf_tensor("twoscol_bf", [128, 1], BF16))
    onescol_f = ctx.enter_context(nc.sbuf_tensor("onescol_f", [128, 1], F32))
    tri_ge = ctx.enter_context(nc.sbuf_tensor("tri_ge", [128, 128], BF16))
    tri_le = ctx.enter_context(nc.sbuf_tensor("tri_le", [128, 128], BF16))
    ident = ctx.enter_context(nc.sbuf_tensor("ident", [128, 128], BF16))
    ptrot = ctx.enter_context(nc.sbuf_tensor("ptrot", [128, 128], BF16))
    kb.consts = dict(ones_bf=ones_bf, ones_f=ones_f, onescol_bf=onescol_bf,
                     twoscol_bf=twoscol_bf, onescol_f=onescol_f, tri_ge=tri_ge,
                     tri_le=tri_le, ident=ident, ptrot=ptrot)
    pbE = ctx.enter_context(nc.psum_tensor("pbE", [128, 4 * TG], F32))
    kb.psum = [pbE[:, i * TG:(i + 1) * TG] for i in range(4)]
    kb.psum += [ctx.enter_context(nc.psum_tensor(f"pb{i}", [128, TG], F32))[:, :]
                for i in range(4, 7)]
    kb.psum_quad = pbE
    kb.psum_bf = ctx.enter_context(nc.psum_tensor("pbbf", [128, 1024], BF16))
    kb.bank_free = [None] * 8

    CF_VALS = (1e-5, 0.5, -0.5, 1.0, -1.0)
    cf = ctx.enter_context(nc.sbuf_tensor("cf", [128, len(CF_VALS)], F32))
    for j, v in enumerate(CF_VALS):
        nc.const_aps.register(F32, v, cf[:, j:j + 1])

    def init(pctx):
        for j, v in enumerate(CF_VALS):
            kb.op("gpsimd", lambda e, j=j, v=v: e.memset(cf[:, j:j + 1], v))
        kb.op("gpsimd", lambda e: e.memset(ones_bf[:, :], 1.0))
        kb.op("gpsimd", lambda e: e.memset(ones_f[:, :], 1.0))
        kb.op("gpsimd", lambda e: e.memset(onescol_bf[:, :], 1.0))
        kb.op("gpsimd", lambda e: e.memset(twoscol_bf[:, :], 2.0))
        kb.op("gpsimd", lambda e: e.memset(onescol_f[:, :], 1.0))
        kb.dma("gpsimd", "d_init", lambda e: e.dma_start(out=tri_ge[:, :], in_=g["c_tri_ge"]))
        kb.dma("gpsimd", "d_init", lambda e: e.dma_start(out=tri_le[:, :], in_=g["c_tri_le"]))
        kb.dma("gpsimd", "d_init", lambda e: e.dma_start(out=ident[:, :], in_=g["c_ident"]))
        kb.dma("gpsimd", "d_init", lambda e: e.dma_start(out=ptrot[:, :], in_=g["c_PT"]))

    out = [("INIT", init)]
    out.append(("LN0", lambda pctx: emit_ln(kb, g, pctx, g["x_fm"], g["xh0"], g["xl0"])))
    out.append(("QKV0", lambda pctx: emit_qkv0(kb, g, pctx)))
    out.append(("SEL0", lambda pctx: emit_sel0(kb, g, pctx)))
    out.append(("COMP0", lambda pctx: emit_comp0(kb, g, pctx)))
    out.append(("OCOMP0", lambda pctx: emit_banded(kb, g, pctx, "oc0", 384, 1,
                g["q0T"], g["ck0T"], g["cv0tm"], 1, g["oc0T"])))
    out.append(("OSEL0", lambda pctx: emit_banded(kb, g, pctx, "os0", 1280, NSEL // 128,
                g["q0T"], g["sk0T"], g["sv0tm"], 0, g["os0T"])))
    out.append(("OWIN0", lambda pctx: emit_banded(kb, g, pctx, "ow0", S, S // 128,
                g["q0T"], g["k0T"], g["v0tm"], 0, g["ow0T"])))
    out.append(("GCOMB0", lambda pctx: emit_gcomb0(kb, g, pctx)))
    out.append(("FFN0A", lambda pctx: emit_ffn0a(kb, g, pctx)))
    out.append(("FFN0B", lambda pctx: emit_ffn0b(kb, g, pctx)))
    out.append(("LN1", lambda pctx: emit_ln(kb, g, pctx, g["x1"], g["xh1"], g["xl1"])))
    out.append(("QKV1", lambda pctx: emit_qkv1(kb, g, pctx)))
    out.append(("ATT1", lambda pctx: emit_att1(kb, g, pctx)))
    out.append(("WO1", lambda pctx: emit_linear(kb, g, pctx, "wo1", g["wo1T"], D,
                lambda oc: (g["s1T"], oc), in_hbm=g["o1v"])))
    out.append(("FFN1A", lambda pctx: emit_ffn1a(kb, g, pctx)))
    out.append(("FFN1B", lambda pctx: emit_ffn1b(kb, g, pctx)))
    out.append(("SHFIN1", lambda pctx: emit_shfin1(kb, g, pctx)))
    return out


def load_xin(kb, pctx, tag, in_hbm, cols=S):
    nc = kb.nc
    xin = pctx.enter_context(nc.sbuf_tensor(f"{tag}_xin", [128, NDC * cols], BF16))
    lde = None
    for c in range(NDC):
        lde = kb.dma("sync", f"d_{tag}x", lambda e, c=c: e.dma_start(
            out=xin[:, c * cols:(c + 1) * cols], in_=in_hbm[c, :, 0:cols]))
    return xin, lde


def emit_linear(kb, g, pctx, tag, wT_hbm, nout, out_map, in_hbm=None, xin=None,
                bias_hbm=None, banks=(0, 1, 2, 3, 4, 5), epi=None):
    nc = kb.nc
    E = pctx.enter_context
    OS = 512
    nos = nout // OS
    if xin is None:
        xin, lde = load_xin(kb, pctx, tag, in_hbm)
        kb.wait_edge("tensor", lde)
    wsl = [E(nc.sbuf_tensor(f"{tag}_w{i}", [128, NDC * OS], BF16)) for i in range(2)]
    ones = kb.consts["ones_bf"]
    brow = None
    if bias_hbm is not None:
        brow_f = E(nc.sbuf_tensor(f"{tag}_bf", [1, nout], F32))
        brow = E(nc.sbuf_tensor(f"{tag}_bb", [1, nout], BF16))
        be = kb.dma("sync", f"d_{tag}b", lambda e: e.dma_start(out=brow_f[:, :], in_=bias_hbm))
        kb.wait_edge("vector", be)
        bce = kb.op("vector", lambda e: e.tensor_copy(brow[:, :], brow_f[:, :]))
        kb.wait_edge("tensor", bce)
    obuf = [E(nc.sbuf_tensor(f"{tag}_o{i}", [128, 4 * TG], BF16)) for i in range(2)]
    obuf_edge = [None] * 2
    slab_last = [None, None]
    nb = len(banks)
    ngrp = 0
    for osl in range(nos):
        sl = osl % 2
        kb.wait_edge("gpsimd", slab_last[sl])
        we = kb.dma("gpsimd", f"d_{tag}w{sl}", lambda e, osl=osl, sl=sl: e.dma_start(
            out=wsl[sl][:, :].rearrange("p (dc o) -> p dc o", dc=NDC),
            in_=wT_hbm.rearrange("(dc p) o -> p dc o", p=128)[:, :, osl * OS:(osl + 1) * OS]))
        kb.wait_edge("tensor", we)
        for tg in range(NTG):
            ob = (osl * NTG + tg) % 2
            oms = []
            for o4 in range(OS // 128):
                oc = osl * 4 + o4
                om = out_map(oc)
                if om is None:
                    continue
                bk = banks[ngrp % nb]
                ngrp += 1
                kb.wait_edge("tensor", kb.bank_free[bk])
                nmm = NDC + (1 if brow is not None else 0)
                for dc in range(NDC):
                    last = (dc == nmm - 1)
                    mme = kb.mm(lambda e, sl=sl, dc=dc, o4=o4, tg=tg, bk=bk, last=last: e.matmul(
                        kb.psum[bk][:, :],
                        wsl[sl][:, dc * OS + o4 * 128: dc * OS + (o4 + 1) * 128],
                        xin[:, dc * S + tg * TG: dc * S + (tg + 1) * TG],
                        start=(dc == 0), stop=last), inc=last)
                if brow is not None:
                    mme = kb.mm(lambda e, oc=oc, bk=bk: e.matmul(
                        kb.psum[bk][:, :], brow[0:1, oc * 128:(oc + 1) * 128],
                        ones[:, :], start=False, stop=True), inc=True)
                slab_last[sl] = mme
                if epi is not None:
                    epi(oc, tg, bk, mme)
                else:
                    oms.append(om)
                    if o4 == 0:
                        kb.wait_edge("vector", obuf_edge[ob])
                    kb.wait_edge("vector", mme)
                    ve = kb.op("vector", lambda e, ob=ob, bk=bk, o4=o4: e.tensor_copy(
                        obuf[ob][:, o4 * TG:(o4 + 1) * TG], kb.psum[bk][:, :]))
                    kb.bank_free[bk] = ve
            if epi is None and oms:
                assert len(oms) == 4 and all(t.tensor is oms[0][0].tensor for t, _ in oms) \
                    and [ocl for _, ocl in oms] == list(range(oms[0][1], oms[0][1] + 4)), oms
                ohbm, ocl0 = oms[0]
                kb.wait_edge("sync", ve)
                obuf_edge[ob] = kb.dma("sync", f"d_{tag}s{ob}", lambda e, ob=ob, ohbm=ohbm,
                                       ocl0=ocl0, tg=tg: e.dma_start(
                    out=ohbm[ocl0:ocl0 + 4, :, tg * TG:(tg + 1) * TG].rearrange("c p t -> p c t"),
                    in_=obuf[ob][:, :].rearrange("p (c t) -> p c t", c=4)))
    return xin


def emit_linear_tm(kb, g, pctx, tag, wT_hbm, xin, out_tm_hbm, bias_row=None,
                   banks=(5, 6)):
    nc = kb.nc
    E = pctx.enter_context
    wsb = E(nc.sbuf_tensor(f"{tag}_wtm", [128, NDC * D], BF16))
    we = kb.dma("gpsimd", f"d_{tag}wt", lambda e: e.dma_start(
        out=wsb[:, :].rearrange("p (dc o) -> p dc o", dc=NDC),
        in_=wT_hbm.rearrange("(dc p) o -> p dc o", p=128)))
    kb.wait_edge("tensor", we)
    obuf = [E(nc.sbuf_tensor(f"{tag}_to{i}", [128, TG], BF16)) for i in range(4)]
    obuf_edge = [None] * 4
    onesr = kb.consts["ones_bf"][:, 0:128]
    ngrp = 0
    for t1 in range(NT128):
        for half in range(2):
            bk = banks[ngrp % len(banks)]
            kb.wait_edge("tensor", kb.bank_free[bk])
            for dc in range(NDC):
                last = (dc == NDC - 1 and bias_row is None)
                mme = kb.mm(lambda e, dc=dc, t1=t1, half=half, bk=bk, last=last: e.matmul(
                    kb.psum[bk][:, :],
                    xin[:, dc * S + t1 * 128: dc * S + (t1 + 1) * 128],
                    wsb[:, dc * D + half * TG: dc * D + (half + 1) * TG],
                    start=(dc == 0), stop=last), inc=last)
            if bias_row is not None:
                mme = kb.mm(lambda e, half=half, bk=bk: e.matmul(
                    kb.psum[bk][:, :], onesr, bias_row[0:1, half * TG:(half + 1) * TG],
                    start=False, stop=True), inc=True)
            ob = ngrp % 4
            ngrp += 1
            kb.wait_edge("vector", obuf_edge[ob])
            kb.wait_edge("vector", mme)
            ve = kb.op("vector", lambda e, ob=ob, bk=bk: e.tensor_copy(
                obuf[ob][:, :], kb.psum[bk][:, :]))
            kb.bank_free[bk] = ve
            kb.wait_edge("sync", ve)
            obuf_edge[ob] = kb.dma("sync", f"d_{tag}ts{ob}", lambda e, ob=ob, t1=t1,
                                   half=half: e.dma_start(
                out=out_tm_hbm[t1 * 128:(t1 + 1) * 128, half * TG:(half + 1) * TG],
                in_=obuf[ob][:, :]))


def emit_qkv0(kb, g, pctx):
    def omap(oc):
        if oc < 8:
            return (g["q0T"], oc)
        if oc < 16:
            return (g["k0T"], oc - 8)
        return (g["v0T"], oc - 16)
    xin = emit_linear(kb, g, pctx, "qkv0", g["wqkvT"], 3 * D, omap,
                      in_hbm=g["xh0"], bias_hbm=g["bqkv"])
    nc = kb.nc
    E = pctx.enter_context
    brow_f = E(nc.sbuf_tensor("qkv0tm_bf", [1, 2 * D], F32))
    brow = E(nc.sbuf_tensor("qkv0tm_bb", [1, 2 * D], BF16))
    be = kb.dma("sync", "d_qkv0tmb", lambda e: e.dma_start(
        out=brow_f[:, :], in_=g["bqkv"][:, D:3 * D]))
    kb.wait_edge("vector", be)
    bce = kb.op("vector", lambda e: e.tensor_copy(brow[:, :], brow_f[:, :]))
    kb.wait_edge("tensor", bce)
    kv = g["wqkvT"]
    emit_linear_tm(kb, g, pctx, "k0tm", kv[:, D:2 * D], xin, g["k0tm"],
                   bias_row=brow[:, 0:D])
    emit_linear_tm(kb, g, pctx, "v0tm", kv[:, 2 * D:3 * D], xin, g["v0tm"],
                   bias_row=brow[:, D:2 * D])


SEL0_PARTS = "ABC"


def emit_sel0(kb, g, pctx):
    nc = kb.nc
    E = pctx.enter_context
    tag = "sel0"
    wrow = E(nc.sbuf_tensor(f"{tag}_wrow", [128, NDC * D], F32))
    wsdo = E(nc.sbuf_tensor(f"{tag}_wsdo", [128, NDC], F32))
    weff = E(nc.sbuf_tensor(f"{tag}_weff", [128, 2 * NDC], F32))
    wh = E(nc.sbuf_tensor(f"{tag}_wh", [128, 2 * NDC], BF16))
    wl = E(nc.sbuf_tensor(f"{tag}_wl", [128, 2 * NDC], BF16))
    de = kb.dma("sync", f"d_{tag}ws", lambda e: e.dma_start(out=wsdo[:, :], in_=g["wsdo"]))
    kb.wait_edge("tensor", de)
    for w, src_w in enumerate(("wkrows", "wvrows")):
        kb.wait_edge("sync", kb.bank_free[5])
        lde = None
        for doc in range(NDC):
            lde = kb.dma("sync", f"d_{tag}wr{w}", lambda e, doc=doc, src_w=src_w: e.dma_start(
                out=wrow[:, doc * D:(doc + 1) * D],
                in_=g[src_w].rearrange("(doc p) di -> doc p di", p=128)[doc]))
        kb.wait_edge("tensor", lde)
        ve_prev = kb.bank_free[5]
        for di in range(NDC):
            kb.wait_edge("tensor", ve_prev)
            for doc in range(NDC):
                mme = kb.mm(lambda e, di=di, doc=doc: e.matmul(
                    kb.psum[5][:, 0:1],
                    wrow[:, doc * D + di * 128: doc * D + (di + 1) * 128],
                    wsdo[:, doc:doc + 1], start=(doc == 0), stop=(doc == NDC - 1)),
                    inc=(doc == NDC - 1))
            kb.wait_edge("vector", mme)
            ve_prev = kb.op("vector", lambda e, w=w, di=di: e.tensor_copy(
                weff[:, w * NDC + di: w * NDC + di + 1], kb.psum[5][:, 0:1]))
        kb.bank_free[5] = ve_prev
    he = kb.op("vector", lambda e: e.tensor_copy(wh[:, :], weff[:, :]))
    le = kb.op("vector", lambda e: e.tensor_sub(wl[:, :], weff[:, :], wh[:, :]))
    kb.wait_edge("tensor", le)

    xh = [E(nc.sbuf_tensor(f"{tag}_xh{i}", [128, NDC * TG], BF16)) for i in range(2)]
    xl = [E(nc.sbuf_tensor(f"{tag}_xl{i}", [128, NDC * TG], BF16)) for i in range(2)]
    kt = [E(nc.sbuf_tensor(f"{tag}_kt{i}", [128, NDC * TG], BF16)) for i in range(2)]
    qt = [E(nc.sbuf_tensor(f"{tag}_qt{i}", [128, NDC * TG], BF16)) for i in range(2)]
    sqq = [E(nc.sbuf_tensor(f"{tag}_sqq{i}", [128, TG], BF16)) for i in range(2)]
    sqk = [E(nc.sbuf_tensor(f"{tag}_sqk{i}", [128, TG], BF16)) for i in range(2)]
    scrk = E(nc.sbuf_tensor(f"{tag}_scrk", [1, S], F32))
    scrv = E(nc.sbuf_tensor(f"{tag}_scrv", [1, S], F32))
    qn = E(nc.sbuf_tensor(f"{tag}_qn", [1, S], F32))
    k2 = E(nc.sbuf_tensor(f"{tag}_k2", [1, S], F32))
    kmax = E(nc.sbuf_tensor(f"{tag}_kmax", [1, 2], F32))
    nm = E(nc.sbuf_tensor(f"{tag}_nm", [1, S], BF16))
    onesc = kb.consts["onescol_bf"]

    slot_pe = [None, None]
    sq_ve = [None, None]
    for tg in range(NTG):
        sl = tg % 2
        kb.wait_edge("sync", slot_pe[sl])
        de = None
        for t, hb in ((xh, "xh0"), (xl, "xl0"), (kt, "k0T"), (qt, "q0T")):
            de = kb.dma("sync", f"d_{tag}l{sl}", lambda e, t=t, hb=hb, tg=tg, sl=sl:
                        e.dma_start(out=t[sl][:, :].rearrange("p (dc t) -> p dc t", dc=NDC),
                                    in_=g[hb][:, :, tg * TG:(tg + 1) * TG].rearrange(
                                        "dc p t -> p dc t")))
        kb.wait_edge("vector", de)
        kb.wait_edge("tensor", de)
        for b in range(4):
            kb.wait_edge("tensor", kb.bank_free[b])
        for dc in range(NDC):
            sv = dc % 2
            kb.wait_edge("vector", sq_ve[sv])
            v1 = kb.op("vector", lambda e, sl=sl, dc=dc, sv=sv: e.tensor_mul(
                sqq[sv][:, :], qt[sl][:, dc * TG:(dc + 1) * TG], qt[sl][:, dc * TG:(dc + 1) * TG]))
            v2 = kb.op("vector", lambda e, sl=sl, dc=dc, sv=sv: e.tensor_mul(
                sqk[sv][:, :], kt[sl][:, dc * TG:(dc + 1) * TG], kt[sl][:, dc * TG:(dc + 1) * TG]))
            kb.wait_edge("tensor", v2)
            for w in range(2):
                for ti, (wt_, xt_) in enumerate(((wh, xh), (wh, xl), (wl, xh))):
                    kb.mm(lambda e, sl=sl, dc=dc, w=w, wt_=wt_, xt_=xt_, ti=ti: e.matmul(
                        kb.psum[w][0:1, :], wt_[:, w * NDC + dc:w * NDC + dc + 1],
                        xt_[sl][:, dc * TG:(dc + 1) * TG],
                        start=(dc == 0 and ti == 0), stop=(dc == NDC - 1 and ti == 2)))
            kb.mm(lambda e, sv=sv, dc=dc: e.matmul(kb.psum[2][0:1, :], onesc[:, :],
                                                   sqq[sv][:, :], start=(dc == 0), stop=(dc == NDC - 1)))
            mme = kb.mm(lambda e, sv=sv, dc=dc: e.matmul(kb.psum[3][0:1, :], onesc[:, :],
                                                         sqk[sv][:, :], start=(dc == 0),
                                                         stop=(dc == NDC - 1)), inc=True)
            sq_ve[sv] = mme
            slot_pe[sl] = mme
        kb.wait_edge("vector", mme)
        kb.wait_edge("scalar", mme)
        e0 = kb.op("vector", lambda e, tg=tg: e.tensor_copy(scrk[:, tg * TG:(tg + 1) * TG], kb.psum[0][0:1, :]))
        e1 = kb.op("vector", lambda e, tg=tg: e.tensor_copy(scrv[:, tg * TG:(tg + 1) * TG], kb.psum[1][0:1, :]))
        e2 = kb.op("scalar", lambda e, tg=tg: e.activation(qn[:, tg * TG:(tg + 1) * TG],
                                                           kb.psum[2][0:1, :], AF.Sqrt))
        e3 = kb.op("vector", lambda e, tg=tg: e.tensor_copy(k2[:, tg * TG:(tg + 1) * TG], kb.psum[3][0:1, :]))
        kb.bank_free[0], kb.bank_free[1], kb.bank_free[2], kb.bank_free[3] = e0, e1, e2, e3

    ve = kb.op("vector", lambda e: e.reduce_max(kmax[:, 0:1], k2[:, :], axis=AX.X))
    kb.wait_edge("scalar", ve)
    se = kb.op("scalar", lambda e: e.activation(kmax[:, 1:2], kmax[:, 0:1], AF.Sqrt))
    kb.wait_edge("vector", se)
    kb.wait_edge("vector", e2)
    ve = kb.op("vector", lambda e: e.tensor_scalar(nm[:, :], qn[:, :], kmax[:, 1:2], -1.0,
                                                   ALU.mult, ALU.mult))
    kb.wait_edge("sync", ve)
    kb.dma("sync", "d_selr", lambda e: e.dma_start(out=g["negm"][0:1, :], in_=nm[:, :]))
    kb.wait_edge("sync", e2)
    kb.dma("sync", "d_selr", lambda e: e.dma_start(out=g["qnorm"][0:1, :], in_=qn[:, :]))
    kb.wait_edge("sync", e1)
    kb.dma("sync", "d_selr", lambda e: e.dma_start(
        out=g["scb"][0:1, :, :].rearrange("a n j -> a (n j)"), in_=scrk[:, :]))
    kb.dma("sync", "d_selr", lambda e: e.dma_start(
        out=g["scb"][1:2, :, :].rearrange("a n j -> a (n j)"), in_=scrv[:, :]))
    dsel = kb.cnt["d_selr"]

    if "B" not in SEL0_PARTS:
        return
    sc64 = E(nc.sbuf_tensor(f"{tag}_sc64", [NSB, SBS], F32))
    sc64r = E(nc.sbuf_tensor(f"{tag}_sc64r", [NSB, SBS], F32))
    v8 = E(nc.sbuf_tensor(f"{tag}_v8", [NSB, 8], F32))
    idx16 = E(nc.sbuf_tensor(f"{tag}_idx16", [NSB, TOPK], U32))
    iot = E(nc.sbuf_tensor(f"{tag}_iot", [NSB, TOPK], I32))
    idxg = E(nc.sbuf_tensor(f"{tag}_idxg", [NSB, TOPK], I32))
    ge = kb.op("gpsimd", lambda e: e.iota(iot[:, :], pattern=[[0, TOPK]], base=0,
                                          channel_multiplier=SBS))
    for w in range(2):
        kb.wait("sync", "d_selr", dsel)
        kb.wait("sync", "sd_vector", kb.cnt["sd_vector"])
        le = kb.dma("sync", f"d_{tag}t{w}", lambda e, w=w: e.dma_start(out=sc64[:, :], in_=g["scb"][w]))
        kb.wait_edge("vector", le)
        kb.op("vector", lambda e: e.max(out=v8[:, :], in_=sc64[:, :]))
        kb.op("vector", lambda e: e.max_index(out=idx16[:, 0:8], in_max=v8[:, :], in_values=sc64[:, :]))
        kb.op("vector", lambda e: e.match_replace(out=sc64r[:, :], in_to_replace=v8[:, :],
                                                  in_values=sc64[:, :], imm_value=-1e30))
        kb.op("vector", lambda e: e.max(out=v8[:, :], in_=sc64r[:, :]))
        ie = kb.op("vector", lambda e: e.max_index(out=idx16[:, 8:16], in_max=v8[:, :], in_values=sc64r[:, :]))
        kb.wait_edge("vector", ge)
        ae = kb.op("vector", lambda e: e.tensor_tensor(idxg[:, :], idx16[:, :], iot[:, :], ALU.add))
        kb.wait_edge("sync", ae)
        kb.dma("sync", f"d_{tag}t{w}", lambda e, w=w: e.dma_start(
            out=g["selidx"][w].rearrange("(n r) -> n r", r=TOPK), in_=idxg[:, :]))
        kb.wait("vector", f"d_{tag}t{w}", kb.cnt[f"d_{tag}t{w}"])

    if "C" not in SEL0_PARTS:
        return
    idxs = E(nc.sbuf_tensor(f"{tag}_idxs", [128, NSEL // 128], I32))
    gat = E(nc.sbuf_tensor(f"{tag}_gat", [128, D], BF16))
    tsb = E(nc.sbuf_tensor(f"{tag}_tsb", [128, 128], BF16))
    ident = kb.consts["ident"]
    for w, (src_tm, dst_tm) in enumerate((("k0tm", "sk0tm"), ("v0tm", "sv0tm"))):
        kb.wait("gpsimd", f"d_{tag}t{w}", kb.cnt[f"d_{tag}t{w}"])
        ie = None
        for i in range(NSEL // 128):
            ie = kb.dma("gpsimd", f"d_{tag}gi{w}", lambda e, w=w, i=i: e.dma_start(
                out=idxs[:, i:i + 1],
                in_=g["selidx"][w].rearrange("(i p) -> i p", p=128)[i][:, None]))
        for i in range(NSEL // 128):
            kb.wait_edge("gpsimd", ie)
            ge = kb.dma("gpsimd", f"d_{tag}g{w}", lambda e, i=i, src_tm=src_tm: e.indirect_dma_start(
                out=gat[:, :], out_offset=None, in_=g[src_tm][:, :],
                in_offset=bass.IndirectOffsetOnAxis(ap=idxs[:, i:i + 1], axis=0)))
            kb.wait_edge("sync", ge)
            ste = kb.dma("sync", f"d_{tag}g{w}s", lambda e, i=i, dst_tm=dst_tm: e.dma_start(
                out=g[dst_tm][i * 128:(i + 1) * 128, :], in_=gat[:, :]))
            if w == 0:
                for dc in range(NDC):
                    kb.wait_edge("tensor", ge)
                    kb.wait_edge("tensor", kb.bank_free[7])
                    te = kb.mm(lambda e, dc=dc: e.transpose(
                        kb.psum_bf[:, 0:128], gat[:, dc * 128:(dc + 1) * 128], ident[:, :]),
                        inc=True)
                    kb.wait_edge("vector", te)
                    kb.wait("vector", f"d_{tag}g{w}s", kb.cnt[f"d_{tag}g{w}s"])
                    ve = kb.op("vector", lambda e: e.tensor_copy(tsb[:, :], kb.psum_bf[:, 0:128]))
                    kb.bank_free[7] = ve
                    kb.wait_edge("sync", ve)
                    kb.dma("sync", f"d_{tag}g{w}s", lambda e, dc=dc, i=i: e.dma_start(
                        out=g["sk0T"][dc, :, i * 128:(i + 1) * 128], in_=tsb[:, :]))
            kb.wait("gpsimd", f"d_{tag}g{w}s", kb.cnt[f"d_{tag}g{w}s"])
            if w == 0:
                kb.wait_edge("gpsimd", kb.bank_free[7])


def emit_comp0(kb, g, pctx):
    nc = kb.nc
    E = pctx.enter_context
    tag = "cmp0"
    stg = [E(nc.sbuf_tensor(f"{tag}_st{i}", [128, S], BF16)) for i in range(2)]
    rkv = E(nc.sbuf_tensor(f"{tag}_rkv", [128, NDC * 2 * S], BF16))
    wsl = [E(nc.sbuf_tensor(f"{tag}_w{i}", [128, NDC * TG], BF16)) for i in range(2)]
    bcf = E(nc.sbuf_tensor(f"{tag}_bcf", [1, D], F32))
    bcb = E(nc.sbuf_tensor(f"{tag}_bcb", [1, D], BF16))
    csb = E(nc.sbuf_tensor(f"{tag}_csb", [128, 2 * NB], BF16))
    tsb = E(nc.sbuf_tensor(f"{tag}_tsb", [128, 128], BF16))
    sqt = E(nc.sbuf_tensor(f"{tag}_sqt", [128, NB], BF16))
    ck2 = E(nc.sbuf_tensor(f"{tag}_ck2", [1, NB + 2], F32))
    qnr = E(nc.sbuf_tensor(f"{tag}_qnr", [1, S], F32))
    nm2 = E(nc.sbuf_tensor(f"{tag}_nm2", [1, S], BF16))
    ones = kb.consts["ones_bf"]
    onesc = kb.consts["onescol_bf"]
    ident = kb.consts["ident"]

    be = kb.dma("sync", f"d_{tag}b", lambda e: e.dma_start(out=bcf[:, :], in_=g["bcomp"]))
    kb.wait_edge("vector", be)
    bce = kb.op("vector", lambda e: e.tensor_copy(bcb[:, :], bcf[:, :]))
    qle = kb.dma("sync", f"d_{tag}q", lambda e: e.dma_start(out=qnr[:, :], in_=g["qnorm"][0:1, :]))

    ve = None
    for dc in range(NDC):
        for w, hb in enumerate(("k0T", "v0T")):
            sl = (dc * 2 + w) % 2
            kb.wait_edge("sync", ve)
            le = kb.dma("sync", f"d_{tag}l{sl}", lambda e, hb=hb, dc=dc, sl=sl: e.dma_start(
                out=stg[sl][:, :], in_=g[hb][dc]))
            kb.wait_edge("vector", le)
            ve = kb.op("vector", lambda e, dc=dc, w=w, sl=sl: e.tensor_copy(
                rkv[:, :].rearrange("p (dc j two n) -> p dc j two n", dc=NDC, j=CBS, two=2)[:, dc, :, w, :],
                stg[sl][:, :].rearrange("p (n j) -> p j n", j=CBS)))

    kb.wait_edge("tensor", ve)
    slab_last = [None, None]
    for sw in range(2):
        for b in range(4):
            kb.wait_edge("tensor", kb.bank_free[b])
        for j in range(CBS):
            sl = j % 2
            kb.wait_edge("gpsimd", slab_last[sl])
            we = kb.dma("gpsimd", f"d_{tag}w{sl}", lambda e, j=j, sw=sw, sl=sl: e.dma_start(
                out=wsl[sl][:, :].rearrange("p (dc o) -> p dc o", dc=NDC),
                in_=g["wcompT"][j].rearrange("(dc p) o -> p dc o", p=128)[:, :, sw * TG:(sw + 1) * TG]))
            kb.wait_edge("tensor", we)
            for b in range(4):
                oc = sw * 4 + b
                for dic in range(NDC):
                    mme = kb.mm(lambda e, sl=sl, b=b, dic=dic, j=j: e.matmul(
                        kb.psum[b][:, 0:2 * NB],
                        wsl[sl][:, dic * TG + b * 128:(dic * TG + (b + 1) * 128)],
                        rkv[:, :].rearrange("p (dc r) -> p dc r", dc=NDC)[:, dic,
                            j * 2 * NB:(j + 1) * 2 * NB],
                        start=(j == 0 and dic == 0), stop=False),
                        inc=(dic == NDC - 1))
            slab_last[sl] = mme
        for b in range(4):
            oc = sw * 4 + b
            kb.wait_edge("tensor", bce)
            mme = kb.mm(lambda e, b=b, oc=oc: e.matmul(
                kb.psum[b][:, 0:2 * NB], bcb[0:1, oc * 128:(oc + 1) * 128], ones[:, 0:2 * NB],
                start=False, stop=True), inc=True)
            kb.wait_edge("vector", mme)
            ve = kb.op("vector", lambda e, b=b: e.tensor_copy(csb[:, :], kb.psum[b][:, 0:2 * NB]))
            kb.bank_free[b] = ve
            kb.wait_edge("sync", ve)
            kb.dma("sync", f"d_{tag}s", lambda e, oc=oc: e.dma_start(
                out=g["ck0T"][oc], in_=csb[:, 0:NB]))
            kb.dma("sync", f"d_{tag}s", lambda e, oc=oc: e.dma_start(
                out=g["cv0T"][oc], in_=csb[:, NB:2 * NB]))
            kb.wait_edge("scalar", ve)
            sqe = kb.op("scalar", lambda e: e.activation(sqt[:, :], csb[:, 0:NB], AF.Square))
            kb.wait_edge("tensor", sqe)
            if oc == 0:
                kb.wait_edge("tensor", kb.bank_free[4])
            sme = kb.mm(lambda e, oc=oc: e.matmul(kb.psum[4][0:1, 0:NB], onesc[:, :], sqt[:, :],
                                                  start=(oc == 0), stop=(oc == 7)),
                        inc=(oc == 7))
            for w, dst in enumerate(("ck0tm", "cv0tm")):
                kb.wait_edge("tensor", kb.bank_free[7])
                te = kb.mm(lambda e, w=w: e.transpose(
                    kb.psum_bf[:, 0:128], csb[:, w * NB:(w + 1) * NB], ident[:, :]), inc=True)
                kb.wait_edge("vector", te)
                kb.wait("vector", f"d_{tag}s", kb.cnt[f"d_{tag}s"])
                ve2 = kb.op("vector", lambda e: e.tensor_copy(tsb[:, :], kb.psum_bf[:, 0:128]))
                kb.bank_free[7] = ve2
                kb.wait_edge("sync", ve2)
                kb.dma("sync", f"d_{tag}s", lambda e, dst=dst, oc=oc: e.dma_start(
                    out=g[dst][:, oc * 128:(oc + 1) * 128], in_=tsb[:, :]))
            kb.wait("vector", f"d_{tag}s", kb.cnt[f"d_{tag}s"])
            kb.wait_edge("vector", sqe)
            kb.wait_edge("vector", ("sd_pe", kb.cnt["sd_pe"]))
    kb.wait_edge("vector", sme)
    ve = kb.op("vector", lambda e: e.reduce_max(ck2[:, NB:NB + 1], kb.psum[4][0:1, 0:NB], axis=AX.X))
    kb.bank_free[4] = ve
    kb.wait_edge("scalar", ve)
    se = kb.op("scalar", lambda e: e.activation(ck2[:, NB + 1:NB + 2], ck2[:, NB:NB + 1], AF.Sqrt))
    kb.wait_edge("vector", se)
    kb.wait("vector", f"d_{tag}q", kb.cnt[f"d_{tag}q"])
    ve = kb.op("vector", lambda e: e.tensor_scalar(nm2[:, :], qnr[:, :], ck2[:, NB + 1:NB + 2], -1.0,
                                                   ALU.mult, ALU.mult))
    kb.wait_edge("sync", ve)
    kb.dma("sync", f"d_{tag}s", lambda e: e.dma_start(out=g["negm"][1:2, :], in_=nm2[:, :]))


def emit_ln(kb, g, pctx, x_hbm, xh_hbm, xl_hbm):
    nc = kb.nc
    E = pctx.enter_context
    ph = "ln_" + xh_hbm.tensor.name
    xt = [E(nc.sbuf_tensor(f"{ph}_x{i}", [128, NDC * TG], F32)) for i in range(2)]
    xb = E(nc.sbuf_tensor(f"{ph}_xb", [128, NDC * TG], BF16))
    xlo = E(nc.sbuf_tensor(f"{ph}_xlo", [128, NDC * TG], BF16))
    sq = E(nc.sbuf_tensor(f"{ph}_sq", [128, NDC * TG], BF16))
    xp = E(nc.sbuf_tensor(f"{ph}_xp", [128, NDC * TG], BF16))
    rows = E(nc.sbuf_tensor(f"{ph}_rows", [1, 6 * TG], F32))
    xnf = E(nc.sbuf_tensor(f"{ph}_xnf", [128, NDC * TG], F32))
    xh = E(nc.sbuf_tensor(f"{ph}_xh", [128, NDC * TG], BF16))
    xl = E(nc.sbuf_tensor(f"{ph}_xl", [128, NDC * TG], BF16))
    onesc = kb.consts["onescol_bf"]
    twosc = kb.consts["twoscol_bf"]
    onesrow_f = kb.consts["ones_f"][:, 0:128]
    mean = rows[:, 0 * TG:1 * TG]
    m2 = rows[:, 1 * TG:2 * TG]
    var = rows[:, 2 * TG:3 * TG]
    sd = rows[:, 3 * TG:4 * TG]
    rcp = rows[:, 4 * TG:5 * TG]
    mrs = rows[:, 5 * TG:6 * TG]

    prev_mm = None
    st_edges = []
    for tg in range(NTG):
        sl = tg % 2
        kb.wait_edge("sync", st_edges[tg - 2][1] if tg >= 2 else None)
        ld = kb.dma("sync", f"d_{ph}{sl}", lambda e, tg=tg, sl=sl: e.dma_start(
            out=xt[sl][:, :].rearrange("p (c t) -> p c t", c=NDC),
            in_=x_hbm[:, :, tg * TG:(tg + 1) * TG].rearrange("c p t -> p c t")))
        kb.wait_edge("vector", ld)
        kb.wait_edge("vector", prev_mm)
        kb.wait_edge("scalar", prev_mm)
        ce = se = None
        for c in range(NDC):
            ce0 = kb.op("vector", lambda e, c=c, sl=sl: e.tensor_copy(
                xb[:, c * TG:(c + 1) * TG], xt[sl][:, c * TG:(c + 1) * TG]))
            kb.op("vector", lambda e, c=c, sl=sl: e.tensor_sub(
                xlo[:, c * TG:(c + 1) * TG], xt[sl][:, c * TG:(c + 1) * TG],
                xb[:, c * TG:(c + 1) * TG]))
            ce = kb.op("vector", lambda e, c=c: e.tensor_mul(
                xp[:, c * TG:(c + 1) * TG], xb[:, c * TG:(c + 1) * TG],
                xlo[:, c * TG:(c + 1) * TG]))
            kb.wait_edge("scalar", ce0)
            se = kb.op("scalar", lambda e, c=c: e.activation(
                sq[:, c * TG:(c + 1) * TG], xb[:, c * TG:(c + 1) * TG], AF.Square))
        kb.wait_edge("tensor", se)
        kb.wait_edge("tensor", ce)
        kb.wait_edge("tensor", kb.bank_free[0])
        kb.wait_edge("tensor", kb.bank_free[1])
        for c in range(NDC):
            kb.mm(lambda e, c=c: e.matmul(kb.psum[0][0:1, :], onesc[:, :],
                                          xb[:, c * TG:(c + 1) * TG],
                                          start=(c == 0), stop=False))
            kb.mm(lambda e, c=c: e.matmul(kb.psum[0][0:1, :], onesc[:, :],
                                          xlo[:, c * TG:(c + 1) * TG],
                                          start=False, stop=(c == NDC - 1)))
            kb.mm(lambda e, c=c: e.matmul(kb.psum[1][0:1, :], onesc[:, :],
                                          sq[:, c * TG:(c + 1) * TG],
                                          start=(c == 0), stop=False))
            prev_mm = kb.mm(lambda e, c=c: e.matmul(kb.psum[1][0:1, :], twosc[:, :],
                                                    xp[:, c * TG:(c + 1) * TG],
                                                    start=False, stop=(c == NDC - 1)),
                            inc=(c == NDC - 1))
        kb.wait_edge("vector", prev_mm)
        kb.op("vector", lambda e: e.tensor_scalar_mul(mean, kb.psum[0][0:1, :], 1.0 / D))
        kb.op("vector", lambda e: e.tensor_scalar_mul(m2, kb.psum[1][0:1, :], 1.0 / D))
        kb.op("vector", lambda e: e.tensor_tensor(var, mean, mean, ALU.mult))
        ve = kb.op("vector", lambda e: e.tensor_sub(var, m2, var))
        kb.bank_free[0] = kb.bank_free[1] = ve
        kb.wait_edge("scalar", ve)
        se = kb.op("scalar", lambda e: e.activation(sd, var, AF.Sqrt, bias=1e-5))
        kb.wait_edge("vector", se)
        kb.op("vector", lambda e: e.reciprocal(rcp, sd))
        ve = kb.op("vector", lambda e: e.tensor_mul(mrs, mean, rcp))
        kb.wait_edge("tensor", ve)
        kb.wait_edge("tensor", kb.bank_free[2])
        kb.wait_edge("tensor", kb.bank_free[3])
        kb.mm(lambda e: e.matmul(kb.psum[2][:, :], onesrow_f, rcp, start=True, stop=True))
        bce = kb.mm(lambda e: e.matmul(kb.psum[3][:, :], onesrow_f, mrs, start=True, stop=True),
                    inc=True)
        kb.wait_edge("vector", bce)
        kb.wait("vector", "sd_scalar", kb.cnt["sd_scalar"])
        kb.wait("vector", "sd_gpsimd", kb.cnt["sd_gpsimd"])
        kb.wait("scalar", f"d_{ph}s", kb.cnt.get(f"d_{ph}s", 0))
        kb.wait("gpsimd", f"d_{ph}s", kb.cnt.get(f"d_{ph}s", 0))
        sub_e = []
        for c in range(NDC):
            kb.op("vector", lambda e, c=c, sl=sl: e.tensor_mul(
                xnf[:, c * TG:(c + 1) * TG], xt[sl][:, c * TG:(c + 1) * TG], kb.psum[2][:, :]))
            sub_e.append(kb.op("vector", lambda e, c=c: e.tensor_sub(
                xnf[:, c * TG:(c + 1) * TG], xnf[:, c * TG:(c + 1) * TG], kb.psum[3][:, :])))
        he_l = []
        for c in range(NDC):
            kb.wait_edge("scalar", sub_e[c])
            he = kb.op("scalar", lambda e, c=c: e.copy(
                xh[:, c * TG:(c + 1) * TG], xnf[:, c * TG:(c + 1) * TG]))
            he_l.append(he)
        for c in range(NDC):
            kb.wait_edge("gpsimd", he_l[c])
            le = kb.op("gpsimd", lambda e, c=c: e.tensor_sub(
                xl[:, c * TG:(c + 1) * TG], xnf[:, c * TG:(c + 1) * TG],
                xh[:, c * TG:(c + 1) * TG]))
        kb.wait_edge("sync", he_l[-1])
        kb.wait_edge("sync", le)
        kb.dma("sync", f"d_{ph}s", lambda e, tg=tg: e.dma_start(
            out=xh_hbm[:, :, tg * TG:(tg + 1) * TG].rearrange("c p t -> p c t"),
            in_=xh[:, :].rearrange("p (c t) -> p c t", c=NDC)))
        kb.dma("sync", f"d_{ph}s", lambda e, tg=tg: e.dma_start(
            out=xl_hbm[:, :, tg * TG:(tg + 1) * TG].rearrange("c p t -> p c t"),
            in_=xl[:, :].rearrange("p (c t) -> p c t", c=NDC)))
        st_edges.append(((f"d_{ph}s", kb.cnt[f"d_{ph}s"]), sub_e[-1]))
        kb.bank_free[2] = kb.bank_free[3] = sub_e[-1]


def emit_banded(kb, g, pctx, tag, nq, kv_nc, qT_hbm, kT_hbm, vtm_hbm, negm_row, out_hbm):
    nc = kb.nc
    E = pctx.enter_context
    MAXKC = min(kv_nc, 8)
    q_sb = [E(nc.sbuf_tensor(f"{tag}_q{i}", [128, NDC * TG], BF16)) for i in range(2)]
    k_sb = [E(nc.sbuf_tensor(f"{tag}_k{i}", [128, NDC * MAXKC * 128], BF16)) for i in range(2)]
    v_sb = [E(nc.sbuf_tensor(f"{tag}_v{i}", [128, MAXKC * D], BF16)) for i in range(2)]
    e_sb = E(nc.sbuf_tensor(f"{tag}_e", [128, MAXKC * TG], BF16))
    mcol = E(nc.sbuf_tensor(f"{tag}_mc", [128, 8], F32))
    nmr = E(nc.sbuf_tensor(f"{tag}_nmr", [1, TG], BF16))
    mrow_f = E(nc.sbuf_tensor(f"{tag}_mrf", [1, TG], F32))
    rs_sb = E(nc.sbuf_tensor(f"{tag}_rs", [1, TG], F32))
    rb_sb = E(nc.sbuf_tensor(f"{tag}_rb", [128, TG], F32))
    o_sb = [E(nc.sbuf_tensor(f"{tag}_o{i}", [128, TG], BF16)) for i in range(4)]
    onesc = kb.consts["onescol_bf"]
    onesrow_b = kb.consts["ones_bf"][:, 0:128]
    onesrow_f = kb.consts["ones_f"][:, 0:128]
    tri_ge, tri_le = kb.consts["tri_ge"], kb.consts["tri_le"]

    ngr = (nq + TG - 1) // TG
    slot_done = [None, None]
    o_store = [None] * 4
    for qg in range(ngr):
        qlo = qg * TG
        W = min(TG, nq - qlo)
        ns = W // 128
        s_first, s_last = qlo // 128, (qlo + W - 1) // 128
        kclo = max(0, s_first - 2)
        kchi = min(kv_nc - 1, s_last + 2)
        nkc = kchi - kclo + 1
        sl = qg % 2
        kb.wait_edge("sync", slot_done[sl])
        lde = kb.dma("sync", f"d_{tag}q{sl}", lambda e, qlo=qlo, W=W, sl=sl: e.dma_start(
            out=q_sb[sl][:, :].rearrange("p (dc t) -> p dc t", t=TG)[:, :, 0:W],
            in_=qT_hbm[:, :, qlo:qlo + W].rearrange("dc p t -> p dc t")))
        lde = kb.dma("sync", f"d_{tag}q{sl}", lambda e, kclo=kclo, nkc=nkc, sl=sl: e.dma_start(
            out=k_sb[sl][:, :].rearrange("p (dc t) -> p dc t", t=MAXKC * 128)[:, :, 0:nkc * 128],
            in_=kT_hbm[:, :, kclo * 128:(kclo + nkc) * 128].rearrange("dc p t -> p dc t")))
        lde = kb.dma("sync", f"d_{tag}q{sl}", lambda e, kclo=kclo, nkc=nkc, sl=sl: e.dma_start(
            out=v_sb[sl][:, 0:nkc * D].rearrange("p (kc d) -> p kc d", d=D),
            in_=vtm_hbm[kclo * 128:(kclo + nkc) * 128, :].rearrange("(kc p) d -> p kc d", p=128)))
        kb.wait_edge("tensor", lde)

        for si in range(ns):
            s = s_first + si
            kcs = [kc for kc in range(kclo, kchi + 1) if abs(kc - s) <= 2]
            for ki, kc in enumerate(kcs):
                bk = 3 + (ki % 2)
                kb.wait_edge("tensor", kb.bank_free[bk])
                for dc in range(NDC):
                    mqe = kb.mm(lambda e, dc=dc, s=s, kc=kc, sl=sl, bk=bk, qlo=qlo, kclo=kclo: e.matmul(
                        kb.psum[bk][:, 0:128],
                        q_sb[sl][:, dc * TG + s * 128 - qlo: dc * TG + (s + 1) * 128 - qlo],
                        k_sb[sl][:, dc * MAXKC * 128 + (kc - kclo) * 128:
                                 dc * MAXKC * 128 + (kc - kclo + 1) * 128],
                        start=(dc == 0), stop=(dc == NDC - 1)), inc=(dc == NDC - 1))
                kb.wait_edge("vector", mqe)
                if ki == 0:
                    ve = kb.op("vector", lambda e, si=si, bk=bk: e.reduce_max(
                        mcol[:, si:si + 1], kb.psum[bk][:, 0:128], axis=AX.X))
                else:
                    ve = kb.op("vector", lambda e, bk=bk: e.reduce_max(
                        mcol[:, 7:8], kb.psum[bk][:, 0:128], axis=AX.X))
                    ve = kb.op("vector", lambda e, si=si: e.tensor_max(
                        mcol[:, si:si + 1], mcol[:, si:si + 1], mcol[:, 7:8]))
                kb.bank_free[bk] = ve
            kb.wait_edge("sync", ve)
            kb.dma("sync", f"d_{tag}mc", lambda e, si=si, s=s: e.dma_start(
                out=g["mrow"][0:1, s * 128:(s + 1) * 128],
                in_=mcol[:, si:si + 1]))
        kb.wait("sync", f"d_{tag}mc", kb.cnt[f"d_{tag}mc"])
        mre = kb.dma("sync", f"d_{tag}mr", lambda e, qlo=qlo, W=W: e.dma_start(
            out=mrow_f[:, 0:W], in_=g["mrow"][0:1, qlo:qlo + W]))
        kb.wait_edge("vector", mre)
        nve = kb.op("vector", lambda e, W=W: e.tensor_scalar_mul(nmr[:, 0:W], mrow_f[:, 0:W], -1.0))
        kb.wait_edge("tensor", nve)

        exp_edges = []
        for kci in range(nkc):
            kc = kclo + kci
            a = max(0, (kc - 2) * 128 - qlo)
            b = min(W, (kc + 3) * 128 - qlo)
            eb = kb.psum[kci % 2]
            if a > 0 or b < W:
                me = kb.op("vector", lambda e, kci=kci, W=W: e.memset(e_sb[:, kci * TG: kci * TG + W], 0.0))
            kb.wait_edge("tensor", kb.bank_free[kci % 2])
            for dc in range(NDC):
                kb.mm(lambda e, dc=dc, kci=kci, a=a, b=b, sl=sl, eb=eb: e.matmul(
                    eb[:, 0:b - a],
                    k_sb[sl][:, dc * MAXKC * 128 + kci * 128: dc * MAXKC * 128 + (kci + 1) * 128],
                    q_sb[sl][:, dc * TG + a: dc * TG + b], start=(dc == 0), stop=False))
            mme = kb.mm(lambda e, a=a, b=b, eb=eb: e.matmul(
                eb[:, 0:b - a], onesrow_b, nmr[:, a:b],
                start=False, stop=True), inc=True)
            kb.wait_edge("scalar", mme)
            if a > 0 or b < W:
                kb.wait_edge("scalar", me)
            xe = kb.op("scalar", lambda e, kci=kci, a=a, b=b, eb=eb: e.activation(
                e_sb[:, kci * TG + a: kci * TG + b], eb[:, 0:b - a], AF.Exp, scale=SCALE0))
            kb.bank_free[kci % 2] = xe
            exp_edges.append(xe)
            for s_m, tri in ((kc - 2, tri_le), (kc + 2, tri_ge)):
                if s_m < s_first or s_m > s_last:
                    continue
                off = s_m * 128 - qlo
                kb.wait_edge("vector", xe)
                kb.op("vector", lambda e, kci=kci, off=off, tri=tri: e.tensor_mul(
                    e_sb[:, kci * TG + off: kci * TG + off + 128],
                    e_sb[:, kci * TG + off: kci * TG + off + 128], tri[:, :]))
        kb.wait("tensor", "sd_scalar", kb.cnt["sd_scalar"])
        kb.wait("tensor", "sd_vector", kb.cnt["sd_vector"])
        kb.wait_edge("tensor", kb.bank_free[2])
        for kci in range(nkc):
            sme = kb.mm(lambda e, kci=kci, W=W, nkc=nkc: e.matmul(
                kb.psum[2][0:1, 0:W], onesc[:, :], e_sb[:, kci * TG: kci * TG + W],
                start=(kci == 0), stop=(kci == nkc - 1)), inc=(kci == nkc - 1))
        kb.wait_edge("vector", sme)
        re = kb.op("vector", lambda e, W=W: e.reciprocal(rs_sb[:, 0:W], kb.psum[2][0:1, 0:W]))
        kb.bank_free[2] = re
        kb.wait_edge("tensor", re)
        kb.wait_edge("tensor", kb.bank_free[0])
        bce = kb.mm(lambda e, W=W: e.matmul(kb.psum[0][:, 0:W], onesrow_f, rs_sb[:, 0:W],
                                            start=True, stop=True), inc=True)
        kb.wait_edge("scalar", bce)
        ce = kb.op("scalar", lambda e, W=W: e.copy(rb_sb[:, 0:W], kb.psum[0][:, 0:W]))
        kb.bank_free[0] = ce
        pe_last = None
        for half in range(2):
            for eo in range(4):
                ec = half * 4 + eo
                bk = 3 + eo
                kb.wait_edge("tensor", kb.bank_free[bk])
                for kci in range(nkc):
                    pe_last = kb.mm(lambda e, kci=kci, ec=ec, sl=sl, W=W, bk=bk, nkc=nkc: e.matmul(
                        kb.psum[bk][:, 0:W], v_sb[sl][:, kci * D + ec * 128: kci * D + (ec + 1) * 128],
                        e_sb[:, kci * TG: kci * TG + W],
                        start=(kci == 0), stop=(kci == nkc - 1)), inc=(kci == nkc - 1))
                ob = eo if half == 0 else eo
                ob = (half * 4 + eo) % 4
                kb.wait_edge("vector", pe_last)
                kb.wait_edge("vector", ce)
                kb.wait_edge("vector", o_store[ob])
                ve = kb.op("vector", lambda e, bk=bk, ob=ob, W=W: e.tensor_mul(
                    o_sb[ob][:, 0:W], kb.psum[bk][:, 0:W], rb_sb[:, 0:W]))
                kb.bank_free[bk] = ve
                kb.wait_edge("sync", ve)
                o_store[ob] = kb.dma("sync", f"d_{tag}o{ob}", lambda e, ec=ec, ob=ob, qlo=qlo,
                                     W=W: e.dma_start(
                    out=out_hbm[ec, :, qlo:qlo + W], in_=o_sb[ob][:, 0:W]))
        slot_done[sl] = pe_last


def emit_gcomb0(kb, g, pctx):
    nc = kb.nc
    E = pctx.enter_context
    tag = "gc0"
    xin, lde = load_xin(kb, pctx, tag, g["xh0"])
    kb.wait_edge("tensor", lde)
    wg_f = E(nc.sbuf_tensor(f"{tag}_wgf", [128, NDC * 3], F32))
    wg_b = E(nc.sbuf_tensor(f"{tag}_wgb", [128, NDC * 3], BF16))
    bg_f = E(nc.sbuf_tensor(f"{tag}_bgf", [1, 3], F32))
    bg_b = E(nc.sbuf_tensor(f"{tag}_bgb", [1, 3], BF16))
    wo_sb = E(nc.sbuf_tensor(f"{tag}_wo", [128, NDC * D], BF16))
    bo_f = E(nc.sbuf_tensor(f"{tag}_bof", [1, D], F32))
    bo_b = E(nc.sbuf_tensor(f"{tag}_bob", [1, D], BF16))
    grow = [E(nc.sbuf_tensor(f"{tag}_gr{c}", [1, TG], F32)) for c in range(3)]
    gb = [E(nc.sbuf_tensor(f"{tag}_gb{c}", [128, TG], F32)) for c in range(3)]
    ot = [E(nc.sbuf_tensor(f"{tag}_ot{c}", [128, TG], BF16)) for c in range(3)]
    tt = [E(nc.sbuf_tensor(f"{tag}_tt{c}", [128, TG], F32)) for c in range(2)]
    comb = E(nc.sbuf_tensor(f"{tag}_cb", [128, NDC * TG], BF16))
    o_sb = [E(nc.sbuf_tensor(f"{tag}_o{i}", [128, 4 * TG], BF16)) for i in range(2)]
    ones = kb.consts["ones_bf"]
    onesrow_f = kb.consts["ones_f"][:, 0:128]
    e1 = kb.dma("gpsimd", f"d_{tag}w", lambda e: e.dma_start(
        out=wg_f[:, :], in_=g["wgateT"].rearrange("p dc c -> p (dc c)")))
    e2 = kb.dma("sync", f"d_{tag}b", lambda e: e.dma_start(out=bg_f[:, :], in_=g["bgate"]))
    e3 = kb.dma("sync", f"d_{tag}b", lambda e: e.dma_start(out=bo_f[:, :], in_=g["bo"]))
    kb.dma("gpsimd", f"d_{tag}w", lambda e: e.dma_start(
        out=wo_sb[:, :].rearrange("p (dc o) -> p dc o", dc=NDC),
        in_=g["woT"].rearrange("(dc p) o -> p dc o", p=128)))
    kb.wait("vector", f"d_{tag}w", kb.cnt[f"d_{tag}w"])
    kb.wait("vector", f"d_{tag}b", kb.cnt[f"d_{tag}b"])
    c1 = kb.op("vector", lambda e: e.tensor_copy(wg_b[:, :], wg_f[:, :]))
    c2 = kb.op("vector", lambda e: e.tensor_copy(bg_b[:, :], bg_f[:, :]))
    c3 = kb.op("vector", lambda e: e.tensor_copy(bo_b[:, :], bo_f[:, :]))
    kb.wait_edge("tensor", c3)

    o_store = [None] * 4
    oc_loads = None
    for tg in range(NTG):
        for c in range(3):
            kb.wait_edge("tensor", kb.bank_free[c])
            for dc in range(NDC):
                kb.mm(lambda e, c=c, dc=dc, tg=tg: e.matmul(
                    kb.psum[c][0:1, :], wg_b[:, dc * 3 + c: dc * 3 + c + 1],
                    xin[:, dc * S + tg * TG:(dc * S + (tg + 1) * TG)],
                    start=(dc == 0), stop=False))
            gme = kb.mm(lambda e, c=c: e.matmul(
                kb.psum[c][0:1, :], bg_b[0:1, c:c + 1], ones[:, :],
                start=False, stop=True), inc=True)
        kb.wait_edge("scalar", gme)
        for c in range(3):
            se = kb.op("scalar", lambda e, c=c: e.activation(grow[c][:, :], kb.psum[c][0:1, :],
                                                             AF.Sigmoid))
            kb.bank_free[c] = se
        kb.wait_edge("tensor", se)
        for c in range(3):
            kb.wait_edge("tensor", kb.bank_free[3 + c])
            be = kb.mm(lambda e, c=c: e.matmul(kb.psum[3 + c][:, :], onesrow_f, grow[c][:, :],
                                               start=True, stop=True), inc=True)
        kb.wait_edge("scalar", be)
        kb.wait("scalar", "sd_vector", kb.cnt["sd_vector"])
        for c in range(3):
            ce = kb.op("scalar", lambda e, c=c: e.copy(gb[c][:, :], kb.psum[3 + c][:, :]))
            kb.bank_free[3 + c] = ce
        nb_r = 384 - tg * TG
        ns_r = 1280 - tg * TG
        kb.wait("vector", "sd_pe", kb.cnt.get("sd_pe", 0))
        for dc in range(NDC):
            lw = kb.dma("sync", f"d_{tag}ow", lambda e, dc=dc, tg=tg: e.dma_start(
                out=ot[2][:, :], in_=g["ow0T"][dc, :, tg * TG:(tg + 1) * TG]))
            if ns_r > 0:
                lw = kb.dma("sync", f"d_{tag}ow", lambda e, dc=dc, tg=tg, w=min(TG, ns_r): e.dma_start(
                    out=ot[1][:, 0:w], in_=g["os0T"][dc, :, tg * TG: tg * TG + w]))
            if nb_r > 0:
                lw = kb.dma("sync", f"d_{tag}ow", lambda e, dc=dc, tg=tg, w=min(TG, nb_r): e.dma_start(
                    out=ot[0][:, 0:w], in_=g["oc0T"][dc, :, tg * TG: tg * TG + w]))
            kb.wait_edge("vector", lw)
            kb.wait_edge("vector", ce)
            ve = kb.op("vector", lambda e, dc=dc: e.tensor_mul(tt[0][:, :], ot[2][:, :], gb[2][:, :]))
            if ns_r > 0:
                w = min(TG, ns_r)
                kb.op("vector", lambda e, w=w: e.tensor_mul(tt[1][:, 0:w], ot[1][:, 0:w], gb[1][:, 0:w]))
                ve = kb.op("vector", lambda e, w=w: e.tensor_add(tt[0][:, 0:w], tt[0][:, 0:w], tt[1][:, 0:w]))
            if nb_r > 0:
                w = min(TG, nb_r)
                kb.op("vector", lambda e, w=w: e.tensor_mul(tt[1][:, 0:w], ot[0][:, 0:w], gb[0][:, 0:w]))
                ve = kb.op("vector", lambda e, w=w: e.tensor_add(tt[0][:, 0:w], tt[0][:, 0:w], tt[1][:, 0:w]))
            ve = kb.op("vector", lambda e, dc=dc: e.tensor_copy(comb[:, dc * TG:(dc + 1) * TG], tt[0][:, :]))
            kb.wait("sync", "sd_vector", kb.cnt["sd_vector"])
        kb.wait_edge("tensor", ve)
        for oc in range(NDC):
            bk = oc % 6
            kb.wait_edge("tensor", kb.bank_free[bk])
            for dc in range(NDC):
                kb.mm(lambda e, oc=oc, dc=dc, bk=bk: e.matmul(
                    kb.psum[bk][:, :], wo_sb[:, dc * D + oc * 128: dc * D + (oc + 1) * 128],
                    comb[:, dc * TG:(dc + 1) * TG], start=(dc == 0), stop=False))
            mme = kb.mm(lambda e, oc=oc, bk=bk: e.matmul(
                kb.psum[bk][:, :], bo_b[0:1, oc * 128:(oc + 1) * 128], ones[:, :],
                start=False, stop=True), inc=True)
            ob = (oc // 4) % 2
            kb.wait_edge("vector", mme)
            if oc % 4 == 0:
                kb.wait_edge("vector", o_store[ob])
            ve2 = kb.op("vector", lambda e, ob=ob, bk=bk, oc=oc: e.tensor_copy(
                o_sb[ob][:, (oc % 4) * TG:(oc % 4 + 1) * TG], kb.psum[bk][:, :]))
            kb.bank_free[bk] = ve2
            if oc % 4 == 3:
                kb.wait_edge("sync", ve2)
                o_store[ob] = kb.dma("sync", f"d_{tag}s{ob}", lambda e, oc=oc, ob=ob, tg=tg: e.dma_start(
                    out=g["s0T"][oc - 3:oc + 1, :, tg * TG:(tg + 1) * TG].rearrange("c p t -> p c t"),
                    in_=o_sb[ob][:, :].rearrange("p (c t) -> p c t", c=4)))


def emit_router(kb, g, pctx, tag, xh_hbm, xl_hbm, wgT_name, lg_hbm):
    nc = kb.nc
    E = pctx.enter_context
    wg_f = E(nc.sbuf_tensor(f"{tag}_wgf", [128, NDC * NEXP], F32))
    wg_h = E(nc.sbuf_tensor(f"{tag}_wgh", [128, NDC * NEXP], BF16))
    wg_l = E(nc.sbuf_tensor(f"{tag}_wgl", [128, NDC * NEXP], BF16))
    xh = [E(nc.sbuf_tensor(f"{tag}_xh{i}", [128, TG], BF16)) for i in range(2)]
    xl = [E(nc.sbuf_tensor(f"{tag}_xl{i}", [128, TG], BF16)) for i in range(2)]
    lg_sb = E(nc.sbuf_tensor(f"{tag}_lg", [NEXP, TG], F32))
    we = kb.dma("sync", f"d_{tag}w", lambda e: e.dma_start(
        out=wg_f[:, :], in_=g[wgT_name].rearrange("p dc c -> p (dc c)")))
    kb.wait_edge("vector", we)
    kb.op("vector", lambda e: e.tensor_copy(wg_h[:, :], wg_f[:, :]))
    ce = kb.op("vector", lambda e: e.tensor_sub(wg_l[:, :], wg_f[:, :], wg_h[:, :]))
    kb.wait_edge("tensor", ce)
    slot_pe = [None, None]
    for tg in range(NTG):
        for dc in range(NDC):
            sl = dc % 2
            kb.wait_edge("sync", slot_pe[sl])
            de = kb.dma("sync", f"d_{tag}l{sl}", lambda e, dc=dc, tg=tg, sl=sl: e.dma_start(
                out=xh[sl][:, :], in_=xh_hbm[dc, :, tg * TG:(tg + 1) * TG]))
            de = kb.dma("sync", f"d_{tag}l{sl}", lambda e, dc=dc, tg=tg, sl=sl: e.dma_start(
                out=xl[sl][:, :], in_=xl_hbm[dc, :, tg * TG:(tg + 1) * TG]))
            kb.wait_edge("tensor", de)
            if dc == 0:
                kb.wait_edge("tensor", kb.bank_free[6])
            for ti, (wt_, xt_) in enumerate(((wg_h, xh), (wg_h, xl), (wg_l, xh))):
                mme = kb.mm(lambda e, dc=dc, sl=sl, ti=ti, wt_=wt_, xt_=xt_: e.matmul(
                    kb.psum[6][0:NEXP, :], wt_[:, dc * NEXP:(dc + 1) * NEXP], xt_[sl][:, :],
                    start=(dc == 0 and ti == 0), stop=(dc == NDC - 1 and ti == 2)),
                    inc=(ti == 2))
            slot_pe[sl] = mme
        kb.wait_edge("vector", mme)
        ve = kb.op("vector", lambda e: e.tensor_copy(lg_sb[:, :], kb.psum[6][0:NEXP, :]))
        kb.bank_free[6] = ve
        kb.wait_edge("sync", ve)
        kb.dma("sync", f"d_{tag}s", lambda e, tg=tg: e.dma_start(
            out=lg_hbm[:, tg * TG:(tg + 1) * TG], in_=lg_sb[:, :]))
        kb.wait("vector", f"d_{tag}s", kb.cnt[f"d_{tag}s"])


def emit_ffn0a(kb, g, pctx):
    nc = kb.nc
    E = pctx.enter_context
    tag = "f0a"
    emit_router(kb, g, pctx, tag + "r", g["xh0"], g["xl0"], "wg0T", g["r0lg"])
    lrow = E(nc.sbuf_tensor(f"{tag}_lr", [32, NEXP * 128], F32))
    mx = E(nc.sbuf_tensor(f"{tag}_mx", [32, 128], F32))
    msk = E(nc.sbuf_tensor(f"{tag}_msk", [32, 128], F32))
    nrow = E(nc.sbuf_tensor(f"{tag}_nr", [32, 128], BF16))
    kb.wait("sync", f"d_{tag}rs", kb.cnt.get(f"d_{tag}rs", 0))
    lde = None
    for e_ in range(NEXP):
        lde = kb.dma("sync", f"d_{tag}lg", lambda e, e_=e_: e.dma_start(
            out=lrow[:, e_ * 128:(e_ + 1) * 128],
            in_=g["r0lg"][e_].rearrange("(a b) -> a b", b=128)))
    kb.wait_edge("vector", lde)
    ve = kb.op("vector", lambda e: e.tensor_copy(mx[:, :], lrow[:, 0:128]))
    for e_ in range(1, NEXP):
        ve = kb.op("vector", lambda e, e_=e_: e.tensor_max(mx[:, :], mx[:, :],
                                                           lrow[:, e_ * 128:(e_ + 1) * 128]))
    for e_ in range(NEXP):
        kb.op("vector", lambda e, e_=e_: e.tensor_tensor(msk[:, :], lrow[:, e_ * 128:(e_ + 1) * 128],
                                                         mx[:, :], ALU.is_lt))
        ne = kb.op("vector", lambda e: e.tensor_scalar_mul(nrow[:, :], msk[:, :], NEG))
        kb.wait_edge("sync", ne)
        kb.dma("sync", f"d_{tag}nr", lambda e, e_=e_: e.dma_start(
            out=g["r0neg"][e_].rearrange("(a b) -> a b", b=128), in_=nrow[:, :]))
        kb.wait("vector", f"d_{tag}nr", kb.cnt[f"d_{tag}nr"])
    xin, lde = load_xin(kb, pctx, tag, g["xh0"])
    kb.wait_edge("tensor", lde)
    gw = [E(nc.sbuf_tensor(f"{tag}_gw{i}", [128, NDC * D], BF16)) for i in range(2)]
    uw = [E(nc.sbuf_tensor(f"{tag}_uw{i}", [128, NDC * D], BF16)) for i in range(2)]
    nr_sb = E(nc.sbuf_tensor(f"{tag}_nrs", [1, S], BF16))
    tmp = E(nc.sbuf_tensor(f"{tag}_tmp", [128, TG], F32))
    h_sb = [E(nc.sbuf_tensor(f"{tag}_h{i}", [128, NDC * TG], BF16)) for i in range(2)]
    onesrow_b = kb.consts["ones_bf"][:, 0:128]
    slab_last = [None, None]
    h_store = [None] * 2
    nre = None
    for e_ in range(NEXP):
        sl = e_ % 2
        kb.wait_edge("gpsimd", slab_last[sl])
        for wsl, bank in ((gw, "gbank"), (uw, "ubank")):
            we = kb.dma("gpsimd", f"d_{tag}w{sl}", lambda e, e_=e_, sl=sl, wsl=wsl, bank=bank:
                        e.dma_start(out=wsl[sl][:, :].rearrange("p (dc o) -> p dc o", dc=NDC),
                                    in_=g[bank][e_].rearrange("(dc p) o -> p dc o", p=128)))
        kb.wait_edge("sync", nre)
        nre_ld = kb.dma("sync", f"d_{tag}nl", lambda e, e_=e_: e.dma_start(
            out=nr_sb[:, :], in_=g["r0neg"][e_:e_ + 1, :]))
        kb.wait_edge("tensor", we)
        kb.wait_edge("tensor", nre_ld)
        for tg in range(NTG):
            for hc in range(NDC):
                bka, bku = hc % 2, 2 + hc % 2
                kb.wait_edge("tensor", kb.bank_free[bka])
                for dc in range(NDC):
                    kb.mm(lambda e, sl=sl, dc=dc, hc=hc, tg=tg, bka=bka: e.matmul(
                        kb.psum[bka][:, :], gw[sl][:, dc * D + hc * 128: dc * D + (hc + 1) * 128],
                        xin[:, dc * S + tg * TG: dc * S + (tg + 1) * TG],
                        start=(dc == 0), stop=False))
                ae = kb.mm(lambda e, tg=tg, bka=bka: e.matmul(
                    kb.psum[bka][:, :], onesrow_b, nr_sb[:, tg * TG:(tg + 1) * TG],
                    start=False, stop=True), inc=True)
                kb.wait_edge("tensor", kb.bank_free[bku])
                for dc in range(NDC):
                    ue = kb.mm(lambda e, sl=sl, dc=dc, hc=hc, tg=tg, bku=bku: e.matmul(
                        kb.psum[bku][:, :], uw[sl][:, dc * D + hc * 128: dc * D + (hc + 1) * 128],
                        xin[:, dc * S + tg * TG: dc * S + (tg + 1) * TG],
                        start=(dc == 0), stop=(dc == NDC - 1)), inc=(dc == NDC - 1))
                kb.wait_edge("scalar", ae)
                kb.wait("scalar", "sd_vector", kb.cnt["sd_vector"])
                se = kb.op("scalar", lambda e, bka=bka: e.activation(tmp[:, :], kb.psum[bka][:, :], AF.Silu))
                kb.bank_free[bka] = se
                hb = (e_ * NTG + tg) % 2
                kb.wait_edge("vector", se)
                kb.wait_edge("vector", ue)
                if hc == 0:
                    kb.wait_edge("vector", h_store[hb])
                ve = kb.op("vector", lambda e, hb=hb, hc=hc, bku=bku: e.tensor_mul(
                    h_sb[hb][:, hc * TG:(hc + 1) * TG], tmp[:, :], kb.psum[bku][:, :]))
                kb.bank_free[bku] = ve
            kb.wait_edge("sync", ve)
            h_store[hb] = kb.dma("sync", f"d_{tag}h{hb}", lambda e, e_=e_, tg=tg, hb=hb:
                                 e.dma_start(out=g["h0"][e_ * NDC:(e_ + 1) * NDC, :,
                                                         tg * TG:(tg + 1) * TG].rearrange(
                                     "hc p t -> p hc t"),
                                     in_=h_sb[hb][:, :].rearrange("p (hc t) -> p hc t", hc=NDC)))
            slab_last[sl] = ue
        nre = ae


def emit_ffn0b(kb, g, pctx):
    nc = kb.nc
    E = pctx.enter_context
    tag = "f0b"
    dw = E(nc.sbuf_tensor(f"{tag}_dw", [128, NEXP * NDC * 512], BF16))
    h_sb = [E(nc.sbuf_tensor(f"{tag}_h{i}", [128, NDC * TG], BF16)) for i in range(2)]
    s0t = E(nc.sbuf_tensor(f"{tag}_s0", [128, TG], BF16))
    xt = E(nc.sbuf_tensor(f"{tag}_x", [128, TG], F32))
    gt = E(nc.sbuf_tensor(f"{tag}_g", [128, TG], F32))
    pt = E(nc.sbuf_tensor(f"{tag}_p", [128, TG], F32))
    x1t = [E(nc.sbuf_tensor(f"{tag}_x1{i}", [128, 4 * TG], F32)) for i in range(2)]
    we = None
    x1_store = [None, None]
    for sw in range(2):
        kb.wait("gpsimd", "sd_pe", kb.cnt.get("sd_pe", 0) if sw > 0 else 0)
        for e_ in range(NEXP):
            we = kb.dma("gpsimd", f"d_{tag}w{sw}", lambda e, e_=e_, sw=sw: e.dma_start(
                out=dw[:, e_ * NDC * 512: (e_ + 1) * NDC * 512].rearrange(
                    "p (hc o) -> p hc o", hc=NDC),
                in_=g["dbank"][e_].rearrange("(hc p) o -> p hc o", p=128)[:, :, sw * 512:(sw + 1) * 512]))
        kb.wait_edge("tensor", we)
        slot_mm = [None, None]
        for tg in range(NTG):
            for e_ in range(NEXP):
                sl = e_ % 2
                kb.wait_edge("sync", slot_mm[sl])
                lde = kb.dma("sync", f"d_{tag}h{sl}", lambda e, e_=e_, tg=tg, sl=sl:
                             e.dma_start(out=h_sb[sl][:, :].rearrange("p (hc t) -> p hc t", hc=NDC),
                                         in_=g["h0"][e_ * NDC:(e_ + 1) * NDC, :,
                                                     tg * TG:(tg + 1) * TG].rearrange(
                                             "hc p t -> p hc t")))
                kb.wait_edge("tensor", lde)
                if e_ == 0:
                    for b in range(4):
                        kb.wait_edge("tensor", kb.bank_free[b])
                for do in range(4):
                    for hc in range(NDC):
                        mme = kb.mm(lambda e, e_=e_, do=do, hc=hc, sl=sl, sw=sw: e.matmul(
                            kb.psum[do][:, :],
                            dw[:, e_ * NDC * 512 + hc * 512 + do * 128:
                               e_ * NDC * 512 + hc * 512 + (do + 1) * 128],
                            h_sb[sl][:, hc * TG:(hc + 1) * TG],
                            start=(e_ == 0 and hc == 0), stop=(e_ == NEXP - 1 and hc == NDC - 1)),
                            inc=(hc == NDC - 1))
                slot_mm[sl] = mme
                h_last = mme
            for do in range(4):
                dc = sw * 4 + do
                kb.wait("sync", "sd_vector", kb.cnt["sd_vector"])
                l1 = kb.dma("sync", f"d_{tag}e", lambda e, dc=dc, tg=tg: e.dma_start(
                    out=s0t[:, :], in_=g["s0T"][dc, :, tg * TG:(tg + 1) * TG]))
                l2 = kb.dma("sync", f"d_{tag}e", lambda e, dc=dc, tg=tg: e.dma_start(
                    out=xt[:, :], in_=g["x_fm"][dc, :, tg * TG:(tg + 1) * TG]))
                kb.wait_edge("scalar", h_last)
                kb.wait("scalar", "sd_vector", kb.cnt["sd_vector"])
                se = kb.op("scalar", lambda e, do=do: e.activation(gt[:, :], kb.psum[do][:, :],
                                                                   AF.Silu))
                kb.bank_free[do] = se
                kb.wait_edge("vector", se)
                kb.wait_edge("vector", l2)
                sx = tg % 2
                if do == 0:
                    kb.wait_edge("vector", x1_store[sx])
                kb.op("vector", lambda e: e.tensor_mul(pt[:, :], gt[:, :], s0t[:, :]))
                ve = kb.op("vector", lambda e, sx=sx, do=do: e.tensor_add(
                    x1t[sx][:, do * TG:(do + 1) * TG], xt[:, :], pt[:, :]))
            kb.wait_edge("sync", ve)
            x1_store[sx] = kb.dma("sync", f"d_{tag}x1{sx}", lambda e, sw=sw, tg=tg, sx=sx:
                                  e.dma_start(out=g["x1"][sw * 4:(sw + 1) * 4, :,
                                                          tg * TG:(tg + 1) * TG].rearrange(
                                      "c p t -> p c t"),
                                      in_=x1t[sx][:, :].rearrange("p (c t) -> p c t", c=4)))


def emit_qkv1(kb, g, pctx):
    nc = kb.nc
    E = pctx.enter_context
    tag = "qkv1"
    cos_sb = E(nc.sbuf_tensor(f"{tag}_cos", [128, S], F32))
    sin_sb = E(nc.sbuf_tensor(f"{tag}_sin", [128, S], F32))
    qsb = E(nc.sbuf_tensor(f"{tag}_qsb", [128, TG], BF16))
    t1 = E(nc.sbuf_tensor(f"{tag}_t1", [128, TG], F32))
    t2 = E(nc.sbuf_tensor(f"{tag}_t2", [128, TG], F32))
    ro = [E(nc.sbuf_tensor(f"{tag}_ro{i}", [128, TG], BF16)) for i in range(4)]
    ro_store = [None] * 4
    kb.dma("sync", f"d_{tag}cs", lambda e: e.dma_start(out=cos_sb[:, :], in_=g["c_cos"]))
    ce = kb.dma("sync", f"d_{tag}cs", lambda e: e.dma_start(out=sin_sb[:, :], in_=g["c_sin"]))
    kb.wait_edge("vector", ce)
    ptrot = kb.consts["ptrot"]
    ri = [0]

    def rope_epi(oc, tg, bk, mme):
        ten = g["q1T"] if oc < NDC else g["k1T"]
        occ = oc % NDC
        kb.wait_edge("vector", mme)
        kb.wait("vector", "sd_pe", kb.cnt["sd_pe"])
        ve = kb.op("vector", lambda e: e.tensor_copy(qsb[:, :], kb.psum[bk][:, :]))
        kb.bank_free[bk] = ve
        rb = 4 + ri[0] % 2
        ri[0] += 1
        kb.wait_edge("tensor", ve)
        kb.wait_edge("tensor", kb.bank_free[rb])
        rme = kb.mm(lambda e, rb=rb: e.matmul(kb.psum[rb][:, :], ptrot[:, :], qsb[:, :],
                                              start=True, stop=True), inc=True)
        ob = ri[0] % 4
        kb.wait_edge("vector", rme)
        kb.wait_edge("vector", ro_store[ob])
        kb.op("vector", lambda e, tg=tg: e.tensor_mul(t1[:, :], qsb[:, :],
                                                      cos_sb[:, tg * TG:(tg + 1) * TG]))
        kb.op("vector", lambda e, tg=tg, rb=rb: e.tensor_mul(t2[:, :], kb.psum[rb][:, :],
                                                             sin_sb[:, tg * TG:(tg + 1) * TG]))
        ve = kb.op("vector", lambda e, ob=ob: e.tensor_add(ro[ob][:, :], t1[:, :], t2[:, :]))
        kb.bank_free[rb] = ve
        kb.wait_edge("sync", ve)
        ro_store[ob] = kb.dma("sync", f"d_{tag}r{ob}", lambda e, ten=ten, occ=occ, tg=tg, ob=ob:
                              e.dma_start(out=ten[occ, :, tg * TG:(tg + 1) * TG], in_=ro[ob][:, :]))

    xin = emit_linear(kb, g, pctx, tag, g["wqk1T"], 2 * D, lambda oc: ((None, oc)),
                      in_hbm=g["xh1"], banks=(0, 1, 2, 3), epi=rope_epi)
    emit_linear_tm(kb, g, pctx, tag + "v", g["wv1T"], xin, g["v1tm"], banks=(5, 6))


def emit_att1(kb, g, pctx):
    nc = kb.nc
    E = pctx.enter_context
    tag = "a1"
    NKC = S // 128
    kt = [E(nc.sbuf_tensor(f"{tag}_kt{i}", [128, S], BF16)) for i in range(2)]
    qt = [E(nc.sbuf_tensor(f"{tag}_qt{i}", [128, S], BF16)) for i in range(2)]
    vt = [E(nc.sbuf_tensor(f"{tag}_vt{i}", [128, NKC * 130], BF16)) for i in range(2)]
    e_sb = [E(nc.sbuf_tensor(f"{tag}_e{i}", [128, NKC * TG], BF16)) for i in range(2)]
    rs = E(nc.sbuf_tensor(f"{tag}_rs", [1, TG], F32))
    rb = E(nc.sbuf_tensor(f"{tag}_rb", [128, TG], F32))
    o_sb = [E(nc.sbuf_tensor(f"{tag}_o{i}", [64, TG], BF16)) for i in range(2)]
    onesrow_f = kb.consts["ones_f"][:, 0:128]
    tri_le = kb.consts["tri_le"]
    o_store = [None, None]
    chunk_last_pv = [None, None]
    items = [(c, hh, qg) for c in range(NDC) for hh in range(2) for qg in range(NTG)]

    def emit_chunk_loads(c):
        slc = c % 2
        kb.wait_edge("sync", chunk_last_pv[slc])
        kb.wait_edge("gpsimd", chunk_last_pv[slc])
        lde = kb.dma("sync", f"d_{tag}l{slc}", lambda e, c=c, slc=slc: e.dma_start(
            out=kt[slc][:, :], in_=g["k1T"][c]))
        lde = kb.dma("sync", f"d_{tag}l{slc}", lambda e, c=c, slc=slc: e.dma_start(
            out=qt[slc][:, :], in_=g["q1T"][c]))
        me = kb.op("gpsimd", lambda e, slc=slc: e.memset(vt[slc][:, :], 1.0))
        kb.wait_edge("gpsimd", me)
        lde2 = None
        for hh in range(2):
            lde2 = kb.dma("gpsimd", f"d_{tag}v{slc}", lambda e, c=c, hh=hh, slc=slc: e.dma_start(
                out=vt[slc][:, :].rearrange("p (kc z) -> p kc z", z=130)[:, :, hh * 65: hh * 65 + 64],
                in_=g["v1tm"].rearrange("(kc p) ch -> p kc ch", p=128)[
                    :, :, c * 128 + hh * 64: c * 128 + hh * 64 + 64]))
        return (lde, lde2)

    prev_exp = None

    def emit_pv_state(i):
        c, hh, qg = items[i]
        slc = c % 2
        par = i % 2
        nkc = 4 * qg + 4
        pb_ = kb.psum[4 + par]
        state = {}

        def first():
            kb.wait_edge("tensor", state["exp_edge"])
            kb.wait_edge("tensor", state["ve_edge"])
            kb.wait_edge("tensor", kb.bank_free[4 + par])

        mms = []
        for kc in range(nkc):
            def mk(kc):
                def f():
                    if kc == 0:
                        first()
                    return kb.mm(lambda e, kc=kc: e.matmul(
                        pb_[0:65, :], vt[slc][:, kc * 130 + hh * 65: kc * 130 + hh * 65 + 65],
                        e_sb[par][:, kc * TG:(kc + 1) * TG],
                        start=(kc == 0), stop=(kc == nkc - 1)), inc=(kc == nkc - 1))
                return f
            mms.append(mk(kc))

        def finish(pve):
            nonlocal_state = state
            kb.wait_edge("vector", pve)
            kb.wait("vector", "sd_scalar", kb.cnt["sd_scalar"])
            re = kb.op("vector", lambda e: e.reciprocal(rs[:, :], pb_[64:65, :]))
            kb.wait_edge("tensor", re)
            kb.wait_edge("tensor", kb.bank_free[6])
            bce = kb.mm(lambda e: e.matmul(kb.psum[6][:, :], onesrow_f, rs[:, :],
                                           start=True, stop=True), inc=True)
            kb.wait_edge("scalar", bce)
            kb.wait("scalar", "sd_vector", kb.cnt["sd_vector"])
            ce = kb.op("scalar", lambda e: e.copy(rb[:, :], kb.psum[6][:, :]))
            kb.bank_free[6] = ce
            ob = i % 2
            kb.wait_edge("vector", ce)
            kb.wait_edge("vector", o_store[ob])
            ve = kb.op("vector", lambda e, ob=ob: e.tensor_mul(
                o_sb[ob][:, :], pb_[0:64, :], rb[0:64, :]))
            kb.bank_free[4 + par] = ve
            kb.wait_edge("sync", ve)
            o_store[ob] = kb.dma("sync", f"d_{tag}o{ob}", lambda e, c=c, hh=hh, qg=qg, ob=ob:
                                 e.dma_start(out=g["o1"][c, hh * 64:(hh + 1) * 64,
                                                         qg * TG:(qg + 1) * TG],
                                             in_=o_sb[ob][:, :]))
            chunk_last_pv[slc] = pve
        return state, mms, finish

    pend = None
    ld_edges = {0: emit_chunk_loads(0)}

    def drain_pv(n):
        nonlocal pend
        if pend is None:
            return
        state, mms, finish, hold = pend
        while mms and n != 0:
            hold[0] = mms.pop(0)()
            n -= 1
        if not mms:
            finish(hold[0])
            pend = None

    for i, (c, hh, qg) in enumerate(items):
        slc = c % 2
        par = i % 2
        hb = hh * 64
        nkc = 4 * qg + 4
        if hh == 0 and qg == 0:
            le = ld_edges[c]
            kb.wait_edge("tensor", le[0])
            kb.wait_edge("tensor", le[1])
        last_ve = None
        exp_edge = None
        for grp in range(nkc // 2):
            pg = grp % 2
            for j in range(2):
                kc = grp * 2 + j
                bj = pg * 2 + j
                a_ = max(0, (kc - 4 * qg) * 128)
                kb.wait_edge("tensor", kb.bank_free[bj])
                kb.mm(lambda e, kc=kc, bj=bj, a_=a_, hb=hb, qg=qg, slc=slc: e.matmul(
                    kb.psum[bj][:, a_:TG], kt[slc][hb:hb + 64, kc * 128:(kc + 1) * 128],
                    qt[slc][hb:hb + 64, qg * TG + a_:(qg + 1) * TG], start=True, stop=True),
                    inc=(j == 1))
            gme = ("sd_pe", kb.cnt["sd_pe"])
            kb.wait_edge("scalar", gme)
            exp_edge = kb.op("scalar", lambda e, grp=grp, par=par, pg=pg: e.activation(
                e_sb[par][:, grp * 2 * TG:(grp + 1) * 2 * TG],
                kb.psum_quad[:, pg * 2 * TG:(pg + 1) * 2 * TG], AF.Exp,
                scale=SCALE0))
            prev_exp = exp_edge
            kb.bank_free[pg * 2] = kb.bank_free[pg * 2 + 1] = exp_edge
            for j in range(2):
                kc = grp * 2 + j
                a_ = max(0, (kc - 4 * qg) * 128)
                if a_ > 0:
                    kb.wait_edge("vector", exp_edge)
                    last_ve = kb.op("vector", lambda e, kc=kc, a_=a_, par=par: e.memset(
                        e_sb[par][:, kc * TG: kc * TG + a_], 0.0))
                sd = kc - 4 * qg
                if 0 <= sd <= 3:
                    off = sd * 128
                    kb.wait_edge("vector", exp_edge)
                    last_ve = kb.op("vector", lambda e, kc=kc, off=off, par=par: e.tensor_mul(
                        e_sb[par][:, kc * TG + off: kc * TG + off + 128],
                        e_sb[par][:, kc * TG + off: kc * TG + off + 128], tri_le[:, :]))
            drain_pv(4)
        drain_pv(-1)
        st, mms, fin = emit_pv_state(i)
        st["exp_edge"] = exp_edge
        st["ve_edge"] = last_ve if last_ve is not None else exp_edge
        pend = (st, mms, fin, [None])
        if hh == 1 and qg == NTG - 1 and c + 1 < NDC:
            ld_edges[c + 1] = emit_chunk_loads(c + 1)
    drain_pv(-1)


def emit_ffn1a(kb, g, pctx):
    nc = kb.nc
    E = pctx.enter_context
    tag = "f1a"
    emit_router(kb, g, pctx, tag + "r", g["xh1"], g["xl1"], "wg1T", g["r1lg"])
    lrow = E(nc.sbuf_tensor(f"{tag}_lr", [32, NEXP * 128], F32))
    prow = E(nc.sbuf_tensor(f"{tag}_pr", [32, NEXP * 128], F32))
    mx = E(nc.sbuf_tensor(f"{tag}_mx", [32, 128], F32))
    sm = E(nc.sbuf_tensor(f"{tag}_sm", [32, 128], F32))
    t0 = E(nc.sbuf_tensor(f"{tag}_t0", [32, 128], F32))
    wrow = E(nc.sbuf_tensor(f"{tag}_wr", [32, 128], F32))
    nrow = E(nc.sbuf_tensor(f"{tag}_nr", [32, 128], BF16))
    arow = E(nc.sbuf_tensor(f"{tag}_ar", [32, 128], BF16))
    kb.wait("sync", f"d_{tag}rs", kb.cnt.get(f"d_{tag}rs", 0))
    lde = None
    for e_ in range(NEXP):
        lde = kb.dma("sync", f"d_{tag}lg", lambda e, e_=e_: e.dma_start(
            out=lrow[:, e_ * 128:(e_ + 1) * 128],
            in_=g["r1lg"][e_].rearrange("(a b) -> a b", b=128)))
    kb.wait_edge("vector", lde)
    sl_ = lambda t, e_: t[:, e_ * 128:(e_ + 1) * 128]
    kb.op("vector", lambda e: e.tensor_copy(mx[:, :], sl_(lrow, 0)))
    for e_ in range(1, NEXP):
        kb.op("vector", lambda e, e_=e_: e.tensor_max(mx[:, :], mx[:, :], sl_(lrow, e_)))
    for e_ in range(NEXP):
        ve = kb.op("vector", lambda e, e_=e_: e.tensor_sub(sl_(prow, e_), sl_(lrow, e_), mx[:, :]))
    kb.wait_edge("scalar", ve)
    for e_ in range(NEXP):
        se = kb.op("scalar", lambda e, e_=e_: e.activation(sl_(prow, e_), sl_(prow, e_), AF.Exp))
    kb.wait_edge("vector", se)
    kb.op("vector", lambda e: e.tensor_copy(sm[:, :], sl_(prow, 0)))
    for e_ in range(1, NEXP):
        kb.op("vector", lambda e, e_=e_: e.tensor_add(sm[:, :], sm[:, :], sl_(prow, e_)))
    kb.op("vector", lambda e: e.reciprocal(sm[:, :], sm[:, :]))
    for e_ in range(NEXP):
        kb.op("vector", lambda e, e_=e_: e.tensor_mul(sl_(prow, e_), sl_(prow, e_), sm[:, :]))
    kb.op("vector", lambda e: e.tensor_copy(mx[:, :], sl_(prow, 0)))
    for e_ in range(1, NEXP):
        kb.op("vector", lambda e, e_=e_: e.tensor_max(mx[:, :], mx[:, :], sl_(prow, e_)))
    first = True
    for e_ in range(NEXP):
        kb.op("vector", lambda e, e_=e_: e.tensor_tensor(t0[:, :], sl_(prow, e_), mx[:, :], ALU.is_ge))
        kb.op("vector", lambda e, e_=e_: e.tensor_scalar(t0[:, :], t0[:, :], -2.0, None, ALU.mult))
        kb.op("vector", lambda e, e_=e_: e.tensor_add(t0[:, :], t0[:, :], sl_(prow, e_)))
        if first:
            kb.op("vector", lambda e: e.tensor_copy(sm[:, :], t0[:, :]))
            first = False
        else:
            kb.op("vector", lambda e: e.tensor_max(sm[:, :], sm[:, :], t0[:, :]))
    for e_ in range(NEXP):
        kb.op("vector", lambda e, e_=e_: e.tensor_tensor(t0[:, :], sl_(prow, e_), sm[:, :], ALU.is_ge))
        ae = kb.op("vector", lambda e, e_=e_: e.tensor_mul(wrow[:, :], sl_(prow, e_), t0[:, :]))
        a2 = kb.op("vector", lambda e: e.tensor_copy(arow[:, :], t0[:, :]))
        kb.op("vector", lambda e: e.tensor_scalar(t0[:, :], t0[:, :], -1.0, -NEG, ALU.add, ALU.mult))
        ne = kb.op("vector", lambda e: e.tensor_copy(nrow[:, :], t0[:, :]))
        kb.wait_edge("sync", ne)
        kb.dma("sync", f"d_{tag}rw", lambda e, e_=e_: e.dma_start(
            out=g["r1w"][e_].rearrange("(a b) -> a b", b=128), in_=wrow[:, :]))
        kb.dma("sync", f"d_{tag}rw", lambda e, e_=e_: e.dma_start(
            out=g["r1act"][e_].rearrange("(a b) -> a b", b=128), in_=arow[:, :]))
        kb.dma("sync", f"d_{tag}rw", lambda e, e_=e_: e.dma_start(
            out=g["r1neg"][e_].rearrange("(a b) -> a b", b=128), in_=nrow[:, :]))
        kb.wait("vector", f"d_{tag}rw", kb.cnt[f"d_{tag}rw"])
    xin, lde = load_xin(kb, pctx, tag, g["xh1"])
    kb.wait_edge("tensor", lde)
    w1s = [E(nc.sbuf_tensor(f"{tag}_w1{i}", [128, NDC * D], BF16)) for i in range(2)]
    w3s = [E(nc.sbuf_tensor(f"{tag}_w3{i}", [128, NDC * D], BF16)) for i in range(2)]
    b1c = E(nc.sbuf_tensor(f"{tag}_b1c", [128, NEXP * NEXP], F32))
    b3c = E(nc.sbuf_tensor(f"{tag}_b3c", [128, NEXP * NEXP], F32))
    wr_sb = E(nc.sbuf_tensor(f"{tag}_wrs", [1, S], F32))
    nr_sb = E(nc.sbuf_tensor(f"{tag}_nrs", [1, S], BF16))
    wb_sb = E(nc.sbuf_tensor(f"{tag}_wb", [128, TG], F32))
    nb_sb = E(nc.sbuf_tensor(f"{tag}_nb", [128, TG], F32))
    tmp = [E(nc.sbuf_tensor(f"{tag}_tmp{i}", [128, TG], F32)) for i in range(2)]
    tmp2 = [E(nc.sbuf_tensor(f"{tag}_tmp2{i}", [128, TG], F32)) for i in range(2)]
    h_sb = [E(nc.sbuf_tensor(f"{tag}_h{i}", [128, NDC * TG], BF16)) for i in range(2)]
    onesrow_f = kb.consts["ones_f"][:, 0:128]
    be = kb.dma("sync", f"d_{tag}bc", lambda e: e.dma_start(
        out=b1c[:, :], in_=g["b11"].rearrange("p a b -> p (a b)")))
    be = kb.dma("sync", f"d_{tag}bc", lambda e: e.dma_start(
        out=b3c[:, :], in_=g["b31"].rearrange("p a b -> p (a b)")))
    kb.wait_edge("scalar", be)
    kb.wait_edge("vector", be)
    slab_last = [None, None]
    h_store = [None] * 2
    silu_e = [None, None]
    row_last = None
    for e_ in range(NEXP):
        sl = e_ % 2
        kb.wait_edge("gpsimd", slab_last[sl])
        for wsl, bank in ((w1s, "w11T"), (w3s, "w31T")):
            we = kb.dma("gpsimd", f"d_{tag}w{sl}", lambda e, e_=e_, sl=sl, wsl=wsl, bank=bank:
                        e.dma_start(out=wsl[sl][:, :].rearrange("p (dc o) -> p dc o", dc=NDC),
                                    in_=g[bank][e_].rearrange("(dc p) o -> p dc o", p=128)))
        kb.wait_edge("sync", row_last)
        rle = kb.dma("sync", f"d_{tag}rl", lambda e, e_=e_: e.dma_start(
            out=wr_sb[:, :], in_=g["r1w"][e_:e_ + 1, :]))
        rle = kb.dma("sync", f"d_{tag}rl", lambda e, e_=e_: e.dma_start(
            out=nr_sb[:, :], in_=g["r1neg"][e_:e_ + 1, :]))
        kb.wait_edge("tensor", we)
        kb.wait_edge("tensor", rle)
        for tg in range(NTG):
            kb.wait_edge("tensor", kb.bank_free[4])
            kb.wait_edge("tensor", kb.bank_free[5])
            kb.mm(lambda e, tg=tg: e.matmul(kb.psum[4][:, :], onesrow_f,
                                            wr_sb[:, tg * TG:(tg + 1) * TG], start=True, stop=True))
            bce = kb.mm(lambda e, tg=tg: e.matmul(kb.psum[5][:, :], kb.consts["ones_bf"][:, 0:128],
                                                  nr_sb[:, tg * TG:(tg + 1) * TG],
                                                  start=True, stop=True), inc=True)
            kb.wait_edge("scalar", bce)
            kb.wait("scalar", "sd_vector", kb.cnt["sd_vector"])
            ce1 = kb.op("scalar", lambda e: e.copy(wb_sb[:, :], kb.psum[4][:, :]))
            ce2 = kb.op("scalar", lambda e: e.copy(nb_sb[:, :], kb.psum[5][:, :]))
            kb.bank_free[4] = ce1
            kb.bank_free[5] = ce2
            kb.wait_edge("vector", ce2)
            for hc in range(NDC):
                bk = hc % 2
                kb.wait_edge("tensor", kb.bank_free[bk])
                for dc in range(NDC):
                    z1e = kb.mm(lambda e, sl=sl, dc=dc, hc=hc, tg=tg, bk=bk: e.matmul(
                        kb.psum[bk][:, :], w1s[sl][:, dc * D + hc * 128: dc * D + (hc + 1) * 128],
                        xin[:, dc * S + tg * TG: dc * S + (tg + 1) * TG],
                        start=(dc == 0), stop=(dc == NDC - 1)), inc=(dc == NDC - 1))
                tp = hc % 2
                kb.wait_edge("vector", z1e)
                kb.wait_edge("vector", silu_e[tp])
                mve = kb.op("vector", lambda e, bk=bk, tp=tp: e.tensor_mul(
                    tmp[tp][:, :], kb.psum[bk][:, :], wb_sb[:, :]))
                kb.bank_free[bk] = mve
                ve = kb.op("vector", lambda e, tp=tp: e.tensor_add(
                    tmp2[tp][:, :], tmp[tp][:, :], nb_sb[:, :]))
                hb = (e_ * NTG + tg) % 2
                kb.wait_edge("scalar", ve)
                if hc == 0:
                    kb.wait_edge("scalar", h_store[hb])
                se = kb.op("scalar", lambda e, e_=e_, hc=hc, hb=hb, tp=tp: e.activation(
                    h_sb[hb][:, hc * TG:(hc + 1) * TG], tmp2[tp][:, :], AF.Silu,
                    bias=b1c[:, e_ * NEXP + hc: e_ * NEXP + hc + 1]))
                silu_e[tp] = se
            kb.wait_edge("sync", se)
            h_store[hb] = kb.dma("sync", f"d_{tag}hs{hb}", lambda e, e_=e_, tg=tg, hb=hb:
                                 e.dma_start(out=g["h1"][e_ * NDC:(e_ + 1) * NDC, :,
                                                          tg * TG:(tg + 1) * TG].rearrange(
                                     "hc p t -> p hc t"),
                                     in_=h_sb[hb][:, :].rearrange("p (hc t) -> p hc t", hc=NDC)))
            for dc in range(NDC):
                bk = 2 + dc % 2
                kb.wait_edge("tensor", kb.bank_free[bk])
                for dci in range(NDC):
                    z3e = kb.mm(lambda e, sl=sl, dci=dci, dc=dc, tg=tg, bk=bk: e.matmul(
                        kb.psum[bk][:, :], w3s[sl][:, dci * D + dc * 128: dci * D + (dc + 1) * 128],
                        xin[:, dci * S + tg * TG: dci * S + (tg + 1) * TG],
                        start=(dci == 0), stop=(dci == NDC - 1)), inc=(dci == NDC - 1))
                tp = dc % 2
                kb.wait_edge("vector", z3e)
                if dc < 2:
                    kb.wait("vector", "sd_scalar", kb.cnt["sd_scalar"])
                mve = kb.op("vector", lambda e, bk=bk, tp=tp: e.tensor_mul(
                    tmp[tp][:, :], kb.psum[bk][:, :], wb_sb[:, :]))
                kb.bank_free[bk] = mve
                hb = (e_ * NTG + tg) % 2
                if dc == 0:
                    kb.wait_edge("vector", h_store[hb])
                ve = kb.op("vector", lambda e, e_=e_, dc=dc, hb=hb, tp=tp: e.tensor_scalar(
                    h_sb[hb][:, dc * TG:(dc + 1) * TG], tmp[tp][:, :],
                    b3c[:, e_ * NEXP + dc: e_ * NEXP + dc + 1], None, ALU.add))
            kb.wait_edge("sync", ve)
            h_store[hb] = kb.dma("sync", f"d_{tag}hs{hb}", lambda e, e_=e_, tg=tg, hb=hb:
                                 e.dma_start(out=g["x31"][e_ * NDC:(e_ + 1) * NDC, :,
                                                          tg * TG:(tg + 1) * TG].rearrange(
                                     "dc p t -> p dc t"),
                                     in_=h_sb[hb][:, :].rearrange("p (dc t) -> p dc t", dc=NDC)))
            slab_last[sl] = z3e
        row_last = z3e


def emit_ffn1b(kb, g, pctx):
    nc = kb.nc
    E = pctx.enter_context
    tag = "f1b"
    w2s = E(nc.sbuf_tensor(f"{tag}_w2", [128, NEXP * NDC * D], BF16))
    h_sb = [E(nc.sbuf_tensor(f"{tag}_h{i}", [128, NDC * TG], BF16)) for i in range(2)]
    x3_sb = [E(nc.sbuf_tensor(f"{tag}_x3{i}", [128, NDC * TG], BF16)) for i in range(2)]
    b2s = [E(nc.sbuf_tensor(f"{tag}_b2s{i}", [1, D], BF16)) for i in range(2)]
    ar_sb = [E(nc.sbuf_tensor(f"{tag}_ar{i}", [1, S], BF16)) for i in range(2)]
    pt = E(nc.sbuf_tensor(f"{tag}_p", [128, TG], F32))
    y_sb = E(nc.sbuf_tensor(f"{tag}_y", [128, NDC * TG], F32))
    yo = [E(nc.sbuf_tensor(f"{tag}_yo{i}", [128, TG], F32)) for i in range(2)]
    we = None
    for e_ in range(NEXP):
        we = kb.dma("gpsimd", f"d_{tag}w", lambda e, e_=e_: e.dma_start(
            out=w2s[:, e_ * NDC * D:(e_ + 1) * NDC * D].rearrange("p (hc o) -> p hc o", hc=NDC),
            in_=g["w21T"][e_].rearrange("(hc p) o -> p hc o", p=128)))
    kb.wait_edge("tensor", we)
    onesrow_b = kb.consts["ones_bf"][:, 0:128]
    y_store = [None, None]
    ar_last = [None, None]
    slot_mm = [None, None]
    slot_ve = [None, None]
    for tg in range(NTG):
        for e_ in range(NEXP):
            sl = e_ % 2
            kb.wait_edge("sync", slot_mm[sl])
            kb.wait_edge("sync", slot_ve[sl])
            lde = kb.dma("sync", f"d_{tag}h{sl}", lambda e, e_=e_, tg=tg, sl=sl:
                         e.dma_start(out=h_sb[sl][:, :].rearrange("p (hc t) -> p hc t", hc=NDC),
                                     in_=g["h1"][e_ * NDC:(e_ + 1) * NDC, :,
                                                 tg * TG:(tg + 1) * TG].rearrange("hc p t -> p hc t")))
            lde = kb.dma("sync", f"d_{tag}h{sl}", lambda e, e_=e_, tg=tg, sl=sl:
                         e.dma_start(out=x3_sb[sl][:, :].rearrange("p (dc t) -> p dc t", dc=NDC),
                                     in_=g["x31"][e_ * NDC:(e_ + 1) * NDC, :,
                                                  tg * TG:(tg + 1) * TG].rearrange("dc p t -> p dc t")))
            st = tg * NEXP + e_
            kb.wait_edge("sync", ar_last[st % 2])
            kb.wait_edge("gpsimd", ar_last[st % 2])
            kb.dma("gpsimd", f"d_{tag}b{st % 2}", lambda e, e_=e_, st=st: e.dma_start(
                out=b2s[st % 2][:, :], in_=g["b21"][e_:e_ + 1, :]))
            are = kb.dma("sync", f"d_{tag}a{st % 2}", lambda e, e_=e_, st=st: e.dma_start(
                out=ar_sb[st % 2][:, :], in_=g["r1act"][e_:e_ + 1, :]))
            kb.wait_edge("tensor", lde)
            for dc in range(NDC):
                bk = dc % 4
                kb.wait_edge("tensor", kb.bank_free[bk])
                for hc in range(NDC):
                    kb.mm(lambda e, e_=e_, dc=dc, hc=hc, sl=sl, bk=bk: e.matmul(
                        kb.psum[bk][:, :],
                        w2s[:, e_ * NDC * D + hc * D + dc * 128: e_ * NDC * D + hc * D + (dc + 1) * 128],
                        h_sb[sl][:, hc * TG:(hc + 1) * TG],
                        start=(hc == 0), stop=False))
                if dc == 0:
                    kb.wait_edge("tensor", are)
                    kb.wait("tensor", f"d_{tag}b{st % 2}", kb.cnt[f"d_{tag}b{st % 2}"])
                mme = kb.mm(lambda e, dc=dc, tg=tg, st=st, bk=bk: e.matmul(
                    kb.psum[bk][:, :], b2s[st % 2][0:1, dc * 128:(dc + 1) * 128],
                    ar_sb[st % 2][:, tg * TG:(tg + 1) * TG], start=False, stop=True), inc=True)
                ar_last[st % 2] = mme
                slot_mm[sl] = mme
                kb.wait_edge("vector", mme)
                ve = kb.op("vector", lambda e, bk=bk, sl=sl, dc=dc: e.tensor_mul(
                    pt[:, :], kb.psum[bk][:, :], x3_sb[sl][:, dc * TG:(dc + 1) * TG]))
                kb.bank_free[bk] = ve
                slot_ve[sl] = ve
                if e_ == 0:
                    ve = kb.op("vector", lambda e, dc=dc: e.tensor_copy(
                        y_sb[:, dc * TG:(dc + 1) * TG], pt[:, :]))
                else:
                    ve = kb.op("vector", lambda e, dc=dc: e.tensor_add(
                        y_sb[:, dc * TG:(dc + 1) * TG], y_sb[:, dc * TG:(dc + 1) * TG], pt[:, :]))
        for dc in range(NDC):
            yb = dc % 2
            kb.wait("vector", f"d_{tag}y{yb}", kb.cnt.get(f"d_{tag}y{yb}", 0))
            ve2 = kb.op("vector", lambda e, dc=dc, yb=yb: e.tensor_copy(
                yo[yb][:, :], y_sb[:, dc * TG:(dc + 1) * TG]))
            kb.wait_edge("sync", ve2)
            kb.dma("sync", f"d_{tag}y{yb}", lambda e, dc=dc, tg=tg, yb=yb: e.dma_start(
                out=g["y1T"][dc, :, tg * TG:(tg + 1) * TG], in_=yo[yb][:, :]))


def emit_shfin1(kb, g, pctx):
    nc = kb.nc
    E = pctx.enter_context
    tag = "sf1"
    xin, lde = load_xin(kb, pctx, tag, g["xh1"])
    kb.wait_edge("tensor", lde)
    NIC = 16
    sw1 = E(nc.sbuf_tensor(f"{tag}_sw1", [128, NDC * 2 * D], BF16))
    sw2 = E(nc.sbuf_tensor(f"{tag}_sw2", [128, NDC * 2 * D], BF16))
    sw3 = E(nc.sbuf_tensor(f"{tag}_sw3", [128, NIC * D], BF16))
    zh = E(nc.sbuf_tensor(f"{tag}_zh", [128, NIC * TG], BF16))
    tmp = E(nc.sbuf_tensor(f"{tag}_tmp", [128, TG], F32))
    yt = E(nc.sbuf_tensor(f"{tag}_y", [128, TG], F32))
    s1t = E(nc.sbuf_tensor(f"{tag}_s1", [128, TG], BF16))
    x1t = E(nc.sbuf_tensor(f"{tag}_x1", [128, TG], F32))
    gt = E(nc.sbuf_tensor(f"{tag}_g", [128, TG], F32))
    ot = [E(nc.sbuf_tensor(f"{tag}_o{i}", [128, 4 * TG], F32)) for i in range(2)]
    for nm, t in (("sw1T", sw1), ("sw2T", sw2)):
        we = kb.dma("gpsimd", f"d_{tag}w", lambda e, nm=nm, t=t: e.dma_start(
            out=t[:, :].rearrange("p (dc o) -> p dc o", dc=NDC),
            in_=g[nm].rearrange("(dc p) o -> p dc o", p=128)))
    we = kb.dma("gpsimd", f"d_{tag}w", lambda e: e.dma_start(
        out=sw3[:, :].rearrange("p (ic o) -> p ic o", ic=NIC),
        in_=g["sw3T"].rearrange("(ic p) o -> p ic o", p=128)))
    kb.wait_edge("tensor", we)
    o_store = [None, None]
    for tg in range(NTG):
        for ic in range(NIC):
            bka, bku = ic % 2, 2 + ic % 2
            kb.wait_edge("tensor", kb.bank_free[bka])
            for dc in range(NDC):
                kb.mm(lambda e, dc=dc, ic=ic, tg=tg, bka=bka: e.matmul(
                    kb.psum[bka][:, :], sw1[:, dc * 2 * D + ic * 128: dc * 2 * D + (ic + 1) * 128],
                    xin[:, dc * S + tg * TG: dc * S + (tg + 1) * TG],
                    start=(dc == 0), stop=(dc == NDC - 1)), inc=(dc == NDC - 1))
            kb.wait_edge("tensor", kb.bank_free[bku])
            for dc in range(NDC):
                ue = kb.mm(lambda e, dc=dc, ic=ic, tg=tg, bku=bku: e.matmul(
                    kb.psum[bku][:, :], sw2[:, dc * 2 * D + ic * 128: dc * 2 * D + (ic + 1) * 128],
                    xin[:, dc * S + tg * TG: dc * S + (tg + 1) * TG],
                    start=(dc == 0), stop=(dc == NDC - 1)), inc=(dc == NDC - 1))
            kb.wait("scalar", "sd_pe", kb.cnt["sd_pe"])
            kb.wait("scalar", "sd_vector", kb.cnt["sd_vector"])
            se = kb.op("scalar", lambda e, bka=bka: e.activation(tmp[:, :], kb.psum[bka][:, :], AF.Silu))
            kb.bank_free[bka] = se
            kb.wait_edge("vector", se)
            kb.wait_edge("vector", ue)
            ve = kb.op("vector", lambda e, ic=ic, bku=bku: e.tensor_mul(
                zh[:, ic * TG:(ic + 1) * TG], tmp[:, :], kb.psum[bku][:, :]))
            kb.bank_free[bku] = ve
        kb.wait_edge("tensor", ve)
        for dc in range(NDC):
            bk = 4 + dc % 2
            kb.wait_edge("tensor", kb.bank_free[bk])
            for ic in range(NIC):
                zme = kb.mm(lambda e, ic=ic, dc=dc, bk=bk: e.matmul(
                    kb.psum[bk][:, :], sw3[:, ic * D + dc * 128: ic * D + (dc + 1) * 128],
                    zh[:, ic * TG:(ic + 1) * TG],
                    start=(ic == 0), stop=(ic == NIC - 1)), inc=(ic == NIC - 1))
            ob = dc % 2
            kb.wait("sync", "sd_vector", kb.cnt["sd_vector"])
            l1 = kb.dma("sync", f"d_{tag}e", lambda e, dc=dc, tg=tg: e.dma_start(
                out=yt[:, :], in_=g["y1T"][dc, :, tg * TG:(tg + 1) * TG]))
            l1 = kb.dma("sync", f"d_{tag}e", lambda e, dc=dc, tg=tg: e.dma_start(
                out=s1t[:, :], in_=g["s1T"][dc, :, tg * TG:(tg + 1) * TG]))
            l1 = kb.dma("sync", f"d_{tag}e", lambda e, dc=dc, tg=tg: e.dma_start(
                out=x1t[:, :], in_=g["x1"][dc, :, tg * TG:(tg + 1) * TG]))
            kb.wait_edge("vector", zme)
            kb.wait_edge("vector", l1)
            kb.wait("vector", "sd_scalar", kb.cnt["sd_scalar"])
            ve2 = kb.op("vector", lambda e, bk=bk: e.tensor_add(tmp[:, :], kb.psum[bk][:, :], yt[:, :]))
            kb.bank_free[bk] = ve2
            kb.wait_edge("scalar", ve2)
            se2 = kb.op("scalar", lambda e: e.activation(gt[:, :], tmp[:, :], AF.Silu))
            ob = (dc // 4) % 2
            kb.wait_edge("vector", se2)
            if dc % 4 == 0:
                kb.wait_edge("vector", o_store[ob])
            kb.op("vector", lambda e: e.tensor_mul(gt[:, :], gt[:, :], s1t[:, :]))
            ve3 = kb.op("vector", lambda e, ob=ob, dc=dc: e.tensor_add(
                ot[ob][:, (dc % 4) * TG:(dc % 4 + 1) * TG], x1t[:, :], gt[:, :]))
            if dc % 4 == 3:
                kb.wait_edge("sync", ve3)
                o_store[ob] = kb.dma("sync", f"d_{tag}o{ob}", lambda e, dc=dc, tg=tg, ob=ob: e.dma_start(
                    out=g["out_fm"][dc - 3:dc + 1, :, tg * TG:(tg + 1) * TG].rearrange("c p t -> p c t"),
                    in_=ot[ob][:, :].rearrange("p (c t) -> p c t", c=4)))


def emit_outstub(kb, g, pctx):
    nc = kb.nc
    E = pctx.enter_context
    t = E(nc.sbuf_tensor("stub_t", [128, S], F32))
    for c in range(NDC):
        e1 = kb.dma("sync", "d_stub", lambda e, c=c: e.dma_start(out=t[:, :], in_=g["x_fm"][c]))
        kb.wait_edge("sync", e1)
        e2 = kb.dma("sync", "d_stub", lambda e, c=c: e.dma_start(out=g["out_fm"][c], in_=t[:, :]))
        kb.wait_edge("sync", e2)



def kernel(**inputs):
    maps = host_prep(inputs)
    nc = build()
    res = run_bass_kernel_spmd(nc, maps, core_ids=list(range(8)))
    out = np.empty((8, S, D), dtype=np.float32)
    for b in range(8):
        out[b] = res.results[b]["out_fm"].reshape(D, S).T
    return out
